# version 14
# baseline (speedup 1.0000x reference)
import contextlib
import numpy as np
import ml_dtypes
import concourse.bass as bass
import concourse.mybir as mybir
from concourse.bass_utils import run_bass_kernel_spmd

F32 = mybir.dt.float32
BF16 = mybir.dt.bfloat16
AX = mybir.AxisListType.X
ALU = mybir.AluOpType
ACT = mybir.ActivationFunctionType
BF = ml_dtypes.bfloat16

ENGINES = ("t", "v", "s", "g", "q")


class Cfg:
    def __init__(self, B=2, T=2048, D=1024, H=16, NQ=256,
                 sub_blocks=("l0a", "l0f", "l1a", "l1f")):
        self.B, self.T, self.D, self.H, self.NQ = B, T, D, H, NQ
        self.DH = D // H
        assert self.DH == 64 and NQ == 256
        self.C = D // 128
        self.TT = T // 128
        self.NSL = T // NQ
        self.SL2 = T // 512
        self.sub_blocks = tuple(sub_blocks)


class Res:
    __slots__ = ("w", "rs")

    def __init__(self):
        self.w = None
        self.rs = []


class Buf:
    def __init__(self, t, ncols, gran):
        self.t = t
        self.gran = gran
        self.res = [Res() for _ in range(-(-ncols // gran))]

    def rr(self, lo, hi):
        return self.res[lo // self.gran: -(-hi // self.gran)]


class Bld:
    def __init__(self):
        self.ops = {e: [] for e in ENGINES}
        self.streams = {}
        self.need_inc = {e: set() for e in ENGINES}

    def op(self, eng, fname, *args, r=(), w=(), deps=(), stream=None, **kw):
        deph = []
        for res in r:
            if res.w is not None:
                deph.append(res.w)
        for res in w:
            if res.w is not None:
                deph.append(res.w)
            deph.extend(res.rs)
        deph.extend(deps)
        deph = [h if h[0] == "e" else ("d", h[1], self.streams[h[1]]) for h in deph]
        seen, dd = set(), []
        for h in deph:
            if h not in seen:
                seen.add(h)
                dd.append(h)
        idx = len(self.ops[eng])
        if stream is not None:
            self.streams[stream] = self.streams.get(stream, 0) + 1
            handle = ("d", stream, self.streams[stream])
        else:
            handle = ("e", eng, idx)
        if eng == "t":
            dd = [h for h in dd if not (h[0] == "e" and h[1] == "t")]
        self.ops[eng].append(dict(f=fname, a=args, k=kw, deps=dd, stream=stream))
        for h in dd:
            if h[0] == "e":
                self.need_inc[h[1]].add(h[2])
        for res in r:
            res.rs.append(handle)
        for res in w:
            res.w = handle
            res.rs = []
        return handle

    def finalize(self):
        self.prefix = {}
        for e in ENGINES:
            pr, c = [], 0
            for i in range(len(self.ops[e])):
                if i in self.need_inc[e]:
                    c += 1
                pr.append(c)
            self.prefix[e] = pr

    def replay(self, eng, obj, ctr_sems, stream_sems):
        covered = {}
        for i, rec in enumerate(self.ops[eng]):
            for h in rec["deps"]:
                if h[0] == "e":
                    key, val, sem = ("e", h[1]), self.prefix[h[1]][h[2]], ctr_sems[h[1]]
                else:
                    key, val, sem = ("d", h[1]), 16 * h[2], stream_sems[h[1]]
                if covered.get(key, 0) < val:
                    obj.wait_ge(sem, val)
                    covered[key] = val
            if rec["f"] is None:
                continue
            try:
                instr = getattr(obj, rec["f"])(*rec["a"], **rec["k"])
            except Exception:
                import sys
                print(f"REPLAY FAIL {eng}[{i}] {rec['f']} args={[getattr(a,'shape',a) for a in rec['a']]}",
                      file=sys.stderr)
                raise
            if rec["stream"] is not None:
                instr.then_inc(stream_sems[rec["stream"]], 16)
            elif i in self.need_inc[eng]:
                instr.then_inc(ctr_sems[eng], 1)



def tile_w(W):
    Din, Dout = W.shape
    return np.ascontiguousarray(
        np.asarray(W, np.float32).reshape(Din // 128, 128, Dout // 128, 128)
        .transpose(2, 0, 1, 3))


def vec_tile(v):
    return np.ascontiguousarray(np.asarray(v, np.float32).reshape(-1, 128).T)


def host_consts(cfg):
    NQ = cfg.NQ
    c = {}
    c["ident"] = np.eye(128, dtype=BF)
    tp = np.arange(128)[:, None]
    u = np.arange(NQ + 130)[None, :]
    c["masks"] = np.ascontiguousarray((u - 129 - tp >= 0).astype(np.float32)).astype(BF)
    c["onesa"] = np.full((128, 1), 1.0 / 63.0, dtype=BF)
    c["onesb"] = np.full((128, 1), 4096.0 / (63.0 * 64.0), dtype=BF)
    c["ones64f"] = np.ones((1, 64), dtype=np.float32)
    return c



def build_program(cfg, input_specs):
    nc = bass.Bass("TRN2", target_bir_lowering=False)
    dram = {}
    for name, (shape, dt) in input_specs.items():
        mdt = F32 if np.dtype(dt) == np.float32 else BF16
        dram[name] = nc.dram_tensor(name, list(shape), mdt, kind="ExternalInput").ap()
    out_d = nc.dram_tensor("out", [cfg.B, cfg.T, cfg.D], F32, kind="ExternalOutput").ap()

    b = Bld()
    T, D, C, TT, NQ, NSL, SL2 = (cfg.T, cfg.D, cfg.C, cfg.TT, cfg.NQ,
                                 cfg.NSL, cfg.SL2)
    ctx = contextlib.ExitStack()

    def S(name, shape, dt=BF16, gran=None):
        t = ctx.enter_context(nc.sbuf_tensor(name, list(shape), dt))
        return Buf(t, shape[-1], gran or shape[-1])

    with ctx:
        X = S("X", [128, TT * 1024], gran=1024)
        XN = S("XN", [128, C * T], gran=T)
        BA = S("BA", [128, max(C * T, TT * 1024)], gran=1024)
        BB = S("BB", [128, max(C * T, TT * 1024)], gran=1024)
        BC = S("BCb", [128, max(2 * TT * NQ, min(TT, 8) * 1024)], gran=NQ)
        CH = S("CH", [128, 5 * T], gran=T)
        VST = S("VST", [128, TT * 130], gran=TT * 130)
        QST = S("QST", [128, 2 * T], gran=T)
        EST = QST
        KST = S("KST", [128, 2 * T], gran=T)
        ODD = S("ODD", [64, 2 * NQ], gran=NQ)
        WP = S("WP", [128, 6 * 1024], gran=1024)
        TBF = S("TBF", [128, 2 * 512], gran=512)
        XT = S("XT", [128, 2 * 1024], gran=1024)
        TF = S("TF", [128, 512], F32, gran=256)
        RRW = TF
        STT = S("STT", [128, 2 * 12], F32, gran=12)
        MV = S("MV", [128, 2 * TT], F32, gran=1)
        RSD = S("RSD", [128, TT], F32)
        BIASN = S("BIASN", [128, TT], F32)
        SCR = S("SCR", [128, 3 * TT], F32)

        MUB = S("MUB", [128, 4 * C], F32, gran=C)
        IDENT = S("IDENTB", [128, 128])
        MASKS = S("MASKSB", [128, NQ + 130])
        ONESA = S("ONESAB", [128, 1])
        ONESB = S("ONESBB", [128, 1])
        ONES64F = S("ONES64FB", [1, 64], F32)

        PSA = Buf(ctx.enter_context(nc.psum_tensor("psa", [128, 1024], F32)), 1024, 512)
        PSB = Buf(ctx.enter_context(nc.psum_tensor("psb", [128, 1024], F32)), 1024, 512)
        PS4 = Buf(ctx.enter_context(nc.psum_tensor("ps4", [128, 512], F32)), 512, 512)
        PS5 = Buf(ctx.enter_context(nc.psum_tensor("ps5", [128, 512], F32)), 512, 512)
        PS6 = Buf(ctx.enter_context(nc.psum_tensor("ps6", [128, 512], F32)), 512, 512)
        PS7 = Buf(ctx.enter_context(nc.psum_tensor("ps7", [128, 512], F32)), 512, 512)
        PT = Buf(PS7.t[:, 0:512].bitcast(BF16), 1024, 1024)
        PT.res = PS7.res

        ident_ap = IDENT.t[:, :]
        MVm = lambda tt: MV.t[:, tt:tt + 1]
        BAv = BA.t[:, 0:TT * 1024].rearrange("p (t j) -> p t j", j=1024)
        BBv = BB.t[:, 0:TT * 1024].rearrange("p (t j) -> p t j", j=1024)
        BCv = BC.t[:, 0:min(TT, 8) * 1024].rearrange("p (t j) -> p t j", j=1024)
        CHv = CH.t[:, 0:(TT - min(TT, 8)) * 1024].rearrange("p (t j) -> p t j", j=1024) if TT > 8 else None
        XNv = XN.t.rearrange("p (c j) -> p c j", c=C)

        def wslot_load(slot, wname, d, nchunk=None):
            nck = nchunk or C
            src = dram[wname][d].transpose([1, 0, 2])
            dst = WP.t[:, slot * 1024: slot * 1024 + nck * 128]
            b.op("g", "dma_start", w=WP.res[slot:slot + 1], stream=f"wg{slot}",
                 out=dst.rearrange("p (c j) -> p c j", c=nck), in_=src)

        def wap(slot, c2):
            return WP.t[:, slot * 1024 + c2 * 128: slot * 1024 + (c2 + 1) * 128]

        def wslice(slot, ncols=None):
            return WP.t[:, slot * 1024: slot * 1024 + (ncols or C * 128)]

        def wres(slot):
            return WP.res[slot:slot + 1]

        def bank_ap(sl2):
            tb = PSA if (sl2 % 4) < 2 else PSB
            return tb, (sl2 % 2) * 512

        def rstd_chain():
            b.op("v", "tensor_scalar_add", SCR.t[:, 0:TT], MV.t[:, TT:2 * TT], 1e-5,
                 r=MV.res[TT:2 * TT], w=SCR.res)
            b.op("s", "activation", SCR.t[:, TT:2 * TT], SCR.t[:, 0:TT], ACT.Sqrt,
                 r=SCR.res, w=SCR.res)
            b.op("v", "reciprocal", SCR.t[:, 2 * TT:3 * TT], SCR.t[:, TT:2 * TT],
                 r=SCR.res, w=SCR.res)
            b.op("v", "tensor_mul", RSD.t[:, :], SCR.t[:, 2 * TT:3 * TT],
                 SCR.t[:, 2 * TT:3 * TT], r=SCR.res, w=RSD.res)
            b.op("v", "tensor_mul", RSD.t[:, :], RSD.t[:, :], SCR.t[:, 0:TT],
                 r=SCR.res + RSD.res, w=RSD.res)
            b.op("v", "tensor_scalar", SCR.t[:, 0:TT], RSD.t[:, :], -0.5, 1.5,
                 ALU.mult, ALU.add, r=RSD.res, w=SCR.res)
            b.op("v", "tensor_mul", RSD.t[:, :], SCR.t[:, 2 * TT:3 * TT],
                 SCR.t[:, 0:TT], r=SCR.res, w=RSD.res)

        def ln_stats(tile_aps):
            ng = -(-D // 512)
            for tt in range(TT):
                ap, res_l = tile_aps[tt]
                st = tt & 1
                for g in range(ng):
                    wseg = min(512, D - g * 512)
                    b.op("v", "bn_stats",
                         STT.t[:, st * 12 + g * 6: st * 12 + g * 6 + 6],
                         ap[:, g * 512: g * 512 + wseg], r=res_l, w=STT.res[st:st + 1])
                mvout = MV.t.rearrange("p (two t) -> p two t", two=2)[:, :, tt]
                b.op("v", "bn_aggr", mvout, STT.t[:, st * 12: st * 12 + ng * 6],
                     r=STT.res[st:st + 1],
                     w=MV.res[tt:tt + 1] + MV.res[TT + tt:TT + tt + 1])
            rstd_chain()
            b.op("v", "scalar_tensor_tensor", BIASN.t[:, :], MV.t[:, 0:TT], -1.0,
                 RSD.t[:, :], ALU.mult, ALU.mult,
                 r=MV.res[0:TT] + RSD.res, w=BIASN.res)

        def pre_ln(dest_tm=None):
            dest_tm = dest_tm or BA
            ln_stats([(X.t[:, tt * 1024: tt * 1024 + D], X.res[tt:tt + 1])
                      for tt in range(TT)])
            tiles = []
            for tt in range(TT):
                dst_ap = dest_tm.t[:, tt * 1024: tt * 1024 + D]
                dst_res = dest_tm.rr(tt * 1024, tt * 1024 + D)
                tiles.append((dst_ap, dst_res))
                b.op("s", "activation", dst_ap, X.t[:, tt * 1024: tt * 1024 + D],
                     ACT.Identity, bias=BIASN.t[:, tt:tt + 1],
                     scale=RSD.t[:, tt:tt + 1],
                     r=list(X.res[tt:tt + 1]) + BIASN.res + RSD.res,
                     w=dst_res)
            for c2 in range(C):
                for g in range(-(-TT // 8)):
                    ts = list(range(g * 8, min(TT, g * 8 + 8)))
                    for k, tt in enumerate(ts):
                        dst_ap, dst_res = tiles[tt]
                        b.op("t", "matmul", PT.t[:, k * 128:(k + 1) * 128],
                             dst_ap[:, c2 * 128:(c2 + 1) * 128], ident_ap,
                             is_transpose=True, start=(k == 0), stop=(k == len(ts) - 1),
                             r=list(dst_res) + IDENT.res, w=PT.res)
                    b.op("v", "tensor_copy",
                         XN.t[:, c2 * T + g * 1024: c2 * T + g * 1024 + len(ts) * 128],
                         PT.t[:, 0: len(ts) * 128],
                         r=PT.res, w=XN.res[c2:c2 + 1])

        def post_ln(src_tiles):
            ln_stats(src_tiles)
            for tt in range(TT):
                ap, res_l = src_tiles[tt]
                xs = tt & 1
                b.op("s", "activation", XT.t[:, xs * 1024: xs * 1024 + D],
                     ap, ACT.Identity, bias=BIASN.t[:, tt:tt + 1],
                     scale=RSD.t[:, tt:tt + 1],
                     r=list(res_l) + BIASN.res + RSD.res,
                     w=XT.res[xs:xs + 1])
                b.op("v", "tensor_add", X.t[:, tt * 1024: tt * 1024 + D],
                     X.t[:, tt * 1024: tt * 1024 + D],
                     XT.t[:, xs * 1024: xs * 1024 + D],
                     r=XT.res[xs:xs + 1] + X.res[tt:tt + 1], w=X.res[tt:tt + 1])

        def proj4(wslot, rhs=None, stop=True):
            src = rhs or XN
            for c2 in range(C):
                for sl2 in range(SL2):
                    tb, col = bank_ap(sl2)
                    b.op("t", "matmul", tb.t[:, col:col + 512], wap(wslot, c2),
                         src.t[:, c2 * T + sl2 * 512: c2 * T + sl2 * 512 + 512],
                         start=(c2 == 0), stop=(stop and c2 == C - 1),
                         r=wres(wslot) + src.rr(c2 * T + sl2 * 512,
                                                c2 * T + sl2 * 512 + 512),
                         w=tb.res[col // 512: col // 512 + 1])

        def proj_segments():
            segs = []
            off = 0
            while off < SL2 * 512:
                tb = PSA if (off // 512) % 4 < 2 else PSB
                wd = min(1024, SL2 * 512 - off)
                segs.append((tb, 0, wd, off))
                off += wd
            return segs

        def proj2(wslot, sl2s, rhs=None):
            src = rhs or XN
            out = []
            for j, sl2 in enumerate(sl2s):
                out.append((PS6 if j == 0 else PS7, sl2))
            for c2 in range(C):
                for j, sl2 in enumerate(sl2s):
                    psb = PS6 if j == 0 else PS7
                    b.op("t", "matmul", psb.t[:, 0:512], wap(wslot, c2),
                         src.t[:, c2 * T + sl2 * 512: c2 * T + sl2 * 512 + 512],
                         start=(c2 == 0), stop=(c2 == C - 1),
                         r=wres(wslot) + src.rr(c2 * T + sl2 * 512,
                                                c2 * T + sl2 * 512 + 512),
                         w=psb.res)
            return out

        def sl2_pairs():
            return [tuple(range(p, min(p + 2, SL2))) for p in range(0, SL2, 2)]

        def transpose_strip(src_ap, src_res, k0, start, stop):
            for k in range(4):
                b.op("t", "matmul", PT.t[:, (k0 + k) * 128:(k0 + k + 1) * 128],
                     src_ap[:, k * 128:(k + 1) * 128], ident_ap, is_transpose=True,
                     start=(start and k == 0), stop=(stop and k == 3),
                     r=list(src_res) + IDENT.res, w=PT.res)

        def fm_to_tm(d, wslot, tiles_of, rhs_buf, epilogue):
            proj4(wslot, rhs=rhs_buf)
            ngrp = 0
            for sl2 in range(SL2):
                tb, col = bank_ap(sl2)
                ps_ap = tb.t[:, col:col + 512]
                ps_res = tb.res[col // 512: col // 512 + 1]
                strip_res = TBF.res[sl2 & 1: (sl2 & 1) + 1]
                strip_ap = TBF.t[:, (sl2 & 1) * 512: (sl2 & 1) * 512 + 512]
                epilogue(ps_ap, ps_res, sl2, strip_ap, strip_res)
                close = (ngrp == 1) or (sl2 == SL2 - 1)
                transpose_strip(strip_ap, strip_res, 4 * ngrp,
                                start=(ngrp == 0), stop=close)
                ngrp += 1
                if close:
                    nt = 4 * ngrp
                    t0 = sl2 * 4 - (nt - 4)
                    dstv, dres = tiles_of(t0, nt)
                    b.op("v", "tensor_copy",
                         dstv[:, :, d * 128:(d + 1) * 128],
                         PT.t[:, 0:nt * 128].rearrange("p (c j) -> p c j", c=nt),
                         r=PT.res, w=dres)
                    ngrp = 0

        def l0_attn():
            pre_ln()
            vstv = VST.t[:, :].rearrange("p (t h e) -> p t h e", t=TT, h=2)
            nh = 2 * C

            def prep_chunks(c):
                wq_s, wk_s, wv_s = (c & 1), 2 + (c & 1), 4 + (c & 1)
                qch, kch, sqch = (c & 1), 2 + (c & 1), 4
                loads = [lambda: wslot_load(wq_s, "Wq", c),
                         lambda: wslot_load(wk_s, "Wk", c),
                         lambda: wslot_load(wv_s, "Wv", c)]
                ch = []

                def load_prep_q(c=c, wq_s=wq_s):
                    w3 = wslice(wq_s).rearrange("p (h e) -> p h e", e=64)
                    b.op("v", "tensor_reduce", SCR.t[:, 0:nh], w3, AX, ALU.add,
                         r=wres(wq_s), w=SCR.res)
                    b.op("v", "tensor_scalar_mul", SCR.t[:, 0:nh], SCR.t[:, 0:nh],
                         1.0 / 64.0, r=SCR.res, w=SCR.res)
                    for hh in range(nh):
                        b.op("v", "tensor_scalar_sub",
                             wslice(wq_s)[:, hh * 64:(hh + 1) * 64],
                             wslice(wq_s)[:, hh * 64:(hh + 1) * 64],
                             SCR.t[:, hh: hh + 1],
                             r=wres(wq_s) + SCR.res, w=wres(wq_s))
                ch.append(load_prep_q)

                def load_prep_k(c=c, wk_s=wk_s):
                    k3 = wslice(wk_s).rearrange("p (h e) -> p h e", e=64)
                    b.op("v", "tensor_reduce", SCR.t[:, 0:nh], k3, AX, ALU.add,
                         r=wres(wk_s), w=SCR.res)
                    b.op("v", "tensor_scalar_mul", SCR.t[:, 0:nh], SCR.t[:, 0:nh],
                         1.0 / (64.0 * 64.0), r=SCR.res, w=SCR.res)
                    b.op("v", "tensor_scalar_mul", wslice(wk_s), wslice(wk_s),
                         1.0 / 64.0, r=wres(wk_s), w=wres(wk_s))
                    for hh in range(nh):
                        b.op("v", "tensor_scalar_sub",
                             wslice(wk_s)[:, hh * 64:(hh + 1) * 64],
                             wslice(wk_s)[:, hh * 64:(hh + 1) * 64],
                             SCR.t[:, hh: hh + 1],
                             r=wres(wk_s) + SCR.res, w=wres(wk_s))
                ch.append(load_prep_k)

                def mk_proj(wsl, dst_ch, pair):
                    def go(wsl=wsl, dst_ch=dst_ch, pair=pair):
                        for psb, sl2 in proj2(wsl, pair):
                            b.op("v", "tensor_copy",
                                 CH.t[:, dst_ch * T + sl2 * 512:
                                      dst_ch * T + sl2 * 512 + 512],
                                 psb.t[:, 0:512], r=psb.res,
                                 w=CH.res[dst_ch:dst_ch + 1])
                    return go

                for pair in sl2_pairs():
                    ch.append(mk_proj(wq_s, qch, pair))
                for pair in sl2_pairs():
                    ch.append(mk_proj(wk_s, kch, pair))

                def mk_projv(pair):
                    def go(pair=pair, wv_s=wv_s):
                        segs = proj2(wv_s, pair)
                        nt = 4 * len(segs)
                        for j, (psb, sl2) in enumerate(segs):
                            b.op("v", "tensor_copy",
                                 TBF.t[:, j * 512:(j + 1) * 512], psb.t[:, 0:512],
                                 r=psb.res, w=TBF.res[j:j + 1])
                        for j, (psb, sl2) in enumerate(segs):
                            transpose_strip(TBF.t[:, j * 512:(j + 1) * 512],
                                            TBF.res[j:j + 1], 4 * j,
                                            start=(j == 0), stop=(j == len(segs) - 1))
                        t0 = segs[0][1] * 4
                        b.op("v", "tensor_copy",
                             vstv[:, t0:t0 + nt, :, 0:64],
                             PT.t[:, 0:nt * 128].rearrange("p (t h e) -> p t h e",
                                                           t=nt, h=2),
                             r=PT.res, w=VST.res[0:1])
                    return go

                vch = [mk_projv(pair) for pair in sl2_pairs()]

                def mk_sqvar(ch_src, stage, onesb_, sname):
                    def go(ch_src=ch_src, stage=stage, onesb_=onesb_, sname=sname,
                           sqch=sqch):
                        b.op("v", "tensor_mul", CH.t[:, sqch * T: sqch * T + T],
                             CH.t[:, ch_src * T: ch_src * T + T],
                             CH.t[:, ch_src * T: ch_src * T + T],
                             r=CH.res[ch_src:ch_src + 1], w=CH.res[sqch:sqch + 1])
                        banks = [PS4, PS5, PS6, PS7]
                        units = [(hs, g) for hs in range(2) for g in range(SL2)]

                        def mm(k):
                            hs, g = units[k]
                            pb_ = banks[k % 4]
                            b.op("t", "matmul", pb_.t[0:1, 0:512],
                                 onesb_.t[hs * 64:(hs + 1) * 64, 0:1],
                                 CH.t[hs * 64:(hs + 1) * 64,
                                      sqch * T + g * 512: sqch * T + g * 512 + 512],
                                 start=True, stop=True,
                                 r=CH.res[sqch:sqch + 1] + onesb_.res, w=pb_.res)

                        def evac(k):
                            hs, g = units[k]
                            pb_ = banks[k % 4]
                            dst = stage.t[0:1, hs * T + g * 512: hs * T + g * 512 + 512]
                            if k % 2 == 0:
                                b.op("s", "activation", dst, pb_.t[0:1, 0:512],
                                     ACT.Copy, r=pb_.res, w=stage.res[hs:hs + 1])
                            else:
                                b.op("v", "tensor_copy", dst, pb_.t[0:1, 0:512],
                                     r=pb_.res, w=stage.res[hs:hs + 1])

                        for k in range(len(units)):
                            if k >= 4:
                                evac(k - 4)
                            mm(k)
                        for k in range(len(units) - 4, len(units)):
                            evac(k)
                        for hs in range(2):
                            b.op("g", "dma_start", stream=f"s{sname}{hs}",
                                 r=CH.res[ch_src:ch_src + 1], w=stage.res[hs:hs + 1],
                                 out=stage.t[1:65, hs * T: hs * T + T],
                                 in_=CH.t[hs * 64: hs * 64 + 64,
                                          ch_src * T: ch_src * T + T])
                    return go

                vch.append(mk_sqvar(qch, QST, ONESA, "q"))
                vch.append(mk_sqvar(kch, KST, ONESB, "k"))
                return loads, ch, vch

            def attn_pair(c, inject):
                qch = c & 1
                for sl in range(NSL):
                    for grp in range(TT // 4):
                        for hs in range(2):
                            PSX = PSA if hs == 0 else PSB
                            for k in range(4):
                                tt = grp * 4 + k
                                b.op("t", "matmul", PSX.t[:, k * NQ:(k + 1) * NQ],
                                     KST.t[0:65, hs * T + tt * 128: hs * T + tt * 128 + 128],
                                     QST.t[0:65, hs * T + sl * NQ: hs * T + sl * NQ + NQ],
                                     start=(k % 2 == 0), stop=(k % 2 == 1),
                                     r=KST.res[hs:hs + 1] + QST.res[hs:hs + 1],
                                     w=PSX.res[k // 2:k // 2 + 1])
                            bci0 = hs * TT + grp * 4
                            b.op("s", "activation",
                                 BC.t[:, bci0 * NQ: (bci0 + 4) * NQ],
                                 PSX.t[:, 0:4 * NQ], ACT.Exp,
                                 r=PSX.res[0:2], w=BC.res[bci0:bci0 + 4])
                    yps = PS4 if sl % 2 == 0 else PS5
                    for hs in range(2):
                        for tt in range(TT):
                            bci = hs * TT + tt
                            b.op("t", "matmul", yps.t[0:65, hs * NQ:(hs + 1) * NQ],
                                 VST.t[:, tt * 130 + hs * 65: tt * 130 + hs * 65 + 65],
                                 BC.t[:, bci * NQ: (bci + 1) * NQ],
                                 start=(tt == 0), stop=(tt == TT - 1),
                                 r=VST.res[0:1] + BC.res[bci:bci + 1],
                                 w=yps.res)
                    b.op("v", "reciprocal", RRW.t[0:1, 0:2 * NQ],
                         yps.t[64:65, 0:2 * NQ], r=yps.res, w=RRW.res[0:2])
                    rb = TF.t[64:128, 0:2 * NQ]
                    b.op("q", "dma_start", stream=f"rb{sl & 1}",
                         r=RRW.res[0:2], w=TF.res[0:2],
                         out=rb.unsqueeze(1),
                         in_=RRW.t[0:1, 0:2 * NQ]
                         .unsqueeze(1).broadcast_to([1, 64, 2 * NQ]))
                    ycols = c * T + sl * NQ
                    b.op("v", "tensor_mul", BA.t[0:64, ycols: ycols + NQ],
                         yps.t[0:64, 0:NQ], TF.t[64:128, 0:NQ],
                         r=yps.res + TF.res[0:1],
                         w=BA.rr(ycols, ycols + NQ))
                    oslot = sl & 1
                    b.op("v", "tensor_mul",
                         ODD.t[0:64, oslot * NQ: oslot * NQ + NQ],
                         yps.t[0:64, NQ:2 * NQ], TF.t[64:128, NQ:2 * NQ],
                         r=yps.res + TF.res[1:2],
                         w=ODD.res[oslot:oslot + 1])
                    b.op("g", "dma_start", stream=f"od{oslot}",
                         r=ODD.res[oslot:oslot + 1], w=BA.rr(ycols, ycols + NQ),
                         out=BA.t[64:128, ycols: ycols + NQ],
                         in_=ODD.t[0:64, oslot * NQ: oslot * NQ + NQ])
                    if inject:
                        inject.pop(0)()

            loads, ch, vch = prep_chunks(0)
            for f in loads + ch + vch:
                f()
            for c in range(C):
                if c + 1 < C:
                    nloads, nxt, vnxt = prep_chunks(c + 1)
                    for f in nloads:
                        f()
                else:
                    nxt, vnxt = [], []
                attn_pair(c, nxt)
                for f in nxt:
                    f()
                for f in vnxt:
                    f()
            for d in range(C):
                wo_s = (d & 1)
                wslot_load(wo_s, "Wo", d)

                def epi(ps_ap, ps_res, sl2, sap, sres):
                    b.op("v", "tensor_copy", sap, ps_ap, r=ps_res, w=sres)

                fm_to_tm(d, wo_s,
                         lambda t0, nt: (BBv[:, t0:t0 + nt, :], BB.res[t0:t0 + nt]),
                         BA, epi)
            post_ln([(BB.t[:, tt * 1024: tt * 1024 + D], BB.res[tt:tt + 1])
                     for tt in range(TT)])

        def l0_ffn():
            pre_ln()
            for half, wnm, mu_off, dstbuf in (("r", "Wr", 0, BA), ("k", "Wkk", C, BB)):
                for d in range(C):
                    raw, w1s, w2s = (d & 1), 2 + (d & 1), 4 + (d & 1)
                    wslot_load(raw, wnm, d)
                    for c2 in range(C):
                        b.op("v", "tensor_scalar_mul", wap(w1s, c2), wap(raw, c2),
                             MUB.t[:, mu_off + c2: mu_off + c2 + 1],
                             r=wres(raw) + MUB.res[mu_off // C: mu_off // C + 1],
                             w=wres(w1s))
                    b.op("v", "tensor_sub", wslice(w2s), wslice(raw), wslice(w1s),
                         r=wres(raw) + wres(w1s), w=wres(w2s))
                    proj4(w1s, stop=False)
                    for c2 in range(C):
                        for sl2 in range(SL2):
                            tb, col = bank_ap(sl2)
                            wr_ps = tb.res[col // 512: col // 512 + 1]
                            if sl2 == 0:
                                b.op("t", "matmul", tb.t[:, col + 1:col + 512],
                                     wap(w2s, c2), XN.t[:, c2 * T: c2 * T + 511],
                                     start=False, stop=(c2 == C - 1),
                                     r=wres(w2s) + XN.res[c2:c2 + 1], w=wr_ps)
                            else:
                                b.op("t", "matmul", tb.t[:, col:col + 512],
                                     wap(w2s, c2),
                                     XN.t[:, c2 * T + sl2 * 512 - 1: c2 * T + sl2 * 512 + 511],
                                     start=False, stop=(c2 == C - 1),
                                     r=wres(w2s) + XN.res[c2:c2 + 1], w=wr_ps)
                    for tb, col0, wd, off in proj_segments():
                        dcol = d * T + off
                        if half == "r":
                            b.op("s", "activation", dstbuf.t[:, dcol: dcol + wd],
                                 tb.t[:, col0:col0 + wd], ACT.Sigmoid,
                                 r=tb.rr(col0, col0 + wd), w=dstbuf.rr(dcol, dcol + wd))
                        else:
                            b.op("v", "tensor_scalar_max", dstbuf.t[:, dcol: dcol + wd],
                                 tb.t[:, col0:col0 + wd], 0.0,
                                 r=tb.rr(col0, col0 + wd), w=dstbuf.rr(dcol, dcol + wd))
                            b.op("s", "activation", dstbuf.t[:, dcol: dcol + wd],
                                 dstbuf.t[:, dcol: dcol + wd], ACT.Square,
                                 r=dstbuf.rr(dcol, dcol + wd),
                                 w=dstbuf.rr(dcol, dcol + wd))
            LO = min(TT, 8)

            def tiles_of(t0, nt):
                if t0 < LO:
                    return (BCv[:, t0:t0 + nt, :], BC.rr(t0 * 1024, (t0 + nt) * 1024))
                return (CHv[:, t0 - LO:t0 - LO + nt, :],
                        CH.rr((t0 - LO) * 1024, (t0 - LO + nt) * 1024))

            for d in range(C):
                wv_s = (d & 1)
                wslot_load(wv_s, "Wvv", d)

                def epi(ps_ap, ps_res, sl2, sap, sres, d=d):
                    dcol = d * T + sl2 * 512
                    b.op("v", "tensor_mul", sap, ps_ap,
                         BA.t[:, dcol: dcol + 512],
                         r=ps_res + BA.rr(dcol, dcol + 512), w=sres)

                fm_to_tm(d, wv_s, tiles_of, BB, epi)
            src = [(BC.t[:, tt * 1024: tt * 1024 + D], BC.rr(tt * 1024, tt * 1024 + D))
                   if tt < LO else
                   (CH.t[:, (tt - LO) * 1024: (tt - LO) * 1024 + D],
                    CH.rr((tt - LO) * 1024, (tt - LO) * 1024 + D))
                   for tt in range(TT)]
            post_ln(src)

        def l1_attn():
            pre_ln(dest_tm=BA)
            vstv = VST.t[:, :].rearrange("p (t h e) -> p t h e", t=TT, h=2)

            def prep_chunks(c):
                wq_s, wk_s, wk2_s = (c & 1), 2 + (c & 1), 4 + (c & 1)
                qch, kch, kach = (c & 1), 2 + (c & 1), (c & 1)
                loads = [lambda: wslot_load(wq_s, "Wattn", c),
                         lambda: wslot_load(wk_s, "Wattn", C + c),
                         lambda: wslot_load(wk2_s, "Wk2", c)]
                ch = []

                def mk_proj(wsl, dst_ch, pair, scale):
                    def go(wsl=wsl, dst_ch=dst_ch, pair=pair, scale=scale):
                        for psb, sl2 in proj2(wsl, pair):
                            if scale is None:
                                b.op("v", "tensor_copy",
                                     CH.t[:, dst_ch * T + sl2 * 512:
                                          dst_ch * T + sl2 * 512 + 512],
                                     psb.t[:, 0:512], r=psb.res,
                                     w=CH.res[dst_ch:dst_ch + 1])
                            else:
                                b.op("v", "tensor_scalar_mul",
                                     CH.t[:, dst_ch * T + sl2 * 512:
                                          dst_ch * T + sl2 * 512 + 512],
                                     psb.t[:, 0:512], scale, r=psb.res,
                                     w=CH.res[dst_ch:dst_ch + 1])
                    return go

                for pair in sl2_pairs():
                    ch.append(mk_proj(wq_s, qch, pair, None))
                for pair in sl2_pairs():
                    ch.append(mk_proj(wk_s, kch, pair, 0.125))

                after = []

                def mk_projk2(pair):
                    def go(pair=pair, wk2_s=wk2_s, kach=kach):
                        for psb, sl2 in proj2(wk2_s, pair):
                            b.op("s", "activation",
                                 KST.t[:, kach * T + sl2 * 512:
                                       kach * T + sl2 * 512 + 512],
                                 psb.t[:, 0:512], ACT.Sigmoid, scale=0.0025,
                                 r=psb.res, w=KST.res[kach:kach + 1])
                    return go

                for pair in sl2_pairs():
                    after.append(mk_projk2(pair))

                def vstage(c=c):
                    b.op("v", "tensor_copy", vstv[:, :, :, 0:64],
                         BAv[:, :, c * 128:(c + 1) * 128]
                         .rearrange("p t (h e) -> p t h e", h=2),
                         r=BA.res[:TT], w=VST.res[0:1])
                after.append(vstage)
                return loads, ch, after

            def attn_pair(c, inject):
                qch, kch = (c & 1), 2 + (c & 1)
                for sl in range(NSL):
                    ntt = min(2 * sl + 2, TT)
                    for grp in range(-(-ntt // 4)):
                        rem = min(4, ntt - grp * 4)
                        for hs in range(2):
                            PSX = PSA if hs == 0 else PSB
                            for k in range(rem):
                                tt = grp * 4 + k
                                b.op("t", "matmul", PSX.t[:, k * NQ:(k + 1) * NQ],
                                     CH.t[hs * 64:(hs + 1) * 64,
                                          kch * T + tt * 128: kch * T + tt * 128 + 128],
                                     CH.t[hs * 64:(hs + 1) * 64,
                                          qch * T + sl * NQ: qch * T + sl * NQ + NQ],
                                     start=(k % 2 == 0),
                                     stop=(k % 2 == 1 or k == rem - 1),
                                     tile_position=(hs * 64, 0),
                                     r=CH.res[kch:kch + 1] + CH.res[qch:qch + 1],
                                     w=PSX.res[k // 2:k // 2 + 1])
                            bci0 = hs * TT + grp * 4
                            b.op("s", "activation",
                                 BC.t[:, bci0 * NQ: (bci0 + rem) * NQ],
                                 PSX.t[:, 0:rem * NQ], ACT.Exp,
                                 r=PSX.res[0:(rem + 1) // 2],
                                 w=BC.res[bci0:bci0 + rem])
                            for k in range(rem):
                                tt = grp * 4 + k
                                if tt >= 2 * sl:
                                    di = tt - 2 * sl
                                    bci = hs * TT + tt
                                    b.op("v", "tensor_mul",
                                         BC.t[:, bci * NQ: (bci + 1) * NQ],
                                         BC.t[:, bci * NQ: (bci + 1) * NQ],
                                         MASKS.t[:, 129 - 128 * di: 129 - 128 * di + NQ],
                                         r=BC.res[bci:bci + 1] + MASKS.res,
                                         w=BC.res[bci:bci + 1])
                    yps = PS4 if sl % 2 == 0 else PS5
                    for hs in range(2):
                        for tt in range(ntt):
                            bci = hs * TT + tt
                            b.op("t", "matmul", yps.t[0:65, hs * NQ:(hs + 1) * NQ],
                                 VST.t[:, tt * 130 + hs * 65: tt * 130 + hs * 65 + 65],
                                 BC.t[:, bci * NQ: (bci + 1) * NQ],
                                 start=(tt == 0), stop=(tt == ntt - 1),
                                 r=VST.res[0:1] + BC.res[bci:bci + 1],
                                 w=yps.res)
                    b.op("v", "reciprocal", RRW.t[0:1, 0:2 * NQ],
                         yps.t[64:65, 0:2 * NQ], r=yps.res, w=RRW.res[0:2])
                    rb = TF.t[64:128, 0:2 * NQ]
                    b.op("q", "dma_start", stream=f"rb{sl & 1}",
                         r=RRW.res[0:2], w=TF.res[0:2],
                         out=rb.unsqueeze(1),
                         in_=RRW.t[0:1, 0:2 * NQ]
                         .unsqueeze(1).broadcast_to([1, 64, 2 * NQ]))
                    ycols = c * T + sl * NQ
                    b.op("v", "tensor_mul", BB.t[0:64, ycols: ycols + NQ],
                         yps.t[0:64, 0:NQ], TF.t[64:128, 0:NQ],
                         r=yps.res + TF.res[0:1],
                         w=BB.rr(ycols, ycols + NQ))
                    oslot = sl & 1
                    b.op("v", "tensor_mul",
                         ODD.t[0:64, oslot * NQ: oslot * NQ + NQ],
                         yps.t[0:64, NQ:2 * NQ], TF.t[64:128, NQ:2 * NQ],
                         r=yps.res + TF.res[1:2],
                         w=ODD.res[oslot:oslot + 1])
                    b.op("g", "dma_start", stream=f"od{oslot}",
                         r=ODD.res[oslot:oslot + 1], w=BB.rr(ycols, ycols + NQ),
                         out=BB.t[64:128, ycols: ycols + NQ],
                         in_=ODD.t[0:64, oslot * NQ: oslot * NQ + NQ])
                    if inject:
                        inject.pop(0)()

            def wterm_pair(c, inject):
                qch, kch, kach = (c & 1), 2 + (c & 1), (c & 1)
                b.op("v", "tensor_sub", EST.t[:, 0: T - 1],
                     XN.t[:, c * T + 1: c * T + T], BB.t[:, c * T: c * T + T - 1],
                     r=XN.res[c:c + 1] + BB.rr(c * T, c * T + T), w=EST.res[0:1])
                b.op("v", "memset", EST.t[:, T - 1: T], 0.0, w=EST.res[0:1])
                for g in range(-(-TT // 8)):
                    nt = min(8, TT - g * 8)
                    for k in range(nt):
                        tt = g * 8 + k
                        b.op("t", "matmul", PT.t[:, k * 128:(k + 1) * 128],
                             EST.t[:, tt * 128:(tt + 1) * 128], ident_ap,
                             is_transpose=True, start=(k == 0), stop=(k == nt - 1),
                             r=EST.res[0:1] + IDENT.res, w=PT.res)
                    b.op("v", "tensor_copy",
                         EST.t[:, T + g * 1024: T + g * 1024 + nt * 128],
                         PT.t[:, 0:nt * 128], r=PT.res, w=EST.res[1:2])
                b.op("v", "tensor_scalar", EST.t[:, 0:T],
                     CH.t[:, qch * T: qch * T + T], 0.0, 0.125, ALU.min, ALU.mult,
                     r=CH.res[qch:qch + 1], w=EST.res[0:1])
                b.op("v", "tensor_scalar", CH.t[:, qch * T: qch * T + T],
                     CH.t[:, qch * T: qch * T + T], 0.0, 0.0025, ALU.max, ALU.mult,
                     r=CH.res[qch:qch + 1], w=CH.res[qch:qch + 1])
                b.op("v", "tensor_add", CH.t[:, qch * T: qch * T + T],
                     CH.t[:, qch * T: qch * T + T], EST.t[:, 0:T],
                     r=CH.res[qch:qch + 1] + EST.res[0:1], w=CH.res[qch:qch + 1])
                for sl in range(NSL):
                    ntt = min(2 * sl + 2, TT)
                    for grp in range(-(-ntt // 4)):
                        rem = min(4, ntt - grp * 4)
                        for hs in range(2):
                            PSX = PSA if hs == 0 else PSB
                            for k in range(rem):
                                tt = grp * 4 + k
                                b.op("t", "matmul", PSX.t[:, k * NQ:(k + 1) * NQ],
                                     KST.t[hs * 64:(hs + 1) * 64,
                                           kach * T + tt * 128: kach * T + tt * 128 + 128],
                                     CH.t[hs * 64:(hs + 1) * 64,
                                          qch * T + sl * NQ: qch * T + sl * NQ + NQ],
                                     start=(k % 2 == 0),
                                     stop=(k % 2 == 1 or k == rem - 1),
                                     tile_position=(hs * 64, 0),
                                     r=KST.res[kach:kach + 1] + CH.res[qch:qch + 1],
                                     w=PSX.res[k // 2:k // 2 + 1])
                            bci0 = hs * TT + grp * 4
                            kd = max(0, min(2 * sl - grp * 4, rem))
                            if kd > 0:
                                b.op("s", "activation",
                                     BC.t[:, bci0 * NQ: (bci0 + kd) * NQ],
                                     PSX.t[:, 0:kd * NQ], ACT.Copy,
                                     r=PSX.res[0:(kd + 1) // 2],
                                     w=BC.res[bci0:bci0 + kd])
                            for k in range(kd, rem):
                                tt = grp * 4 + k
                                di = tt - 2 * sl
                                bci = hs * TT + tt
                                b.op("v", "tensor_mul",
                                     BC.t[:, bci * NQ: (bci + 1) * NQ],
                                     PSX.t[:, k * NQ:(k + 1) * NQ],
                                     MASKS.t[:, 128 - 128 * di: 128 - 128 * di + NQ],
                                     r=PSX.res[k // 2:k // 2 + 1] + MASKS.res,
                                     w=BC.res[bci:bci + 1])
                    psy = PS4 if sl % 2 == 0 else PS5
                    for tt in range(ntt):
                        for hs in range(2):
                            bci = hs * TT + tt
                            b.op("t", "matmul",
                                 psy.t[hs * 64:(hs + 1) * 64, 0:NQ],
                                 EST.t[:, T + tt * 128 + hs * 64: T + tt * 128 + hs * 64 + 64],
                                 BC.t[:, bci * NQ: (bci + 1) * NQ],
                                 start=(tt == 0), stop=(tt == ntt - 1),
                                 tile_position=(0, hs * 64),
                                 r=EST.res[1:2] + BC.res[bci:bci + 1], w=psy.res)
                    zcols = c * T + sl * NQ
                    b.op("v", "tensor_add", BB.t[:, zcols: zcols + NQ],
                         BB.t[:, zcols: zcols + NQ], psy.t[:, 0:NQ],
                         r=psy.res + BB.rr(zcols, zcols + NQ),
                         w=BB.rr(zcols, zcols + NQ))
                    if inject:
                        inject.pop(0)()

            loads, ch, after = prep_chunks(0)
            for f in loads + ch + after:
                f()
            for c in range(C):
                if c + 1 < C:
                    nloads, nxt, anxt = prep_chunks(c + 1)
                    for f in nloads:
                        f()
                else:
                    nxt, anxt = [], []
                attn_pair(c, nxt)
                for f in nxt:
                    f()
                wterm_pair(c, anxt)
                for f in anxt:
                    f()
            for d in range(C):
                wp_s = (d & 1)
                wslot_load(wp_s, "Wproj", d)

                def epi(ps_ap, ps_res, sl2, sap, sres):
                    b.op("v", "tensor_copy", sap, ps_ap, r=ps_res, w=sres)

                fm_to_tm(d, wp_s,
                         lambda t0, nt: (BAv[:, t0:t0 + nt, :], BA.res[t0:t0 + nt]),
                         BB, epi)
            post_ln([(BA.t[:, tt * 1024: tt * 1024 + D], BA.res[tt:tt + 1])
                     for tt in range(TT)])

        def l1_ffn():
            pre_ln()
            for d in range(C):
                ws = (d & 1)
                wslot_load(ws, "Wsign", d)
                b.op("v", "tensor_scalar", wslice(ws), wslice(ws), 0.0, None,
                     ALU.is_ge, r=wres(ws), w=wres(ws))
                b.op("v", "tensor_scalar", wslice(ws), wslice(ws), 2.0, -1.0,
                     ALU.mult, ALU.add, r=wres(ws), w=wres(ws))

                def epi(ps_ap, ps_res, sl2, sap, sres, d=d):
                    b.op("v", "tensor_scalar", TF.t[:, 0:512],
                         ps_ap, MUB.t[:, 2 * C + d: 2 * C + d + 1],
                         MUB.t[:, 3 * C + d: 3 * C + d + 1], ALU.mult, ALU.add,
                         r=ps_res + MUB.res[2:4], w=TF.res[0:2])
                    b.op("v", "tensor_scalar", sap, TF.t[:, 0:512],
                         100.0, -100.0, ALU.min, ALU.max,
                         r=TF.res[0:2], w=sres)

                fm_to_tm(d, ws,
                         lambda t0, nt: (BAv[:, t0:t0 + nt, :], BA.res[t0:t0 + nt]),
                         XN, epi)
            post_ln([(BA.t[:, tt * 1024: tt * 1024 + D], BA.res[tt:tt + 1])
                     for tt in range(TT)])

        for name, buf in (("ident", IDENT), ("masks", MASKS), ("onesa", ONESA),
                          ("onesb", ONESB), ("ones64f", ONES64F)):
            b.op("q", "dma_start", w=buf.res, stream="qc",
                 out=buf.t[:, :], in_=dram[name][:, :])
        for i, nm in enumerate(("mur_t", "muk_t", "scale_t", "bias_t")):
            b.op("q", "dma_start", w=MUB.res[i:i + 1], stream="qc",
                 out=MUB.t[:, i * C:(i + 1) * C], in_=dram[nm][:, :])
        vv_ones = VST.t[:, :].rearrange("p (t h e) -> p t h e", t=TT, h=2)
        b.op("v", "memset", vv_ones[:, :, :, 64:65], 1.0, w=VST.res[0:1])

        SUBS = {"l0a": l0_attn, "l0f": l0_ffn, "l1a": l1_attn, "l1f": l1_ffn}
        for bi in range(cfg.B):
            for tt in range(TT):
                b.op("g", "dma_start", w=X.res[tt:tt + 1], stream="gx",
                     out=X.t[:, tt * 1024: tt * 1024 + D],
                     in_=dram["x"][bi, tt * 128:(tt + 1) * 128, :])
            for sbk in cfg.sub_blocks:
                SUBS[sbk]()
            for tt in range(TT):
                b.op("g", "dma_start", r=X.res[tt:tt + 1], stream="go",
                     out=out_d[bi, tt * 128:(tt + 1) * 128, :],
                     in_=X.t[:, tt * 1024: tt * 1024 + D])

        b.op("g", None, deps=[("d", "go", b.streams.get("go", 0))])
        b.op("q", None, deps=[("d", "go", b.streams.get("go", 0))])

        b.finalize()
        with contextlib.ExitStack() as sctx:
            ctr_sems = {e: sctx.enter_context(nc.semaphore(f"c_{e}")) for e in ENGINES}
            stream_sems = {s: sctx.enter_context(nc.semaphore(f"d_{s}"))
                           for s in b.streams}
            with nc.Block() as block:
                @block.tensor
                def _(eng):
                    b.replay("t", eng, ctr_sems, stream_sems)

                @block.vector
                def _(eng):
                    b.replay("v", eng, ctr_sems, stream_sems)

                @block.scalar
                def _(eng):
                    b.replay("s", eng, ctr_sems, stream_sems)

                @block.gpsimd
                def _(eng):
                    b.replay("g", eng, ctr_sems, stream_sems)

                @block.sync
                def _(eng):
                    b.replay("q", eng, ctr_sems, stream_sems)

    return nc



def prep_shared(cfg, inputs):
    shared = {}
    shared["Wq"] = tile_w(inputs["L0_attn_Wq"])
    shared["Wk"] = tile_w(inputs["L0_attn_Wk"])
    shared["Wv"] = tile_w(inputs["L0_attn_Wv"])
    shared["Wo"] = tile_w(inputs["L0_attn_Wo"])
    shared["Wr"] = tile_w(np.ascontiguousarray(np.asarray(inputs["L0_ffn_W_r"]).T))
    shared["Wkk"] = tile_w(np.ascontiguousarray(np.asarray(inputs["L0_ffn_W_k"]).T))
    shared["Wvv"] = tile_w(np.ascontiguousarray(np.asarray(inputs["L0_ffn_W_v"]).T))
    shared["Wattn"] = tile_w(inputs["L1_attn_W_attn"])
    shared["Wk2"] = tile_w(inputs["L1_attn_W_k2"])
    shared["Wproj"] = tile_w(inputs["L1_attn_W_proj"])
    shared["Wsign"] = tile_w(np.ascontiguousarray(np.asarray(inputs["L1_ffn_weights"]).T))
    shared["mur_t"] = vec_tile(inputs["L0_ffn_mu_r"])
    shared["muk_t"] = vec_tile(inputs["L0_ffn_mu_k"])
    shared["scale_t"] = vec_tile(inputs["L1_ffn_scale"])
    shared["bias_t"] = vec_tile(inputs["L1_ffn_bias"])
    shared.update(host_consts(cfg))
    return shared


def kernel(**inputs):
    cfg = Cfg()
    shared = prep_shared(cfg, inputs)
    x = np.ascontiguousarray(np.asarray(inputs["L0_attn_x"], np.float32))
    nb_total = x.shape[0]
    ncores = 8
    per = nb_total // ncores
    specs = {k: (v.shape, v.dtype) for k, v in shared.items()}
    specs["x"] = ((per, cfg.T, cfg.D), np.float32)
    nc = build_program(cfg, specs)
    in_maps = []
    for i in range(ncores):
        m = dict(shared)
        m["x"] = np.ascontiguousarray(x[i * per:(i + 1) * per])
        in_maps.append(m)
    res = run_bass_kernel_spmd(nc, in_maps, core_ids=list(range(ncores)))
    out = np.concatenate([np.asarray(r["out"]) for r in res.results], axis=0)
    return out.astype(np.float32)


# version 15
# speedup vs baseline: 1.0252x; 1.0252x over previous
import contextlib
import numpy as np
import ml_dtypes
import concourse.bass as bass
import concourse.mybir as mybir
from concourse.bass_utils import run_bass_kernel_spmd

F32 = mybir.dt.float32
BF16 = mybir.dt.bfloat16
AX = mybir.AxisListType.X
ALU = mybir.AluOpType
ACT = mybir.ActivationFunctionType
BF = ml_dtypes.bfloat16

ENGINES = ("t", "v", "s", "g", "q")


class Cfg:
    def __init__(self, B=2, T=2048, D=1024, H=16, NQ=256,
                 sub_blocks=("l0a", "l0f", "l1a", "l1f")):
        self.B, self.T, self.D, self.H, self.NQ = B, T, D, H, NQ
        self.DH = D // H
        assert self.DH == 64 and NQ == 256
        self.C = D // 128
        self.TT = T // 128
        self.NSL = T // NQ
        self.SL2 = T // 512
        self.sub_blocks = tuple(sub_blocks)


class Res:
    __slots__ = ("w", "rs")

    def __init__(self):
        self.w = None
        self.rs = []


class Buf:
    def __init__(self, t, ncols, gran):
        self.t = t
        self.gran = gran
        self.res = [Res() for _ in range(-(-ncols // gran))]

    def rr(self, lo, hi):
        return self.res[lo // self.gran: -(-hi // self.gran)]


class Bld:
    def __init__(self):
        self.ops = {e: [] for e in ENGINES}
        self.streams = {}
        self.need_inc = {e: set() for e in ENGINES}

    def op(self, eng, fname, *args, r=(), w=(), deps=(), stream=None, **kw):
        deph = []
        for res in r:
            if res.w is not None:
                deph.append(res.w)
        for res in w:
            if res.w is not None:
                deph.append(res.w)
            deph.extend(res.rs)
        deph.extend(deps)
        deph = [h if h[0] == "e" else ("d", h[1], self.streams[h[1]]) for h in deph]
        seen, dd = set(), []
        for h in deph:
            if h not in seen:
                seen.add(h)
                dd.append(h)
        idx = len(self.ops[eng])
        if stream is not None:
            self.streams[stream] = self.streams.get(stream, 0) + 1
            handle = ("d", stream, self.streams[stream])
        else:
            handle = ("e", eng, idx)
        if eng == "t":
            dd = [h for h in dd if not (h[0] == "e" and h[1] == "t")]
        self.ops[eng].append(dict(f=fname, a=args, k=kw, deps=dd, stream=stream))
        for h in dd:
            if h[0] == "e":
                self.need_inc[h[1]].add(h[2])
        for res in r:
            res.rs.append(handle)
        for res in w:
            res.w = handle
            res.rs = []
        return handle

    def finalize(self):
        self.prefix = {}
        for e in ENGINES:
            pr, c = [], 0
            for i in range(len(self.ops[e])):
                if i in self.need_inc[e]:
                    c += 1
                pr.append(c)
            self.prefix[e] = pr

    def replay(self, eng, obj, ctr_sems, stream_sems):
        covered = {}
        for i, rec in enumerate(self.ops[eng]):
            for h in rec["deps"]:
                if h[0] == "e":
                    key, val, sem = ("e", h[1]), self.prefix[h[1]][h[2]], ctr_sems[h[1]]
                else:
                    key, val, sem = ("d", h[1]), 16 * h[2], stream_sems[h[1]]
                if covered.get(key, 0) < val:
                    obj.wait_ge(sem, val)
                    covered[key] = val
            if rec["f"] is None:
                continue
            try:
                instr = getattr(obj, rec["f"])(*rec["a"], **rec["k"])
            except Exception:
                import sys
                print(f"REPLAY FAIL {eng}[{i}] {rec['f']} args={[getattr(a,'shape',a) for a in rec['a']]}",
                      file=sys.stderr)
                raise
            if rec["stream"] is not None:
                instr.then_inc(stream_sems[rec["stream"]], 16)
            elif i in self.need_inc[eng]:
                instr.then_inc(ctr_sems[eng], 1)



def tile_w(W):
    Din, Dout = W.shape
    return np.ascontiguousarray(
        np.asarray(W, np.float32).reshape(Din // 128, 128, Dout // 128, 128)
        .transpose(2, 0, 1, 3))


def vec_tile(v):
    return np.ascontiguousarray(np.asarray(v, np.float32).reshape(-1, 128).T)


def host_consts(cfg):
    NQ = cfg.NQ
    c = {}
    c["ident"] = np.eye(128, dtype=BF)
    tp = np.arange(128)[:, None]
    u = np.arange(NQ + 130)[None, :]
    c["masks"] = np.ascontiguousarray((u - 129 - tp >= 0).astype(np.float32)).astype(BF)
    c["onesa"] = np.full((128, 1), 1.0 / 63.0, dtype=BF)
    c["onesb"] = np.full((128, 1), 4096.0 / (63.0 * 64.0), dtype=BF)
    c["ones64f"] = np.ones((1, 64), dtype=np.float32)
    return c



def build_program(cfg, input_specs):
    nc = bass.Bass("TRN2", target_bir_lowering=False)
    dram = {}
    for name, (shape, dt) in input_specs.items():
        mdt = F32 if np.dtype(dt) == np.float32 else BF16
        dram[name] = nc.dram_tensor(name, list(shape), mdt, kind="ExternalInput").ap()
    out_d = nc.dram_tensor("out", [cfg.B, cfg.T, cfg.D], F32, kind="ExternalOutput").ap()

    b = Bld()
    T, D, C, TT, NQ, NSL, SL2 = (cfg.T, cfg.D, cfg.C, cfg.TT, cfg.NQ,
                                 cfg.NSL, cfg.SL2)
    ctx = contextlib.ExitStack()

    def S(name, shape, dt=BF16, gran=None):
        t = ctx.enter_context(nc.sbuf_tensor(name, list(shape), dt))
        return Buf(t, shape[-1], gran or shape[-1])

    with ctx:
        X = S("X", [128, TT * 1024], gran=1024)
        XN = S("XN", [128, C * T], gran=T)
        BA = S("BA", [128, max(C * T, TT * 1024)], gran=1024)
        BB = S("BB", [128, max(C * T, TT * 1024)], gran=1024)
        BC = S("BCb", [128, max(2 * TT * NQ, min(TT, 8) * 1024)], gran=NQ)
        CH = S("CH", [128, 5 * T], gran=T)
        VST = S("VST", [128, TT * 130], gran=TT * 130)
        QST = S("QST", [128, 2 * T], gran=T)
        EST = QST
        KST = S("KST", [128, 2 * T], gran=T)
        ODD = S("ODD", [64, 2 * NQ], gran=NQ)
        WP = S("WP", [128, 6 * 1024], gran=1024)
        TBF = S("TBF", [128, 2 * 512], gran=512)
        XT = S("XT", [128, 2 * 1024], gran=1024)
        TF = S("TF", [128, 512], F32, gran=256)
        RRW = TF
        STT = S("STT", [128, 2 * 12], F32, gran=12)
        MV = S("MV", [128, 2 * TT], F32, gran=1)
        RSD = S("RSD", [128, TT], F32)
        BIASN = S("BIASN", [128, TT], F32)
        SCR = S("SCR", [128, 3 * TT], F32)

        MUB = S("MUB", [128, 4 * C], F32, gran=C)
        IDENT = S("IDENTB", [128, 128])
        MASKS = S("MASKSB", [128, NQ + 130])
        ONESA = S("ONESAB", [128, 1])
        ONESB = S("ONESBB", [128, 1])
        ONES64F = S("ONES64FB", [1, 64], F32)

        PSA = Buf(ctx.enter_context(nc.psum_tensor("psa", [128, 1024], F32)), 1024, 512)
        PSB = Buf(ctx.enter_context(nc.psum_tensor("psb", [128, 1024], F32)), 1024, 512)
        PS4 = Buf(ctx.enter_context(nc.psum_tensor("ps4", [128, 512], F32)), 512, 512)
        PS5 = Buf(ctx.enter_context(nc.psum_tensor("ps5", [128, 512], F32)), 512, 512)
        PS6 = Buf(ctx.enter_context(nc.psum_tensor("ps6", [128, 512], F32)), 512, 512)
        PS7 = Buf(ctx.enter_context(nc.psum_tensor("ps7", [128, 512], F32)), 512, 512)
        PT = Buf(PS7.t[:, 0:512].bitcast(BF16), 1024, 1024)
        PT.res = PS7.res

        ident_ap = IDENT.t[:, :]
        MVm = lambda tt: MV.t[:, tt:tt + 1]
        BAv = BA.t[:, 0:TT * 1024].rearrange("p (t j) -> p t j", j=1024)
        BBv = BB.t[:, 0:TT * 1024].rearrange("p (t j) -> p t j", j=1024)
        BCv = BC.t[:, 0:min(TT, 8) * 1024].rearrange("p (t j) -> p t j", j=1024)
        CHv = CH.t[:, 0:(TT - min(TT, 8)) * 1024].rearrange("p (t j) -> p t j", j=1024) if TT > 8 else None
        XNv = XN.t.rearrange("p (c j) -> p c j", c=C)

        def wslot_load(slot, wname, d, nchunk=None):
            nck = nchunk or C
            src = dram[wname][d].transpose([1, 0, 2])
            dst = WP.t[:, slot * 1024: slot * 1024 + nck * 128]
            b.op("g", "dma_start", w=WP.res[slot:slot + 1], stream=f"wg{slot}",
                 out=dst.rearrange("p (c j) -> p c j", c=nck), in_=src)

        def wap(slot, c2):
            return WP.t[:, slot * 1024 + c2 * 128: slot * 1024 + (c2 + 1) * 128]

        def wslice(slot, ncols=None):
            return WP.t[:, slot * 1024: slot * 1024 + (ncols or C * 128)]

        def wres(slot):
            return WP.res[slot:slot + 1]

        def bank_ap(sl2):
            tb = PSA if (sl2 % 4) < 2 else PSB
            return tb, (sl2 % 2) * 512

        def rstd_chain():
            b.op("v", "tensor_scalar_add", SCR.t[:, 0:TT], MV.t[:, TT:2 * TT], 1e-5,
                 r=MV.res[TT:2 * TT], w=SCR.res)
            b.op("s", "activation", SCR.t[:, TT:2 * TT], SCR.t[:, 0:TT], ACT.Sqrt,
                 r=SCR.res, w=SCR.res)
            b.op("v", "reciprocal", SCR.t[:, 2 * TT:3 * TT], SCR.t[:, TT:2 * TT],
                 r=SCR.res, w=SCR.res)
            b.op("v", "tensor_mul", RSD.t[:, :], SCR.t[:, 2 * TT:3 * TT],
                 SCR.t[:, 2 * TT:3 * TT], r=SCR.res, w=RSD.res)
            b.op("v", "tensor_mul", RSD.t[:, :], RSD.t[:, :], SCR.t[:, 0:TT],
                 r=SCR.res + RSD.res, w=RSD.res)
            b.op("v", "tensor_scalar", SCR.t[:, 0:TT], RSD.t[:, :], -0.5, 1.5,
                 ALU.mult, ALU.add, r=RSD.res, w=SCR.res)
            b.op("v", "tensor_mul", RSD.t[:, :], SCR.t[:, 2 * TT:3 * TT],
                 SCR.t[:, 0:TT], r=SCR.res, w=RSD.res)

        def ln_stats(tile_aps):
            ng = -(-D // 512)
            for tt in range(TT):
                ap, res_l = tile_aps[tt]
                st = tt & 1
                for g in range(ng):
                    wseg = min(512, D - g * 512)
                    b.op("v", "bn_stats",
                         STT.t[:, st * 12 + g * 6: st * 12 + g * 6 + 6],
                         ap[:, g * 512: g * 512 + wseg], r=res_l, w=STT.res[st:st + 1])
                mvout = MV.t.rearrange("p (two t) -> p two t", two=2)[:, :, tt]
                b.op("v", "bn_aggr", mvout, STT.t[:, st * 12: st * 12 + ng * 6],
                     r=STT.res[st:st + 1],
                     w=MV.res[tt:tt + 1] + MV.res[TT + tt:TT + tt + 1])
            rstd_chain()
            b.op("v", "scalar_tensor_tensor", BIASN.t[:, :], MV.t[:, 0:TT], -1.0,
                 RSD.t[:, :], ALU.mult, ALU.mult,
                 r=MV.res[0:TT] + RSD.res, w=BIASN.res)

        def pre_ln(dest_tm=None):
            dest_tm = dest_tm or BA
            ln_stats([(X.t[:, tt * 1024: tt * 1024 + D], X.res[tt:tt + 1])
                      for tt in range(TT)])
            for tt in range(TT):
                dst_ap = dest_tm.t[:, tt * 1024: tt * 1024 + D]
                dst_res = dest_tm.rr(tt * 1024, tt * 1024 + D)
                b.op("s", "activation", dst_ap, X.t[:, tt * 1024: tt * 1024 + D],
                     ACT.Identity, bias=BIASN.t[:, tt:tt + 1],
                     scale=RSD.t[:, tt:tt + 1],
                     r=list(X.res[tt:tt + 1]) + BIASN.res + RSD.res,
                     w=dst_res)
                ngrp = -(-C // 8)
                for gi in range(ngrp):
                    cs = list(range(gi * 8, min(C, gi * 8 + 8)))
                    for k, c2 in enumerate(cs):
                        b.op("t", "matmul", PT.t[:, k * 128:(k + 1) * 128],
                             dst_ap[:, c2 * 128:(c2 + 1) * 128], ident_ap,
                             is_transpose=True, start=(k == 0), stop=(k == len(cs) - 1),
                             r=list(dst_res) + IDENT.res, w=PT.res)
                    b.op("v", "tensor_copy",
                         XNv[:, gi * 8: gi * 8 + len(cs), tt * 128: tt * 128 + 128],
                         PT.t[:, 0: len(cs) * 128].rearrange("p (c j) -> p c j", c=len(cs)),
                         r=PT.res, w=XN.res[gi * 8: gi * 8 + len(cs)])

        def post_ln(src_tiles):
            ln_stats(src_tiles)
            for tt in range(TT):
                ap, res_l = src_tiles[tt]
                xs = tt & 1
                b.op("s", "activation", XT.t[:, xs * 1024: xs * 1024 + D],
                     ap, ACT.Identity, bias=BIASN.t[:, tt:tt + 1],
                     scale=RSD.t[:, tt:tt + 1],
                     r=list(res_l) + BIASN.res + RSD.res,
                     w=XT.res[xs:xs + 1])
                b.op("v", "tensor_add", X.t[:, tt * 1024: tt * 1024 + D],
                     X.t[:, tt * 1024: tt * 1024 + D],
                     XT.t[:, xs * 1024: xs * 1024 + D],
                     r=XT.res[xs:xs + 1] + X.res[tt:tt + 1], w=X.res[tt:tt + 1])

        def proj4(wslot, rhs=None, stop=True):
            src = rhs or XN
            for c2 in range(C):
                for sl2 in range(SL2):
                    tb, col = bank_ap(sl2)
                    b.op("t", "matmul", tb.t[:, col:col + 512], wap(wslot, c2),
                         src.t[:, c2 * T + sl2 * 512: c2 * T + sl2 * 512 + 512],
                         start=(c2 == 0), stop=(stop and c2 == C - 1),
                         r=wres(wslot) + src.rr(c2 * T + sl2 * 512,
                                                c2 * T + sl2 * 512 + 512),
                         w=tb.res[col // 512: col // 512 + 1])

        def proj_segments():
            segs = []
            off = 0
            while off < SL2 * 512:
                tb = PSA if (off // 512) % 4 < 2 else PSB
                wd = min(1024, SL2 * 512 - off)
                segs.append((tb, 0, wd, off))
                off += wd
            return segs

        def proj2(wslot, sl2s, rhs=None):
            src = rhs or XN
            out = []
            for j, sl2 in enumerate(sl2s):
                out.append((PS6 if j == 0 else PS7, sl2))
            for c2 in range(C):
                for j, sl2 in enumerate(sl2s):
                    psb = PS6 if j == 0 else PS7
                    b.op("t", "matmul", psb.t[:, 0:512], wap(wslot, c2),
                         src.t[:, c2 * T + sl2 * 512: c2 * T + sl2 * 512 + 512],
                         start=(c2 == 0), stop=(c2 == C - 1),
                         r=wres(wslot) + src.rr(c2 * T + sl2 * 512,
                                                c2 * T + sl2 * 512 + 512),
                         w=psb.res)
            return out

        def sl2_pairs():
            return [tuple(range(p, min(p + 2, SL2))) for p in range(0, SL2, 2)]

        def transpose_strip(src_ap, src_res, k0, start, stop):
            for k in range(4):
                b.op("t", "matmul", PT.t[:, (k0 + k) * 128:(k0 + k + 1) * 128],
                     src_ap[:, k * 128:(k + 1) * 128], ident_ap, is_transpose=True,
                     start=(start and k == 0), stop=(stop and k == 3),
                     r=list(src_res) + IDENT.res, w=PT.res)

        def fm_to_tm(d, wslot, tiles_of, rhs_buf, epilogue):
            proj4(wslot, rhs=rhs_buf)
            ngrp = 0
            for sl2 in range(SL2):
                tb, col = bank_ap(sl2)
                ps_ap = tb.t[:, col:col + 512]
                ps_res = tb.res[col // 512: col // 512 + 1]
                strip_res = TBF.res[sl2 & 1: (sl2 & 1) + 1]
                strip_ap = TBF.t[:, (sl2 & 1) * 512: (sl2 & 1) * 512 + 512]
                epilogue(ps_ap, ps_res, sl2, strip_ap, strip_res)
                close = (ngrp == 1) or (sl2 == SL2 - 1)
                transpose_strip(strip_ap, strip_res, 4 * ngrp,
                                start=(ngrp == 0), stop=close)
                ngrp += 1
                if close:
                    nt = 4 * ngrp
                    t0 = sl2 * 4 - (nt - 4)
                    dstv, dres = tiles_of(t0, nt)
                    b.op("v", "tensor_copy",
                         dstv[:, :, d * 128:(d + 1) * 128],
                         PT.t[:, 0:nt * 128].rearrange("p (c j) -> p c j", c=nt),
                         r=PT.res, w=dres)
                    ngrp = 0

        def l0_attn():
            pre_ln()
            vstv = VST.t[:, :].rearrange("p (t h e) -> p t h e", t=TT, h=2)
            nh = 2 * C

            def prep_chunks(c):
                wq_s, wk_s, wv_s = (c & 1), 2 + (c & 1), 4 + (c & 1)
                qch, kch, sqch = (c & 1), 2 + (c & 1), 4
                loads = [lambda: wslot_load(wq_s, "Wq", c),
                         lambda: wslot_load(wk_s, "Wk", c),
                         lambda: wslot_load(wv_s, "Wv", c)]
                ch = []

                def load_prep_q(c=c, wq_s=wq_s):
                    w3 = wslice(wq_s).rearrange("p (h e) -> p h e", e=64)
                    b.op("v", "tensor_reduce", SCR.t[:, 0:nh], w3, AX, ALU.add,
                         r=wres(wq_s), w=SCR.res)
                    b.op("v", "tensor_scalar_mul", SCR.t[:, 0:nh], SCR.t[:, 0:nh],
                         1.0 / 64.0, r=SCR.res, w=SCR.res)
                    b.op("v", "tensor_sub", w3, w3,
                         SCR.t[:, 0:nh].unsqueeze(2).broadcast_to([128, nh, 64]),
                         r=wres(wq_s) + SCR.res, w=wres(wq_s))
                ch.append(load_prep_q)

                def load_prep_k(c=c, wk_s=wk_s):
                    k3 = wslice(wk_s).rearrange("p (h e) -> p h e", e=64)
                    b.op("v", "tensor_reduce", SCR.t[:, 0:nh], k3, AX, ALU.add,
                         r=wres(wk_s), w=SCR.res)
                    b.op("v", "tensor_scalar_mul", SCR.t[:, 0:nh], SCR.t[:, 0:nh],
                         1.0 / (64.0 * 64.0), r=SCR.res, w=SCR.res)
                    b.op("v", "tensor_scalar_mul", wslice(wk_s), wslice(wk_s),
                         1.0 / 64.0, r=wres(wk_s), w=wres(wk_s))
                    b.op("v", "tensor_sub", k3, k3,
                         SCR.t[:, 0:nh].unsqueeze(2).broadcast_to([128, nh, 64]),
                         r=wres(wk_s) + SCR.res, w=wres(wk_s))
                ch.append(load_prep_k)

                def mk_proj(wsl, dst_ch, pair):
                    def go(wsl=wsl, dst_ch=dst_ch, pair=pair):
                        for psb, sl2 in proj2(wsl, pair):
                            b.op("v", "tensor_copy",
                                 CH.t[:, dst_ch * T + sl2 * 512:
                                      dst_ch * T + sl2 * 512 + 512],
                                 psb.t[:, 0:512], r=psb.res,
                                 w=CH.res[dst_ch:dst_ch + 1])
                    return go

                for pair in sl2_pairs():
                    ch.append(mk_proj(wq_s, qch, pair))
                for pair in sl2_pairs():
                    ch.append(mk_proj(wk_s, kch, pair))

                def mk_projv(pair):
                    def go(pair=pair, wv_s=wv_s):
                        segs = proj2(wv_s, pair)
                        nt = 4 * len(segs)
                        for j, (psb, sl2) in enumerate(segs):
                            b.op("v", "tensor_copy",
                                 TBF.t[:, j * 512:(j + 1) * 512], psb.t[:, 0:512],
                                 r=psb.res, w=TBF.res[j:j + 1])
                        for j, (psb, sl2) in enumerate(segs):
                            transpose_strip(TBF.t[:, j * 512:(j + 1) * 512],
                                            TBF.res[j:j + 1], 4 * j,
                                            start=(j == 0), stop=(j == len(segs) - 1))
                        t0 = segs[0][1] * 4
                        b.op("v", "tensor_copy",
                             vstv[:, t0:t0 + nt, :, 0:64],
                             PT.t[:, 0:nt * 128].rearrange("p (t h e) -> p t h e",
                                                           t=nt, h=2),
                             r=PT.res, w=VST.res[0:1])
                    return go

                vch = [mk_projv(pair) for pair in sl2_pairs()]

                def mk_sqvar(ch_src, stage, onesb_, sname):
                    def go(ch_src=ch_src, stage=stage, onesb_=onesb_, sname=sname,
                           sqch=sqch):
                        b.op("v", "tensor_mul", CH.t[:, sqch * T: sqch * T + T],
                             CH.t[:, ch_src * T: ch_src * T + T],
                             CH.t[:, ch_src * T: ch_src * T + T],
                             r=CH.res[ch_src:ch_src + 1], w=CH.res[sqch:sqch + 1])
                        banks = [PS4, PS5, PS6, PS7]
                        units = [(hs, g) for hs in range(2) for g in range(SL2)]

                        def mm(k):
                            hs, g = units[k]
                            pb_ = banks[k % 4]
                            b.op("t", "matmul", pb_.t[0:1, 0:512],
                                 onesb_.t[hs * 64:(hs + 1) * 64, 0:1],
                                 CH.t[hs * 64:(hs + 1) * 64,
                                      sqch * T + g * 512: sqch * T + g * 512 + 512],
                                 start=True, stop=True,
                                 r=CH.res[sqch:sqch + 1] + onesb_.res, w=pb_.res)

                        def evac(k):
                            hs, g = units[k]
                            pb_ = banks[k % 4]
                            dst = stage.t[0:1, hs * T + g * 512: hs * T + g * 512 + 512]
                            if k % 2 == 0:
                                b.op("s", "activation", dst, pb_.t[0:1, 0:512],
                                     ACT.Copy, r=pb_.res, w=stage.res[hs:hs + 1])
                            else:
                                b.op("v", "tensor_copy", dst, pb_.t[0:1, 0:512],
                                     r=pb_.res, w=stage.res[hs:hs + 1])

                        for k in range(len(units)):
                            if k >= 4:
                                evac(k - 4)
                            mm(k)
                        for k in range(len(units) - 4, len(units)):
                            evac(k)
                        for hs in range(2):
                            b.op("g", "dma_start", stream=f"s{sname}{hs}",
                                 r=CH.res[ch_src:ch_src + 1], w=stage.res[hs:hs + 1],
                                 out=stage.t[1:65, hs * T: hs * T + T],
                                 in_=CH.t[hs * 64: hs * 64 + 64,
                                          ch_src * T: ch_src * T + T])
                    return go

                vch.append(mk_sqvar(qch, QST, ONESA, "q"))
                vch.append(mk_sqvar(kch, KST, ONESB, "k"))
                return loads, ch, vch

            def attn_pair(c, inject):
                qch = c & 1
                for sl in range(NSL):
                    for grp in range(TT // 4):
                        for hs in range(2):
                            PSX = PSA if hs == 0 else PSB
                            for k in range(4):
                                tt = grp * 4 + k
                                b.op("t", "matmul", PSX.t[:, k * NQ:(k + 1) * NQ],
                                     KST.t[0:65, hs * T + tt * 128: hs * T + tt * 128 + 128],
                                     QST.t[0:65, hs * T + sl * NQ: hs * T + sl * NQ + NQ],
                                     start=(k % 2 == 0), stop=(k % 2 == 1),
                                     r=KST.res[hs:hs + 1] + QST.res[hs:hs + 1],
                                     w=PSX.res[k // 2:k // 2 + 1])
                            bci0 = hs * TT + grp * 4
                            b.op("s", "activation",
                                 BC.t[:, bci0 * NQ: (bci0 + 4) * NQ],
                                 PSX.t[:, 0:4 * NQ], ACT.Exp,
                                 r=PSX.res[0:2], w=BC.res[bci0:bci0 + 4])
                    yps = PS4 if sl % 2 == 0 else PS5
                    for hs in range(2):
                        for tt in range(TT):
                            bci = hs * TT + tt
                            b.op("t", "matmul", yps.t[0:65, hs * NQ:(hs + 1) * NQ],
                                 VST.t[:, tt * 130 + hs * 65: tt * 130 + hs * 65 + 65],
                                 BC.t[:, bci * NQ: (bci + 1) * NQ],
                                 start=(tt == 0), stop=(tt == TT - 1),
                                 r=VST.res[0:1] + BC.res[bci:bci + 1],
                                 w=yps.res)
                    b.op("v", "reciprocal", RRW.t[0:1, 0:2 * NQ],
                         yps.t[64:65, 0:2 * NQ], r=yps.res, w=RRW.res[0:2])
                    rb = TF.t[64:128, 0:2 * NQ]
                    b.op("q", "dma_start", stream=f"rb{sl & 1}",
                         r=RRW.res[0:2], w=TF.res[0:2],
                         out=rb.unsqueeze(1),
                         in_=RRW.t[0:1, 0:2 * NQ]
                         .unsqueeze(1).broadcast_to([1, 64, 2 * NQ]))
                    ycols = c * T + sl * NQ
                    b.op("v", "tensor_mul", BA.t[0:64, ycols: ycols + NQ],
                         yps.t[0:64, 0:NQ], TF.t[64:128, 0:NQ],
                         r=yps.res + TF.res[0:1],
                         w=BA.rr(ycols, ycols + NQ))
                    oslot = sl & 1
                    b.op("v", "tensor_mul",
                         ODD.t[0:64, oslot * NQ: oslot * NQ + NQ],
                         yps.t[0:64, NQ:2 * NQ], TF.t[64:128, NQ:2 * NQ],
                         r=yps.res + TF.res[1:2],
                         w=ODD.res[oslot:oslot + 1])
                    b.op("g", "dma_start", stream=f"od{oslot}",
                         r=ODD.res[oslot:oslot + 1], w=BA.rr(ycols, ycols + NQ),
                         out=BA.t[64:128, ycols: ycols + NQ],
                         in_=ODD.t[0:64, oslot * NQ: oslot * NQ + NQ])
                    if inject:
                        inject.pop(0)()

            loads, ch, vch = prep_chunks(0)
            for f in loads + ch + vch:
                f()
            for c in range(C):
                if c + 1 < C:
                    nloads, nxt, vnxt = prep_chunks(c + 1)
                    for f in nloads:
                        f()
                else:
                    nxt, vnxt = [], []
                attn_pair(c, nxt)
                for f in nxt:
                    f()
                for f in vnxt:
                    f()
            for d in range(C):
                wo_s = (d & 1)
                wslot_load(wo_s, "Wo", d)

                def epi(ps_ap, ps_res, sl2, sap, sres):
                    b.op("v", "tensor_copy", sap, ps_ap, r=ps_res, w=sres)

                fm_to_tm(d, wo_s,
                         lambda t0, nt: (BBv[:, t0:t0 + nt, :], BB.res[t0:t0 + nt]),
                         BA, epi)
            post_ln([(BB.t[:, tt * 1024: tt * 1024 + D], BB.res[tt:tt + 1])
                     for tt in range(TT)])

        def l0_ffn():
            pre_ln()
            for half, wnm, mu_off, dstbuf in (("r", "Wr", 0, BA), ("k", "Wkk", C, BB)):
                for d in range(C):
                    raw, w1s, w2s = (d & 1), 2 + (d & 1), 4 + (d & 1)
                    wslot_load(raw, wnm, d)
                    for c2 in range(C):
                        b.op("v", "tensor_scalar_mul", wap(w1s, c2), wap(raw, c2),
                             MUB.t[:, mu_off + c2: mu_off + c2 + 1],
                             r=wres(raw) + MUB.res[mu_off // C: mu_off // C + 1],
                             w=wres(w1s))
                    b.op("v", "tensor_sub", wslice(w2s), wslice(raw), wslice(w1s),
                         r=wres(raw) + wres(w1s), w=wres(w2s))
                    proj4(w1s, stop=False)
                    for c2 in range(C):
                        for sl2 in range(SL2):
                            tb, col = bank_ap(sl2)
                            wr_ps = tb.res[col // 512: col // 512 + 1]
                            if sl2 == 0:
                                b.op("t", "matmul", tb.t[:, col + 1:col + 512],
                                     wap(w2s, c2), XN.t[:, c2 * T: c2 * T + 511],
                                     start=False, stop=(c2 == C - 1),
                                     r=wres(w2s) + XN.res[c2:c2 + 1], w=wr_ps)
                            else:
                                b.op("t", "matmul", tb.t[:, col:col + 512],
                                     wap(w2s, c2),
                                     XN.t[:, c2 * T + sl2 * 512 - 1: c2 * T + sl2 * 512 + 511],
                                     start=False, stop=(c2 == C - 1),
                                     r=wres(w2s) + XN.res[c2:c2 + 1], w=wr_ps)
                    for tb, col0, wd, off in proj_segments():
                        dcol = d * T + off
                        if half == "r":
                            b.op("s", "activation", dstbuf.t[:, dcol: dcol + wd],
                                 tb.t[:, col0:col0 + wd], ACT.Sigmoid,
                                 r=tb.rr(col0, col0 + wd), w=dstbuf.rr(dcol, dcol + wd))
                        else:
                            b.op("v", "tensor_scalar_max", dstbuf.t[:, dcol: dcol + wd],
                                 tb.t[:, col0:col0 + wd], 0.0,
                                 r=tb.rr(col0, col0 + wd), w=dstbuf.rr(dcol, dcol + wd))
                            b.op("s", "activation", dstbuf.t[:, dcol: dcol + wd],
                                 dstbuf.t[:, dcol: dcol + wd], ACT.Square,
                                 r=dstbuf.rr(dcol, dcol + wd),
                                 w=dstbuf.rr(dcol, dcol + wd))
            LO = min(TT, 8)

            def tiles_of(t0, nt):
                if t0 < LO:
                    return (BCv[:, t0:t0 + nt, :], BC.rr(t0 * 1024, (t0 + nt) * 1024))
                return (CHv[:, t0 - LO:t0 - LO + nt, :],
                        CH.rr((t0 - LO) * 1024, (t0 - LO + nt) * 1024))

            for d in range(C):
                wv_s = (d & 1)
                wslot_load(wv_s, "Wvv", d)

                def epi(ps_ap, ps_res, sl2, sap, sres, d=d):
                    dcol = d * T + sl2 * 512
                    b.op("v", "tensor_mul", sap, ps_ap,
                         BA.t[:, dcol: dcol + 512],
                         r=ps_res + BA.rr(dcol, dcol + 512), w=sres)

                fm_to_tm(d, wv_s, tiles_of, BB, epi)
            src = [(BC.t[:, tt * 1024: tt * 1024 + D], BC.rr(tt * 1024, tt * 1024 + D))
                   if tt < LO else
                   (CH.t[:, (tt - LO) * 1024: (tt - LO) * 1024 + D],
                    CH.rr((tt - LO) * 1024, (tt - LO) * 1024 + D))
                   for tt in range(TT)]
            post_ln(src)

        def l1_attn():
            pre_ln(dest_tm=BA)
            vstv = VST.t[:, :].rearrange("p (t h e) -> p t h e", t=TT, h=2)

            def prep_chunks(c):
                wq_s, wk_s, wk2_s = (c & 1), 2 + (c & 1), 4 + (c & 1)
                qch, kch, kach = (c & 1), 2 + (c & 1), (c & 1)
                loads = [lambda: wslot_load(wq_s, "Wattn", c),
                         lambda: wslot_load(wk_s, "Wattn", C + c),
                         lambda: wslot_load(wk2_s, "Wk2", c)]
                ch = []

                def mk_proj(wsl, dst_ch, pair, scale):
                    def go(wsl=wsl, dst_ch=dst_ch, pair=pair, scale=scale):
                        for psb, sl2 in proj2(wsl, pair):
                            if scale is None:
                                b.op("v", "tensor_copy",
                                     CH.t[:, dst_ch * T + sl2 * 512:
                                          dst_ch * T + sl2 * 512 + 512],
                                     psb.t[:, 0:512], r=psb.res,
                                     w=CH.res[dst_ch:dst_ch + 1])
                            else:
                                b.op("v", "tensor_scalar_mul",
                                     CH.t[:, dst_ch * T + sl2 * 512:
                                          dst_ch * T + sl2 * 512 + 512],
                                     psb.t[:, 0:512], scale, r=psb.res,
                                     w=CH.res[dst_ch:dst_ch + 1])
                    return go

                for pair in sl2_pairs():
                    ch.append(mk_proj(wq_s, qch, pair, None))
                for pair in sl2_pairs():
                    ch.append(mk_proj(wk_s, kch, pair, 0.125))

                after = []

                def mk_projk2(pair):
                    def go(pair=pair, wk2_s=wk2_s, kach=kach):
                        for psb, sl2 in proj2(wk2_s, pair):
                            b.op("s", "activation",
                                 KST.t[:, kach * T + sl2 * 512:
                                       kach * T + sl2 * 512 + 512],
                                 psb.t[:, 0:512], ACT.Sigmoid, scale=0.0025,
                                 r=psb.res, w=KST.res[kach:kach + 1])
                    return go

                for pair in sl2_pairs():
                    after.append(mk_projk2(pair))

                def vstage(c=c):
                    b.op("v", "tensor_copy", vstv[:, :, :, 0:64],
                         BAv[:, :, c * 128:(c + 1) * 128]
                         .rearrange("p t (h e) -> p t h e", h=2),
                         r=BA.res[:TT], w=VST.res[0:1])
                after.append(vstage)
                return loads, ch, after

            def attn_pair(c, inject):
                qch, kch = (c & 1), 2 + (c & 1)
                for sl in range(NSL):
                    ntt = min(2 * sl + 2, TT)
                    for grp in range(-(-ntt // 4)):
                        rem = min(4, ntt - grp * 4)
                        for hs in range(2):
                            PSX = PSA if hs == 0 else PSB
                            for k in range(rem):
                                tt = grp * 4 + k
                                b.op("t", "matmul", PSX.t[:, k * NQ:(k + 1) * NQ],
                                     CH.t[hs * 64:(hs + 1) * 64,
                                          kch * T + tt * 128: kch * T + tt * 128 + 128],
                                     CH.t[hs * 64:(hs + 1) * 64,
                                          qch * T + sl * NQ: qch * T + sl * NQ + NQ],
                                     start=(k % 2 == 0),
                                     stop=(k % 2 == 1 or k == rem - 1),
                                     tile_position=(hs * 64, 0),
                                     r=CH.res[kch:kch + 1] + CH.res[qch:qch + 1],
                                     w=PSX.res[k // 2:k // 2 + 1])
                            bci0 = hs * TT + grp * 4
                            b.op("s", "activation",
                                 BC.t[:, bci0 * NQ: (bci0 + rem) * NQ],
                                 PSX.t[:, 0:rem * NQ], ACT.Exp,
                                 r=PSX.res[0:(rem + 1) // 2],
                                 w=BC.res[bci0:bci0 + rem])
                            for k in range(rem):
                                tt = grp * 4 + k
                                if tt >= 2 * sl:
                                    di = tt - 2 * sl
                                    bci = hs * TT + tt
                                    b.op("v", "tensor_mul",
                                         BC.t[:, bci * NQ: (bci + 1) * NQ],
                                         BC.t[:, bci * NQ: (bci + 1) * NQ],
                                         MASKS.t[:, 129 - 128 * di: 129 - 128 * di + NQ],
                                         r=BC.res[bci:bci + 1] + MASKS.res,
                                         w=BC.res[bci:bci + 1])
                    yps = PS4 if sl % 2 == 0 else PS5
                    for hs in range(2):
                        for tt in range(ntt):
                            bci = hs * TT + tt
                            b.op("t", "matmul", yps.t[0:65, hs * NQ:(hs + 1) * NQ],
                                 VST.t[:, tt * 130 + hs * 65: tt * 130 + hs * 65 + 65],
                                 BC.t[:, bci * NQ: (bci + 1) * NQ],
                                 start=(tt == 0), stop=(tt == ntt - 1),
                                 r=VST.res[0:1] + BC.res[bci:bci + 1],
                                 w=yps.res)
                    b.op("v", "reciprocal", RRW.t[0:1, 0:2 * NQ],
                         yps.t[64:65, 0:2 * NQ], r=yps.res, w=RRW.res[0:2])
                    rb = TF.t[64:128, 0:2 * NQ]
                    b.op("q", "dma_start", stream=f"rb{sl & 1}",
                         r=RRW.res[0:2], w=TF.res[0:2],
                         out=rb.unsqueeze(1),
                         in_=RRW.t[0:1, 0:2 * NQ]
                         .unsqueeze(1).broadcast_to([1, 64, 2 * NQ]))
                    ycols = c * T + sl * NQ
                    b.op("v", "tensor_mul", BB.t[0:64, ycols: ycols + NQ],
                         yps.t[0:64, 0:NQ], TF.t[64:128, 0:NQ],
                         r=yps.res + TF.res[0:1],
                         w=BB.rr(ycols, ycols + NQ))
                    oslot = sl & 1
                    b.op("v", "tensor_mul",
                         ODD.t[0:64, oslot * NQ: oslot * NQ + NQ],
                         yps.t[0:64, NQ:2 * NQ], TF.t[64:128, NQ:2 * NQ],
                         r=yps.res + TF.res[1:2],
                         w=ODD.res[oslot:oslot + 1])
                    b.op("g", "dma_start", stream=f"od{oslot}",
                         r=ODD.res[oslot:oslot + 1], w=BB.rr(ycols, ycols + NQ),
                         out=BB.t[64:128, ycols: ycols + NQ],
                         in_=ODD.t[0:64, oslot * NQ: oslot * NQ + NQ])
                    if inject:
                        inject.pop(0)()

            def wterm_pair(c, inject):
                qch, kch, kach = (c & 1), 2 + (c & 1), (c & 1)
                b.op("v", "tensor_sub", EST.t[:, 0: T - 1],
                     XN.t[:, c * T + 1: c * T + T], BB.t[:, c * T: c * T + T - 1],
                     r=XN.res[c:c + 1] + BB.rr(c * T, c * T + T), w=EST.res[0:1])
                b.op("v", "memset", EST.t[:, T - 1: T], 0.0, w=EST.res[0:1])
                for g in range(-(-TT // 8)):
                    nt = min(8, TT - g * 8)
                    for k in range(nt):
                        tt = g * 8 + k
                        b.op("t", "matmul", PT.t[:, k * 128:(k + 1) * 128],
                             EST.t[:, tt * 128:(tt + 1) * 128], ident_ap,
                             is_transpose=True, start=(k == 0), stop=(k == nt - 1),
                             r=EST.res[0:1] + IDENT.res, w=PT.res)
                    b.op("v", "tensor_copy",
                         EST.t[:, T + g * 1024: T + g * 1024 + nt * 128],
                         PT.t[:, 0:nt * 128], r=PT.res, w=EST.res[1:2])
                b.op("v", "tensor_scalar", EST.t[:, 0:T],
                     CH.t[:, qch * T: qch * T + T], 0.0, 0.125, ALU.min, ALU.mult,
                     r=CH.res[qch:qch + 1], w=EST.res[0:1])
                b.op("v", "tensor_scalar", CH.t[:, qch * T: qch * T + T],
                     CH.t[:, qch * T: qch * T + T], 0.0, 0.0025, ALU.max, ALU.mult,
                     r=CH.res[qch:qch + 1], w=CH.res[qch:qch + 1])
                b.op("v", "tensor_add", CH.t[:, qch * T: qch * T + T],
                     CH.t[:, qch * T: qch * T + T], EST.t[:, 0:T],
                     r=CH.res[qch:qch + 1] + EST.res[0:1], w=CH.res[qch:qch + 1])
                for sl in range(NSL):
                    ntt = min(2 * sl + 2, TT)
                    for grp in range(-(-ntt // 4)):
                        rem = min(4, ntt - grp * 4)
                        for hs in range(2):
                            PSX = PSA if hs == 0 else PSB
                            for k in range(rem):
                                tt = grp * 4 + k
                                b.op("t", "matmul", PSX.t[:, k * NQ:(k + 1) * NQ],
                                     KST.t[hs * 64:(hs + 1) * 64,
                                           kach * T + tt * 128: kach * T + tt * 128 + 128],
                                     CH.t[hs * 64:(hs + 1) * 64,
                                          qch * T + sl * NQ: qch * T + sl * NQ + NQ],
                                     start=(k % 2 == 0),
                                     stop=(k % 2 == 1 or k == rem - 1),
                                     tile_position=(hs * 64, 0),
                                     r=KST.res[kach:kach + 1] + CH.res[qch:qch + 1],
                                     w=PSX.res[k // 2:k // 2 + 1])
                            bci0 = hs * TT + grp * 4
                            kd = max(0, min(2 * sl - grp * 4, rem))
                            if kd > 0:
                                b.op("s", "activation",
                                     BC.t[:, bci0 * NQ: (bci0 + kd) * NQ],
                                     PSX.t[:, 0:kd * NQ], ACT.Copy,
                                     r=PSX.res[0:(kd + 1) // 2],
                                     w=BC.res[bci0:bci0 + kd])
                            for k in range(kd, rem):
                                tt = grp * 4 + k
                                di = tt - 2 * sl
                                bci = hs * TT + tt
                                b.op("v", "tensor_mul",
                                     BC.t[:, bci * NQ: (bci + 1) * NQ],
                                     PSX.t[:, k * NQ:(k + 1) * NQ],
                                     MASKS.t[:, 128 - 128 * di: 128 - 128 * di + NQ],
                                     r=PSX.res[k // 2:k // 2 + 1] + MASKS.res,
                                     w=BC.res[bci:bci + 1])
                    psy = PS4 if sl % 2 == 0 else PS5
                    for tt in range(ntt):
                        for hs in range(2):
                            bci = hs * TT + tt
                            b.op("t", "matmul",
                                 psy.t[hs * 64:(hs + 1) * 64, 0:NQ],
                                 EST.t[:, T + tt * 128 + hs * 64: T + tt * 128 + hs * 64 + 64],
                                 BC.t[:, bci * NQ: (bci + 1) * NQ],
                                 start=(tt == 0), stop=(tt == ntt - 1),
                                 tile_position=(0, hs * 64),
                                 r=EST.res[1:2] + BC.res[bci:bci + 1], w=psy.res)
                    zcols = c * T + sl * NQ
                    b.op("v", "tensor_add", BB.t[:, zcols: zcols + NQ],
                         BB.t[:, zcols: zcols + NQ], psy.t[:, 0:NQ],
                         r=psy.res + BB.rr(zcols, zcols + NQ),
                         w=BB.rr(zcols, zcols + NQ))
                    if inject:
                        inject.pop(0)()

            loads, ch, after = prep_chunks(0)
            for f in loads + ch + after:
                f()
            for c in range(C):
                if c + 1 < C:
                    nloads, nxt, anxt = prep_chunks(c + 1)
                    for f in nloads:
                        f()
                else:
                    nxt, anxt = [], []
                attn_pair(c, nxt)
                for f in nxt:
                    f()
                wterm_pair(c, anxt)
                for f in anxt:
                    f()
            for d in range(C):
                wp_s = (d & 1)
                wslot_load(wp_s, "Wproj", d)

                def epi(ps_ap, ps_res, sl2, sap, sres):
                    b.op("v", "tensor_copy", sap, ps_ap, r=ps_res, w=sres)

                fm_to_tm(d, wp_s,
                         lambda t0, nt: (BAv[:, t0:t0 + nt, :], BA.res[t0:t0 + nt]),
                         BB, epi)
            post_ln([(BA.t[:, tt * 1024: tt * 1024 + D], BA.res[tt:tt + 1])
                     for tt in range(TT)])

        def l1_ffn():
            pre_ln()
            for d in range(C):
                ws = (d & 1)
                wslot_load(ws, "Wsign", d)
                b.op("v", "tensor_scalar", wslice(ws), wslice(ws), 0.0, None,
                     ALU.is_ge, r=wres(ws), w=wres(ws))
                b.op("v", "tensor_scalar", wslice(ws), wslice(ws), 2.0, -1.0,
                     ALU.mult, ALU.add, r=wres(ws), w=wres(ws))

                def epi(ps_ap, ps_res, sl2, sap, sres, d=d):
                    b.op("v", "tensor_scalar", TF.t[:, 0:512],
                         ps_ap, MUB.t[:, 2 * C + d: 2 * C + d + 1],
                         MUB.t[:, 3 * C + d: 3 * C + d + 1], ALU.mult, ALU.add,
                         r=ps_res + MUB.res[2:4], w=TF.res[0:2])
                    b.op("v", "tensor_scalar", sap, TF.t[:, 0:512],
                         100.0, -100.0, ALU.min, ALU.max,
                         r=TF.res[0:2], w=sres)

                fm_to_tm(d, ws,
                         lambda t0, nt: (BAv[:, t0:t0 + nt, :], BA.res[t0:t0 + nt]),
                         XN, epi)
            post_ln([(BA.t[:, tt * 1024: tt * 1024 + D], BA.res[tt:tt + 1])
                     for tt in range(TT)])

        for name, buf in (("ident", IDENT), ("masks", MASKS), ("onesa", ONESA),
                          ("onesb", ONESB), ("ones64f", ONES64F)):
            b.op("q", "dma_start", w=buf.res, stream="qc",
                 out=buf.t[:, :], in_=dram[name][:, :])
        for i, nm in enumerate(("mur_t", "muk_t", "scale_t", "bias_t")):
            b.op("q", "dma_start", w=MUB.res[i:i + 1], stream="qc",
                 out=MUB.t[:, i * C:(i + 1) * C], in_=dram[nm][:, :])
        vv_ones = VST.t[:, :].rearrange("p (t h e) -> p t h e", t=TT, h=2)
        b.op("v", "memset", vv_ones[:, :, :, 64:65], 1.0, w=VST.res[0:1])

        SUBS = {"l0a": l0_attn, "l0f": l0_ffn, "l1a": l1_attn, "l1f": l1_ffn}
        for bi in range(cfg.B):
            for tt in range(TT):
                b.op("g", "dma_start", w=X.res[tt:tt + 1], stream="gx",
                     out=X.t[:, tt * 1024: tt * 1024 + D],
                     in_=dram["x"][bi, tt * 128:(tt + 1) * 128, :])
            for sbk in cfg.sub_blocks:
                SUBS[sbk]()
            for tt in range(TT):
                b.op("g", "dma_start", r=X.res[tt:tt + 1], stream="go",
                     out=out_d[bi, tt * 128:(tt + 1) * 128, :],
                     in_=X.t[:, tt * 1024: tt * 1024 + D])

        b.op("g", None, deps=[("d", "go", b.streams.get("go", 0))])
        b.op("q", None, deps=[("d", "go", b.streams.get("go", 0))])

        b.finalize()
        with contextlib.ExitStack() as sctx:
            ctr_sems = {e: sctx.enter_context(nc.semaphore(f"c_{e}")) for e in ENGINES}
            stream_sems = {s: sctx.enter_context(nc.semaphore(f"d_{s}"))
                           for s in b.streams}
            with nc.Block() as block:
                @block.tensor
                def _(eng):
                    b.replay("t", eng, ctr_sems, stream_sems)

                @block.vector
                def _(eng):
                    b.replay("v", eng, ctr_sems, stream_sems)

                @block.scalar
                def _(eng):
                    b.replay("s", eng, ctr_sems, stream_sems)

                @block.gpsimd
                def _(eng):
                    b.replay("g", eng, ctr_sems, stream_sems)

                @block.sync
                def _(eng):
                    b.replay("q", eng, ctr_sems, stream_sems)

    return nc



def prep_shared(cfg, inputs):
    shared = {}
    shared["Wq"] = tile_w(inputs["L0_attn_Wq"])
    shared["Wk"] = tile_w(inputs["L0_attn_Wk"])
    shared["Wv"] = tile_w(inputs["L0_attn_Wv"])
    shared["Wo"] = tile_w(inputs["L0_attn_Wo"])
    shared["Wr"] = tile_w(np.ascontiguousarray(np.asarray(inputs["L0_ffn_W_r"]).T))
    shared["Wkk"] = tile_w(np.ascontiguousarray(np.asarray(inputs["L0_ffn_W_k"]).T))
    shared["Wvv"] = tile_w(np.ascontiguousarray(np.asarray(inputs["L0_ffn_W_v"]).T))
    shared["Wattn"] = tile_w(inputs["L1_attn_W_attn"])
    shared["Wk2"] = tile_w(inputs["L1_attn_W_k2"])
    shared["Wproj"] = tile_w(inputs["L1_attn_W_proj"])
    shared["Wsign"] = tile_w(np.ascontiguousarray(np.asarray(inputs["L1_ffn_weights"]).T))
    shared["mur_t"] = vec_tile(inputs["L0_ffn_mu_r"])
    shared["muk_t"] = vec_tile(inputs["L0_ffn_mu_k"])
    shared["scale_t"] = vec_tile(inputs["L1_ffn_scale"])
    shared["bias_t"] = vec_tile(inputs["L1_ffn_bias"])
    shared.update(host_consts(cfg))
    return shared


def kernel(**inputs):
    cfg = Cfg()
    shared = prep_shared(cfg, inputs)
    x = np.ascontiguousarray(np.asarray(inputs["L0_attn_x"], np.float32))
    nb_total = x.shape[0]
    ncores = 8
    per = nb_total // ncores
    specs = {k: (v.shape, v.dtype) for k, v in shared.items()}
    specs["x"] = ((per, cfg.T, cfg.D), np.float32)
    nc = build_program(cfg, specs)
    in_maps = []
    for i in range(ncores):
        m = dict(shared)
        m["x"] = np.ascontiguousarray(x[i * per:(i + 1) * per])
        in_maps.append(m)
    res = run_bass_kernel_spmd(nc, in_maps, core_ids=list(range(ncores)))
    out = np.concatenate([np.asarray(r["out"]) for r in res.results], axis=0)
    return out.astype(np.float32)


# version 16
# speedup vs baseline: 1.0395x; 1.0140x over previous
import contextlib
import numpy as np
import ml_dtypes
import concourse.bass as bass
import concourse.mybir as mybir
from concourse.bass_utils import run_bass_kernel_spmd

F32 = mybir.dt.float32
BF16 = mybir.dt.bfloat16
AX = mybir.AxisListType.X
ALU = mybir.AluOpType
ACT = mybir.ActivationFunctionType
BF = ml_dtypes.bfloat16

ENGINES = ("t", "v", "s", "g", "q")


class Cfg:
    def __init__(self, B=2, T=2048, D=1024, H=16, NQ=256,
                 sub_blocks=("l0a", "l0f", "l1a", "l1f")):
        self.B, self.T, self.D, self.H, self.NQ = B, T, D, H, NQ
        self.DH = D // H
        assert self.DH == 64 and NQ == 256
        self.C = D // 128
        self.TT = T // 128
        self.NSL = T // NQ
        self.SL2 = T // 512
        self.sub_blocks = tuple(sub_blocks)


class Res:
    __slots__ = ("w", "rs")

    def __init__(self):
        self.w = None
        self.rs = []


class Buf:
    def __init__(self, t, ncols, gran):
        self.t = t
        self.gran = gran
        self.res = [Res() for _ in range(-(-ncols // gran))]

    def rr(self, lo, hi):
        return self.res[lo // self.gran: -(-hi // self.gran)]


class Bld:
    def __init__(self):
        self.ops = {e: [] for e in ENGINES}
        self.streams = {}
        self.need_inc = {e: set() for e in ENGINES}

    def op(self, eng, fname, *args, r=(), w=(), deps=(), stream=None, **kw):
        deph = []
        for res in r:
            if res.w is not None:
                deph.append(res.w)
        for res in w:
            if res.w is not None:
                deph.append(res.w)
            deph.extend(res.rs)
        deph.extend(deps)
        deph = [h if h[0] == "e" else ("d", h[1], self.streams[h[1]]) for h in deph]
        seen, dd = set(), []
        for h in deph:
            if h not in seen:
                seen.add(h)
                dd.append(h)
        idx = len(self.ops[eng])
        if stream is not None:
            self.streams[stream] = self.streams.get(stream, 0) + 1
            handle = ("d", stream, self.streams[stream])
        else:
            handle = ("e", eng, idx)
        if eng == "t":
            dd = [h for h in dd if not (h[0] == "e" and h[1] == "t")]
        self.ops[eng].append(dict(f=fname, a=args, k=kw, deps=dd, stream=stream))
        for h in dd:
            if h[0] == "e":
                self.need_inc[h[1]].add(h[2])
        for res in r:
            res.rs.append(handle)
        for res in w:
            res.w = handle
            res.rs = []
        return handle

    def finalize(self):
        self.prefix = {}
        for e in ENGINES:
            pr, c = [], 0
            for i in range(len(self.ops[e])):
                if i in self.need_inc[e]:
                    c += 1
                pr.append(c)
            self.prefix[e] = pr

    def replay(self, eng, obj, ctr_sems, stream_sems):
        covered = {}
        for i, rec in enumerate(self.ops[eng]):
            for h in rec["deps"]:
                if h[0] == "e":
                    key, val, sem = ("e", h[1]), self.prefix[h[1]][h[2]], ctr_sems[h[1]]
                else:
                    key, val, sem = ("d", h[1]), 16 * h[2], stream_sems[h[1]]
                if covered.get(key, 0) < val:
                    obj.wait_ge(sem, val)
                    covered[key] = val
            if rec["f"] is None:
                continue
            try:
                instr = getattr(obj, rec["f"])(*rec["a"], **rec["k"])
            except Exception:
                import sys
                print(f"REPLAY FAIL {eng}[{i}] {rec['f']} args={[getattr(a,'shape',a) for a in rec['a']]}",
                      file=sys.stderr)
                raise
            if rec["stream"] is not None:
                instr.then_inc(stream_sems[rec["stream"]], 16)
            elif i in self.need_inc[eng]:
                instr.then_inc(ctr_sems[eng], 1)



def tile_w(W):
    Din, Dout = W.shape
    return np.ascontiguousarray(
        np.asarray(W, np.float32).reshape(Din // 128, 128, Dout // 128, 128)
        .transpose(2, 0, 1, 3))


def vec_tile(v):
    return np.ascontiguousarray(np.asarray(v, np.float32).reshape(-1, 128).T)


def host_consts(cfg):
    NQ = cfg.NQ
    c = {}
    c["ident"] = np.eye(128, dtype=BF)
    tp = np.arange(128)[:, None]
    u = np.arange(NQ + 130)[None, :]
    c["masks"] = np.ascontiguousarray((u - 129 - tp >= 0).astype(np.float32)).astype(BF)
    c["onesa"] = np.full((128, 1), 1.0 / 63.0, dtype=BF)
    c["onesb"] = np.full((128, 1), 4096.0 / (63.0 * 64.0), dtype=BF)
    c["ones64f"] = np.ones((1, 64), dtype=np.float32)
    return c



def build_program(cfg, input_specs):
    nc = bass.Bass("TRN2", target_bir_lowering=False)
    dram = {}
    for name, (shape, dt) in input_specs.items():
        mdt = F32 if np.dtype(dt) == np.float32 else BF16
        dram[name] = nc.dram_tensor(name, list(shape), mdt, kind="ExternalInput").ap()
    out_d = nc.dram_tensor("out", [cfg.B, cfg.T, cfg.D], F32, kind="ExternalOutput").ap()

    b = Bld()
    T, D, C, TT, NQ, NSL, SL2 = (cfg.T, cfg.D, cfg.C, cfg.TT, cfg.NQ,
                                 cfg.NSL, cfg.SL2)
    ctx = contextlib.ExitStack()

    def S(name, shape, dt=BF16, gran=None):
        t = ctx.enter_context(nc.sbuf_tensor(name, list(shape), dt))
        return Buf(t, shape[-1], gran or shape[-1])

    with ctx:
        X = S("X", [128, TT * 1024], gran=1024)
        XN = S("XN", [128, C * T], gran=T)
        BA = S("BA", [128, max(C * T, TT * 1024)], gran=1024)
        BB = S("BB", [128, max(C * T, TT * 1024)], gran=1024)
        BC = S("BCb", [128, max(2 * TT * NQ, min(TT, 8) * 1024)], gran=NQ)
        CH = S("CH", [128, 5 * T], gran=T)
        VST = S("VST", [128, TT * 130], gran=TT * 130)
        QST = S("QST", [128, 2 * T], gran=T)
        EST = QST
        KST = S("KST", [128, 2 * T], gran=T)
        ODD = S("ODD", [64, 2 * NQ], gran=NQ)
        WP = S("WP", [128, 6 * 1024], gran=1024)
        TBF = S("TBF", [128, 2 * 512], gran=512)
        XT = S("XT", [128, 2 * 1024], gran=1024)
        TF = S("TF", [128, 512], F32, gran=256)
        RRW = TF
        STT = S("STT", [128, 2 * 12], F32, gran=12)
        MV = S("MV", [128, 2 * TT], F32, gran=1)
        RSD = S("RSD", [128, TT], F32)
        BIASN = S("BIASN", [128, TT], F32)
        SUMS = S("SUMS", [128, 2 * TT], F32, gran=1)
        SCR = S("SCR", [128, 3 * TT], F32)

        MUB = S("MUB", [128, 4 * C], F32, gran=C)
        IDENT = S("IDENTB", [128, 128])
        MASKS = S("MASKSB", [128, NQ + 130])
        ONESA = S("ONESAB", [128, 1])
        ONESB = S("ONESBB", [128, 1])
        ONES64F = S("ONES64FB", [1, 64], F32)

        PSA = Buf(ctx.enter_context(nc.psum_tensor("psa", [128, 1024], F32)), 1024, 512)
        PSB = Buf(ctx.enter_context(nc.psum_tensor("psb", [128, 1024], F32)), 1024, 512)
        PS4 = Buf(ctx.enter_context(nc.psum_tensor("ps4", [128, 512], F32)), 512, 512)
        PS5 = Buf(ctx.enter_context(nc.psum_tensor("ps5", [128, 512], F32)), 512, 512)
        PS6 = Buf(ctx.enter_context(nc.psum_tensor("ps6", [128, 512], F32)), 512, 512)
        PS7 = Buf(ctx.enter_context(nc.psum_tensor("ps7", [128, 512], F32)), 512, 512)
        PT = Buf(PS7.t[:, 0:512].bitcast(BF16), 1024, 1024)
        PT.res = PS7.res

        ident_ap = IDENT.t[:, :]
        MVm = lambda tt: MV.t[:, tt:tt + 1]
        BAv = BA.t[:, 0:TT * 1024].rearrange("p (t j) -> p t j", j=1024)
        BBv = BB.t[:, 0:TT * 1024].rearrange("p (t j) -> p t j", j=1024)
        BCv = BC.t[:, 0:min(TT, 8) * 1024].rearrange("p (t j) -> p t j", j=1024)
        CHv = CH.t[:, 0:(TT - min(TT, 8)) * 1024].rearrange("p (t j) -> p t j", j=1024) if TT > 8 else None
        XNv = XN.t.rearrange("p (c j) -> p c j", c=C)

        def wslot_load(slot, wname, d, nchunk=None):
            nck = nchunk or C
            src = dram[wname][d].transpose([1, 0, 2])
            dst = WP.t[:, slot * 1024: slot * 1024 + nck * 128]
            b.op("g", "dma_start", w=WP.res[slot:slot + 1], stream=f"wg{slot}",
                 out=dst.rearrange("p (c j) -> p c j", c=nck), in_=src)

        def wap(slot, c2):
            return WP.t[:, slot * 1024 + c2 * 128: slot * 1024 + (c2 + 1) * 128]

        def wslice(slot, ncols=None):
            return WP.t[:, slot * 1024: slot * 1024 + (ncols or C * 128)]

        def wres(slot):
            return WP.res[slot:slot + 1]

        def bank_ap(sl2):
            tb = PSA if (sl2 % 4) < 2 else PSB
            return tb, (sl2 % 2) * 512

        def rstd_chain():
            b.op("v", "tensor_scalar_add", SCR.t[:, 0:TT], MV.t[:, TT:2 * TT], 1e-5,
                 r=MV.res[TT:2 * TT], w=SCR.res)
            b.op("s", "activation", SCR.t[:, TT:2 * TT], SCR.t[:, 0:TT], ACT.Sqrt,
                 r=SCR.res, w=SCR.res)
            b.op("v", "reciprocal", SCR.t[:, 2 * TT:3 * TT], SCR.t[:, TT:2 * TT],
                 r=SCR.res, w=SCR.res)
            b.op("v", "tensor_mul", RSD.t[:, :], SCR.t[:, 2 * TT:3 * TT],
                 SCR.t[:, 2 * TT:3 * TT], r=SCR.res, w=RSD.res)
            b.op("v", "tensor_mul", RSD.t[:, :], RSD.t[:, :], SCR.t[:, 0:TT],
                 r=SCR.res + RSD.res, w=RSD.res)
            b.op("v", "tensor_scalar", SCR.t[:, 0:TT], RSD.t[:, :], -0.5, 1.5,
                 ALU.mult, ALU.add, r=RSD.res, w=SCR.res)
            b.op("v", "tensor_mul", RSD.t[:, :], SCR.t[:, 2 * TT:3 * TT],
                 SCR.t[:, 0:TT], r=SCR.res, w=RSD.res)

        def ln_stats(tile_aps):
            ng = -(-D // 512)
            odd = [tt for tt in range(TT) if tt & 1]
            for tt in range(TT):
                ap, res_l = tile_aps[tt]
                st = (tt // 2) & 1
                if tt & 1:
                    xs = (tt // 2) & 1
                    b.op("s", "activation", XT.t[:, xs * 1024: xs * 1024 + D],
                         ap, ACT.Copy, accum_out=SUMS.t[:, tt:tt + 1],
                         r=res_l, w=XT.res[xs:xs + 1] + SUMS.res[tt:tt + 1])
                    b.op("s", "activation", XT.t[:, xs * 1024: xs * 1024 + D],
                         ap, ACT.Square, accum_out=SUMS.t[:, TT + tt:TT + tt + 1],
                         r=res_l,
                         w=XT.res[xs:xs + 1] + SUMS.res[TT + tt:TT + tt + 1])
                    continue
                for g in range(ng):
                    wseg = min(512, D - g * 512)
                    b.op("v", "bn_stats",
                         STT.t[:, st * 12 + g * 6: st * 12 + g * 6 + 6],
                         ap[:, g * 512: g * 512 + wseg], r=res_l, w=STT.res[st:st + 1])
                mvout = MV.t.rearrange("p (two t) -> p two t", two=2)[:, :, tt]
                b.op("v", "bn_aggr", mvout, STT.t[:, st * 12: st * 12 + ng * 6],
                     r=STT.res[st:st + 1],
                     w=MV.res[tt:tt + 1] + MV.res[TT + tt:TT + tt + 1])
            mv2 = MV.t.rearrange("p (two t2 o) -> p two t2 o", two=2, o=2)
            s2v = SUMS.t.rearrange("p (two t2 o) -> p two t2 o", two=2, o=2)
            b.op("v", "tensor_scalar_mul", mv2[:, 0, :, 1], s2v[:, 0, :, 1],
                 1.0 / D, r=[SUMS.res[tt] for tt in odd],
                 w=[MV.res[tt] for tt in odd])
            b.op("v", "tensor_scalar_mul", mv2[:, 1, :, 1], s2v[:, 1, :, 1],
                 1.0 / D, r=[SUMS.res[TT + tt] for tt in odd],
                 w=[MV.res[TT + tt] for tt in odd])
            b.op("v", "tensor_mul", SCR.t[:, 0:TT // 2], mv2[:, 0, :, 1],
                 mv2[:, 0, :, 1], r=[MV.res[tt] for tt in odd], w=SCR.res)
            b.op("v", "tensor_sub", mv2[:, 1, :, 1], mv2[:, 1, :, 1],
                 SCR.t[:, 0:TT // 2],
                 r=SCR.res + [MV.res[TT + tt] for tt in odd],
                 w=[MV.res[TT + tt] for tt in odd])
            rstd_chain()
            b.op("v", "scalar_tensor_tensor", BIASN.t[:, :], MV.t[:, 0:TT], -1.0,
                 RSD.t[:, :], ALU.mult, ALU.mult,
                 r=MV.res[0:TT] + RSD.res, w=BIASN.res)

        def pre_ln(dest_tm=None):
            dest_tm = dest_tm or BA
            ln_stats([(X.t[:, tt * 1024: tt * 1024 + D], X.res[tt:tt + 1])
                      for tt in range(TT)])
            for tt in range(TT):
                dst_ap = dest_tm.t[:, tt * 1024: tt * 1024 + D]
                dst_res = dest_tm.rr(tt * 1024, tt * 1024 + D)
                b.op("s", "activation", dst_ap, X.t[:, tt * 1024: tt * 1024 + D],
                     ACT.Identity, bias=BIASN.t[:, tt:tt + 1],
                     scale=RSD.t[:, tt:tt + 1],
                     r=list(X.res[tt:tt + 1]) + BIASN.res + RSD.res,
                     w=dst_res)
                ngrp = -(-C // 8)
                for gi in range(ngrp):
                    cs = list(range(gi * 8, min(C, gi * 8 + 8)))
                    for k, c2 in enumerate(cs):
                        b.op("t", "matmul", PT.t[:, k * 128:(k + 1) * 128],
                             dst_ap[:, c2 * 128:(c2 + 1) * 128], ident_ap,
                             is_transpose=True, start=(k == 0), stop=(k == len(cs) - 1),
                             r=list(dst_res) + IDENT.res, w=PT.res)
                    b.op("v", "tensor_copy",
                         XNv[:, gi * 8: gi * 8 + len(cs), tt * 128: tt * 128 + 128],
                         PT.t[:, 0: len(cs) * 128].rearrange("p (c j) -> p c j", c=len(cs)),
                         r=PT.res, w=XN.res[gi * 8: gi * 8 + len(cs)])

        def post_ln(src_tiles):
            ln_stats(src_tiles)
            for tt in range(TT):
                ap, res_l = src_tiles[tt]
                xs = tt & 1
                b.op("s", "activation", XT.t[:, xs * 1024: xs * 1024 + D],
                     ap, ACT.Identity, bias=BIASN.t[:, tt:tt + 1],
                     scale=RSD.t[:, tt:tt + 1],
                     r=list(res_l) + BIASN.res + RSD.res,
                     w=XT.res[xs:xs + 1])
                b.op("v", "tensor_add", X.t[:, tt * 1024: tt * 1024 + D],
                     X.t[:, tt * 1024: tt * 1024 + D],
                     XT.t[:, xs * 1024: xs * 1024 + D],
                     r=XT.res[xs:xs + 1] + X.res[tt:tt + 1], w=X.res[tt:tt + 1])

        def proj4(wslot, rhs=None, stop=True):
            src = rhs or XN
            for c2 in range(C):
                for sl2 in range(SL2):
                    tb, col = bank_ap(sl2)
                    b.op("t", "matmul", tb.t[:, col:col + 512], wap(wslot, c2),
                         src.t[:, c2 * T + sl2 * 512: c2 * T + sl2 * 512 + 512],
                         start=(c2 == 0), stop=(stop and c2 == C - 1),
                         r=wres(wslot) + src.rr(c2 * T + sl2 * 512,
                                                c2 * T + sl2 * 512 + 512),
                         w=tb.res[col // 512: col // 512 + 1])

        def proj_segments():
            segs = []
            off = 0
            while off < SL2 * 512:
                tb = PSA if (off // 512) % 4 < 2 else PSB
                wd = min(1024, SL2 * 512 - off)
                segs.append((tb, 0, wd, off))
                off += wd
            return segs

        def proj2(wslot, sl2s, rhs=None):
            src = rhs or XN
            out = []
            for j, sl2 in enumerate(sl2s):
                out.append((PS6 if j == 0 else PS7, sl2))
            for c2 in range(C):
                for j, sl2 in enumerate(sl2s):
                    psb = PS6 if j == 0 else PS7
                    b.op("t", "matmul", psb.t[:, 0:512], wap(wslot, c2),
                         src.t[:, c2 * T + sl2 * 512: c2 * T + sl2 * 512 + 512],
                         start=(c2 == 0), stop=(c2 == C - 1),
                         r=wres(wslot) + src.rr(c2 * T + sl2 * 512,
                                                c2 * T + sl2 * 512 + 512),
                         w=psb.res)
            return out

        def sl2_pairs():
            return [tuple(range(p, min(p + 2, SL2))) for p in range(0, SL2, 2)]

        def transpose_strip(src_ap, src_res, k0, start, stop):
            for k in range(4):
                b.op("t", "matmul", PT.t[:, (k0 + k) * 128:(k0 + k + 1) * 128],
                     src_ap[:, k * 128:(k + 1) * 128], ident_ap, is_transpose=True,
                     start=(start and k == 0), stop=(stop and k == 3),
                     r=list(src_res) + IDENT.res, w=PT.res)

        def fm_to_tm(d, wslot, tiles_of, rhs_buf, epilogue):
            proj4(wslot, rhs=rhs_buf)
            ngrp = 0
            for sl2 in range(SL2):
                tb, col = bank_ap(sl2)
                ps_ap = tb.t[:, col:col + 512]
                ps_res = tb.res[col // 512: col // 512 + 1]
                strip_res = TBF.res[sl2 & 1: (sl2 & 1) + 1]
                strip_ap = TBF.t[:, (sl2 & 1) * 512: (sl2 & 1) * 512 + 512]
                epilogue(ps_ap, ps_res, sl2, strip_ap, strip_res)
                close = (ngrp == 1) or (sl2 == SL2 - 1)
                transpose_strip(strip_ap, strip_res, 4 * ngrp,
                                start=(ngrp == 0), stop=close)
                ngrp += 1
                if close:
                    nt = 4 * ngrp
                    t0 = sl2 * 4 - (nt - 4)
                    dstv, dres = tiles_of(t0, nt)
                    b.op("v", "tensor_copy",
                         dstv[:, :, d * 128:(d + 1) * 128],
                         PT.t[:, 0:nt * 128].rearrange("p (c j) -> p c j", c=nt),
                         r=PT.res, w=dres)
                    ngrp = 0

        def l0_attn():
            pre_ln()
            vstv = VST.t[:, :].rearrange("p (t h e) -> p t h e", t=TT, h=2)
            nh = 2 * C

            def prep_chunks(c):
                wq_s, wk_s, wv_s = (c & 1), 2 + (c & 1), 4 + (c & 1)
                qch, kch, sqch = (c & 1), 2 + (c & 1), 4
                loads = [lambda: wslot_load(wq_s, "Wq", c),
                         lambda: wslot_load(wk_s, "Wk", c),
                         lambda: wslot_load(wv_s, "Wv", c)]
                ch = []

                def load_prep_q(c=c, wq_s=wq_s):
                    w3 = wslice(wq_s).rearrange("p (h e) -> p h e", e=64)
                    b.op("v", "tensor_reduce", SCR.t[:, 0:nh], w3, AX, ALU.add,
                         r=wres(wq_s), w=SCR.res)
                    b.op("v", "tensor_scalar_mul", SCR.t[:, 0:nh], SCR.t[:, 0:nh],
                         1.0 / 64.0, r=SCR.res, w=SCR.res)
                    b.op("v", "tensor_sub", w3, w3,
                         SCR.t[:, 0:nh].unsqueeze(2).broadcast_to([128, nh, 64]),
                         r=wres(wq_s) + SCR.res, w=wres(wq_s))
                ch.append(load_prep_q)

                def load_prep_k(c=c, wk_s=wk_s):
                    k3 = wslice(wk_s).rearrange("p (h e) -> p h e", e=64)
                    b.op("v", "tensor_reduce", SCR.t[:, 0:nh], k3, AX, ALU.add,
                         r=wres(wk_s), w=SCR.res)
                    b.op("v", "tensor_scalar_mul", SCR.t[:, 0:nh], SCR.t[:, 0:nh],
                         1.0 / (64.0 * 64.0), r=SCR.res, w=SCR.res)
                    b.op("v", "tensor_scalar_mul", wslice(wk_s), wslice(wk_s),
                         1.0 / 64.0, r=wres(wk_s), w=wres(wk_s))
                    b.op("v", "tensor_sub", k3, k3,
                         SCR.t[:, 0:nh].unsqueeze(2).broadcast_to([128, nh, 64]),
                         r=wres(wk_s) + SCR.res, w=wres(wk_s))
                ch.append(load_prep_k)

                def mk_proj(wsl, dst_ch, pair):
                    def go(wsl=wsl, dst_ch=dst_ch, pair=pair):
                        for psb, sl2 in proj2(wsl, pair):
                            b.op("v", "tensor_copy",
                                 CH.t[:, dst_ch * T + sl2 * 512:
                                      dst_ch * T + sl2 * 512 + 512],
                                 psb.t[:, 0:512], r=psb.res,
                                 w=CH.res[dst_ch:dst_ch + 1])
                    return go

                for pair in sl2_pairs():
                    ch.append(mk_proj(wq_s, qch, pair))
                for pair in sl2_pairs():
                    ch.append(mk_proj(wk_s, kch, pair))

                def mk_projv(pair):
                    def go(pair=pair, wv_s=wv_s):
                        segs = proj2(wv_s, pair)
                        nt = 4 * len(segs)
                        for j, (psb, sl2) in enumerate(segs):
                            b.op("v", "tensor_copy",
                                 TBF.t[:, j * 512:(j + 1) * 512], psb.t[:, 0:512],
                                 r=psb.res, w=TBF.res[j:j + 1])
                        for j, (psb, sl2) in enumerate(segs):
                            transpose_strip(TBF.t[:, j * 512:(j + 1) * 512],
                                            TBF.res[j:j + 1], 4 * j,
                                            start=(j == 0), stop=(j == len(segs) - 1))
                        t0 = segs[0][1] * 4
                        b.op("v", "tensor_copy",
                             vstv[:, t0:t0 + nt, :, 0:64],
                             PT.t[:, 0:nt * 128].rearrange("p (t h e) -> p t h e",
                                                           t=nt, h=2),
                             r=PT.res, w=VST.res[0:1])
                    return go

                vch = [mk_projv(pair) for pair in sl2_pairs()]

                def mk_sqvar(ch_src, stage, onesb_, sname):
                    def go(ch_src=ch_src, stage=stage, onesb_=onesb_, sname=sname,
                           sqch=sqch):
                        b.op("v", "tensor_mul", CH.t[:, sqch * T: sqch * T + T],
                             CH.t[:, ch_src * T: ch_src * T + T],
                             CH.t[:, ch_src * T: ch_src * T + T],
                             r=CH.res[ch_src:ch_src + 1], w=CH.res[sqch:sqch + 1])
                        banks = [PS4, PS5, PS6, PS7]
                        units = [(hs, g) for hs in range(2) for g in range(SL2)]

                        def mm(k):
                            hs, g = units[k]
                            pb_ = banks[k % 4]
                            b.op("t", "matmul", pb_.t[0:1, 0:512],
                                 onesb_.t[hs * 64:(hs + 1) * 64, 0:1],
                                 CH.t[hs * 64:(hs + 1) * 64,
                                      sqch * T + g * 512: sqch * T + g * 512 + 512],
                                 start=True, stop=True,
                                 r=CH.res[sqch:sqch + 1] + onesb_.res, w=pb_.res)

                        def evac(k):
                            hs, g = units[k]
                            pb_ = banks[k % 4]
                            dst = stage.t[0:1, hs * T + g * 512: hs * T + g * 512 + 512]
                            if k % 2 == 0:
                                b.op("s", "activation", dst, pb_.t[0:1, 0:512],
                                     ACT.Copy, r=pb_.res, w=stage.res[hs:hs + 1])
                            else:
                                b.op("v", "tensor_copy", dst, pb_.t[0:1, 0:512],
                                     r=pb_.res, w=stage.res[hs:hs + 1])

                        for k in range(len(units)):
                            if k >= 4:
                                evac(k - 4)
                            mm(k)
                        for k in range(len(units) - 4, len(units)):
                            evac(k)
                        for hs in range(2):
                            b.op("g", "dma_start", stream=f"s{sname}{hs}",
                                 r=CH.res[ch_src:ch_src + 1], w=stage.res[hs:hs + 1],
                                 out=stage.t[1:65, hs * T: hs * T + T],
                                 in_=CH.t[hs * 64: hs * 64 + 64,
                                          ch_src * T: ch_src * T + T])
                    return go

                vch.append(mk_sqvar(qch, QST, ONESA, "q"))
                vch.append(mk_sqvar(kch, KST, ONESB, "k"))
                return loads, ch, vch

            def attn_pair(c, inject):
                qch = c & 1
                for sl in range(NSL):
                    for grp in range(TT // 4):
                        for hs in range(2):
                            PSX = PSA if hs == 0 else PSB
                            for k in range(4):
                                tt = grp * 4 + k
                                b.op("t", "matmul", PSX.t[:, k * NQ:(k + 1) * NQ],
                                     KST.t[0:65, hs * T + tt * 128: hs * T + tt * 128 + 128],
                                     QST.t[0:65, hs * T + sl * NQ: hs * T + sl * NQ + NQ],
                                     start=(k % 2 == 0), stop=(k % 2 == 1),
                                     r=KST.res[hs:hs + 1] + QST.res[hs:hs + 1],
                                     w=PSX.res[k // 2:k // 2 + 1])
                            bci0 = hs * TT + grp * 4
                            b.op("s", "activation",
                                 BC.t[:, bci0 * NQ: (bci0 + 4) * NQ],
                                 PSX.t[:, 0:4 * NQ], ACT.Exp,
                                 r=PSX.res[0:2], w=BC.res[bci0:bci0 + 4])
                    yps = PS4 if sl % 2 == 0 else PS5
                    for hs in range(2):
                        for tt in range(TT):
                            bci = hs * TT + tt
                            b.op("t", "matmul", yps.t[0:65, hs * NQ:(hs + 1) * NQ],
                                 VST.t[:, tt * 130 + hs * 65: tt * 130 + hs * 65 + 65],
                                 BC.t[:, bci * NQ: (bci + 1) * NQ],
                                 start=(tt == 0), stop=(tt == TT - 1),
                                 r=VST.res[0:1] + BC.res[bci:bci + 1],
                                 w=yps.res)
                    b.op("v", "reciprocal", RRW.t[0:1, 0:2 * NQ],
                         yps.t[64:65, 0:2 * NQ], r=yps.res, w=RRW.res[0:2])
                    rb = TF.t[64:128, 0:2 * NQ]
                    b.op("q", "dma_start", stream=f"rb{sl & 1}",
                         r=RRW.res[0:2], w=TF.res[0:2],
                         out=rb.unsqueeze(1),
                         in_=RRW.t[0:1, 0:2 * NQ]
                         .unsqueeze(1).broadcast_to([1, 64, 2 * NQ]))
                    ycols = c * T + sl * NQ
                    b.op("v", "tensor_mul", BA.t[0:64, ycols: ycols + NQ],
                         yps.t[0:64, 0:NQ], TF.t[64:128, 0:NQ],
                         r=yps.res + TF.res[0:1],
                         w=BA.rr(ycols, ycols + NQ))
                    oslot = sl & 1
                    b.op("v", "tensor_mul",
                         ODD.t[0:64, oslot * NQ: oslot * NQ + NQ],
                         yps.t[0:64, NQ:2 * NQ], TF.t[64:128, NQ:2 * NQ],
                         r=yps.res + TF.res[1:2],
                         w=ODD.res[oslot:oslot + 1])
                    b.op("g", "dma_start", stream=f"od{oslot}",
                         r=ODD.res[oslot:oslot + 1], w=BA.rr(ycols, ycols + NQ),
                         out=BA.t[64:128, ycols: ycols + NQ],
                         in_=ODD.t[0:64, oslot * NQ: oslot * NQ + NQ])
                    if inject:
                        inject.pop(0)()

            loads, ch, vch = prep_chunks(0)
            for f in loads + ch + vch:
                f()
            for c in range(C):
                if c + 1 < C:
                    nloads, nxt, vnxt = prep_chunks(c + 1)
                    for f in nloads:
                        f()
                else:
                    nxt, vnxt = [], []
                attn_pair(c, nxt)
                for f in nxt:
                    f()
                for f in vnxt:
                    f()
            for d in range(C):
                wo_s = (d & 1)
                wslot_load(wo_s, "Wo", d)

                def epi(ps_ap, ps_res, sl2, sap, sres):
                    b.op("v", "tensor_copy", sap, ps_ap, r=ps_res, w=sres)

                fm_to_tm(d, wo_s,
                         lambda t0, nt: (BBv[:, t0:t0 + nt, :], BB.res[t0:t0 + nt]),
                         BA, epi)
            post_ln([(BB.t[:, tt * 1024: tt * 1024 + D], BB.res[tt:tt + 1])
                     for tt in range(TT)])

        def l0_ffn():
            pre_ln()
            for half, wnm, mu_off, dstbuf in (("r", "Wr", 0, BA), ("k", "Wkk", C, BB)):
                for d in range(C):
                    raw, w1s, w2s = (d & 1), 2 + (d & 1), 4 + (d & 1)
                    wslot_load(raw, wnm, d)
                    for c2 in range(C):
                        b.op("v", "tensor_scalar_mul", wap(w1s, c2), wap(raw, c2),
                             MUB.t[:, mu_off + c2: mu_off + c2 + 1],
                             r=wres(raw) + MUB.res[mu_off // C: mu_off // C + 1],
                             w=wres(w1s))
                    b.op("v", "tensor_sub", wslice(w2s), wslice(raw), wslice(w1s),
                         r=wres(raw) + wres(w1s), w=wres(w2s))
                    proj4(w1s, stop=False)
                    for c2 in range(C):
                        for sl2 in range(SL2):
                            tb, col = bank_ap(sl2)
                            wr_ps = tb.res[col // 512: col // 512 + 1]
                            if sl2 == 0:
                                b.op("t", "matmul", tb.t[:, col + 1:col + 512],
                                     wap(w2s, c2), XN.t[:, c2 * T: c2 * T + 511],
                                     start=False, stop=(c2 == C - 1),
                                     r=wres(w2s) + XN.res[c2:c2 + 1], w=wr_ps)
                            else:
                                b.op("t", "matmul", tb.t[:, col:col + 512],
                                     wap(w2s, c2),
                                     XN.t[:, c2 * T + sl2 * 512 - 1: c2 * T + sl2 * 512 + 511],
                                     start=False, stop=(c2 == C - 1),
                                     r=wres(w2s) + XN.res[c2:c2 + 1], w=wr_ps)
                    for tb, col0, wd, off in proj_segments():
                        dcol = d * T + off
                        if half == "r":
                            b.op("s", "activation", dstbuf.t[:, dcol: dcol + wd],
                                 tb.t[:, col0:col0 + wd], ACT.Sigmoid,
                                 r=tb.rr(col0, col0 + wd), w=dstbuf.rr(dcol, dcol + wd))
                        else:
                            b.op("v", "tensor_scalar_max", dstbuf.t[:, dcol: dcol + wd],
                                 tb.t[:, col0:col0 + wd], 0.0,
                                 r=tb.rr(col0, col0 + wd), w=dstbuf.rr(dcol, dcol + wd))
                            b.op("s", "activation", dstbuf.t[:, dcol: dcol + wd],
                                 dstbuf.t[:, dcol: dcol + wd], ACT.Square,
                                 r=dstbuf.rr(dcol, dcol + wd),
                                 w=dstbuf.rr(dcol, dcol + wd))
            LO = min(TT, 8)

            def tiles_of(t0, nt):
                if t0 < LO:
                    return (BCv[:, t0:t0 + nt, :], BC.rr(t0 * 1024, (t0 + nt) * 1024))
                return (CHv[:, t0 - LO:t0 - LO + nt, :],
                        CH.rr((t0 - LO) * 1024, (t0 - LO + nt) * 1024))

            for d in range(C):
                wv_s = (d & 1)
                wslot_load(wv_s, "Wvv", d)

                def epi(ps_ap, ps_res, sl2, sap, sres, d=d):
                    dcol = d * T + sl2 * 512
                    b.op("v", "tensor_mul", sap, ps_ap,
                         BA.t[:, dcol: dcol + 512],
                         r=ps_res + BA.rr(dcol, dcol + 512), w=sres)

                fm_to_tm(d, wv_s, tiles_of, BB, epi)
            src = [(BC.t[:, tt * 1024: tt * 1024 + D], BC.rr(tt * 1024, tt * 1024 + D))
                   if tt < LO else
                   (CH.t[:, (tt - LO) * 1024: (tt - LO) * 1024 + D],
                    CH.rr((tt - LO) * 1024, (tt - LO) * 1024 + D))
                   for tt in range(TT)]
            post_ln(src)

        def l1_attn():
            pre_ln(dest_tm=BA)
            vstv = VST.t[:, :].rearrange("p (t h e) -> p t h e", t=TT, h=2)

            def prep_chunks(c):
                wq_s, wk_s, wk2_s = (c & 1), 2 + (c & 1), 4 + (c & 1)
                qch, kch, kach = (c & 1), 2 + (c & 1), (c & 1)
                loads = [lambda: wslot_load(wq_s, "Wattn", c),
                         lambda: wslot_load(wk_s, "Wattn", C + c),
                         lambda: wslot_load(wk2_s, "Wk2", c)]
                ch = []

                def mk_proj(wsl, dst_ch, pair, scale):
                    def go(wsl=wsl, dst_ch=dst_ch, pair=pair, scale=scale):
                        for psb, sl2 in proj2(wsl, pair):
                            if scale is None:
                                b.op("v", "tensor_copy",
                                     CH.t[:, dst_ch * T + sl2 * 512:
                                          dst_ch * T + sl2 * 512 + 512],
                                     psb.t[:, 0:512], r=psb.res,
                                     w=CH.res[dst_ch:dst_ch + 1])
                            else:
                                b.op("v", "tensor_scalar_mul",
                                     CH.t[:, dst_ch * T + sl2 * 512:
                                          dst_ch * T + sl2 * 512 + 512],
                                     psb.t[:, 0:512], scale, r=psb.res,
                                     w=CH.res[dst_ch:dst_ch + 1])
                    return go

                for pair in sl2_pairs():
                    ch.append(mk_proj(wq_s, qch, pair, None))
                for pair in sl2_pairs():
                    ch.append(mk_proj(wk_s, kch, pair, 0.125))

                after = []

                def mk_projk2(pair):
                    def go(pair=pair, wk2_s=wk2_s, kach=kach):
                        for psb, sl2 in proj2(wk2_s, pair):
                            b.op("s", "activation",
                                 KST.t[:, kach * T + sl2 * 512:
                                       kach * T + sl2 * 512 + 512],
                                 psb.t[:, 0:512], ACT.Sigmoid, scale=0.0025,
                                 r=psb.res, w=KST.res[kach:kach + 1])
                    return go

                for pair in sl2_pairs():
                    after.append(mk_projk2(pair))

                def vstage(c=c):
                    b.op("v", "tensor_copy", vstv[:, :, :, 0:64],
                         BAv[:, :, c * 128:(c + 1) * 128]
                         .rearrange("p t (h e) -> p t h e", h=2),
                         r=BA.res[:TT], w=VST.res[0:1])
                after.append(vstage)
                return loads, ch, after

            def attn_pair(c, inject):
                qch, kch = (c & 1), 2 + (c & 1)
                for sl in range(NSL):
                    ntt = min(2 * sl + 2, TT)
                    for grp in range(-(-ntt // 4)):
                        rem = min(4, ntt - grp * 4)
                        for hs in range(2):
                            PSX = PSA if hs == 0 else PSB
                            for k in range(rem):
                                tt = grp * 4 + k
                                b.op("t", "matmul", PSX.t[:, k * NQ:(k + 1) * NQ],
                                     CH.t[hs * 64:(hs + 1) * 64,
                                          kch * T + tt * 128: kch * T + tt * 128 + 128],
                                     CH.t[hs * 64:(hs + 1) * 64,
                                          qch * T + sl * NQ: qch * T + sl * NQ + NQ],
                                     start=(k % 2 == 0),
                                     stop=(k % 2 == 1 or k == rem - 1),
                                     tile_position=(hs * 64, 0),
                                     r=CH.res[kch:kch + 1] + CH.res[qch:qch + 1],
                                     w=PSX.res[k // 2:k // 2 + 1])
                            bci0 = hs * TT + grp * 4
                            b.op("s", "activation",
                                 BC.t[:, bci0 * NQ: (bci0 + rem) * NQ],
                                 PSX.t[:, 0:rem * NQ], ACT.Exp,
                                 r=PSX.res[0:(rem + 1) // 2],
                                 w=BC.res[bci0:bci0 + rem])
                            for k in range(rem):
                                tt = grp * 4 + k
                                if tt >= 2 * sl:
                                    di = tt - 2 * sl
                                    bci = hs * TT + tt
                                    b.op("v", "tensor_mul",
                                         BC.t[:, bci * NQ: (bci + 1) * NQ],
                                         BC.t[:, bci * NQ: (bci + 1) * NQ],
                                         MASKS.t[:, 129 - 128 * di: 129 - 128 * di + NQ],
                                         r=BC.res[bci:bci + 1] + MASKS.res,
                                         w=BC.res[bci:bci + 1])
                    yps = PS4 if sl % 2 == 0 else PS5
                    for hs in range(2):
                        for tt in range(ntt):
                            bci = hs * TT + tt
                            b.op("t", "matmul", yps.t[0:65, hs * NQ:(hs + 1) * NQ],
                                 VST.t[:, tt * 130 + hs * 65: tt * 130 + hs * 65 + 65],
                                 BC.t[:, bci * NQ: (bci + 1) * NQ],
                                 start=(tt == 0), stop=(tt == ntt - 1),
                                 r=VST.res[0:1] + BC.res[bci:bci + 1],
                                 w=yps.res)
                    b.op("v", "reciprocal", RRW.t[0:1, 0:2 * NQ],
                         yps.t[64:65, 0:2 * NQ], r=yps.res, w=RRW.res[0:2])
                    rb = TF.t[64:128, 0:2 * NQ]
                    b.op("q", "dma_start", stream=f"rb{sl & 1}",
                         r=RRW.res[0:2], w=TF.res[0:2],
                         out=rb.unsqueeze(1),
                         in_=RRW.t[0:1, 0:2 * NQ]
                         .unsqueeze(1).broadcast_to([1, 64, 2 * NQ]))
                    ycols = c * T + sl * NQ
                    b.op("v", "tensor_mul", BB.t[0:64, ycols: ycols + NQ],
                         yps.t[0:64, 0:NQ], TF.t[64:128, 0:NQ],
                         r=yps.res + TF.res[0:1],
                         w=BB.rr(ycols, ycols + NQ))
                    oslot = sl & 1
                    b.op("v", "tensor_mul",
                         ODD.t[0:64, oslot * NQ: oslot * NQ + NQ],
                         yps.t[0:64, NQ:2 * NQ], TF.t[64:128, NQ:2 * NQ],
                         r=yps.res + TF.res[1:2],
                         w=ODD.res[oslot:oslot + 1])
                    b.op("g", "dma_start", stream=f"od{oslot}",
                         r=ODD.res[oslot:oslot + 1], w=BB.rr(ycols, ycols + NQ),
                         out=BB.t[64:128, ycols: ycols + NQ],
                         in_=ODD.t[0:64, oslot * NQ: oslot * NQ + NQ])
                    if inject:
                        inject.pop(0)()

            def wterm_pair(c, inject):
                qch, kch, kach = (c & 1), 2 + (c & 1), (c & 1)
                b.op("v", "tensor_sub", EST.t[:, 0: T - 1],
                     XN.t[:, c * T + 1: c * T + T], BB.t[:, c * T: c * T + T - 1],
                     r=XN.res[c:c + 1] + BB.rr(c * T, c * T + T), w=EST.res[0:1])
                b.op("v", "memset", EST.t[:, T - 1: T], 0.0, w=EST.res[0:1])
                for g in range(-(-TT // 8)):
                    nt = min(8, TT - g * 8)
                    for k in range(nt):
                        tt = g * 8 + k
                        b.op("t", "matmul", PT.t[:, k * 128:(k + 1) * 128],
                             EST.t[:, tt * 128:(tt + 1) * 128], ident_ap,
                             is_transpose=True, start=(k == 0), stop=(k == nt - 1),
                             r=EST.res[0:1] + IDENT.res, w=PT.res)
                    b.op("v", "tensor_copy",
                         EST.t[:, T + g * 1024: T + g * 1024 + nt * 128],
                         PT.t[:, 0:nt * 128], r=PT.res, w=EST.res[1:2])
                b.op("v", "tensor_scalar", EST.t[:, 0:T],
                     CH.t[:, qch * T: qch * T + T], 0.0, 0.125, ALU.min, ALU.mult,
                     r=CH.res[qch:qch + 1], w=EST.res[0:1])
                b.op("v", "tensor_scalar", CH.t[:, qch * T: qch * T + T],
                     CH.t[:, qch * T: qch * T + T], 0.0, 0.0025, ALU.max, ALU.mult,
                     r=CH.res[qch:qch + 1], w=CH.res[qch:qch + 1])
                b.op("v", "tensor_add", CH.t[:, qch * T: qch * T + T],
                     CH.t[:, qch * T: qch * T + T], EST.t[:, 0:T],
                     r=CH.res[qch:qch + 1] + EST.res[0:1], w=CH.res[qch:qch + 1])
                for sl in range(NSL):
                    ntt = min(2 * sl + 2, TT)
                    for grp in range(-(-ntt // 4)):
                        rem = min(4, ntt - grp * 4)
                        for hs in range(2):
                            PSX = PSA if hs == 0 else PSB
                            for k in range(rem):
                                tt = grp * 4 + k
                                b.op("t", "matmul", PSX.t[:, k * NQ:(k + 1) * NQ],
                                     KST.t[hs * 64:(hs + 1) * 64,
                                           kach * T + tt * 128: kach * T + tt * 128 + 128],
                                     CH.t[hs * 64:(hs + 1) * 64,
                                          qch * T + sl * NQ: qch * T + sl * NQ + NQ],
                                     start=(k % 2 == 0),
                                     stop=(k % 2 == 1 or k == rem - 1),
                                     tile_position=(hs * 64, 0),
                                     r=KST.res[kach:kach + 1] + CH.res[qch:qch + 1],
                                     w=PSX.res[k // 2:k // 2 + 1])
                            bci0 = hs * TT + grp * 4
                            kd = max(0, min(2 * sl - grp * 4, rem))
                            if kd > 0:
                                b.op("s", "activation",
                                     BC.t[:, bci0 * NQ: (bci0 + kd) * NQ],
                                     PSX.t[:, 0:kd * NQ], ACT.Copy,
                                     r=PSX.res[0:(kd + 1) // 2],
                                     w=BC.res[bci0:bci0 + kd])
                            for k in range(kd, rem):
                                tt = grp * 4 + k
                                di = tt - 2 * sl
                                bci = hs * TT + tt
                                b.op("v", "tensor_mul",
                                     BC.t[:, bci * NQ: (bci + 1) * NQ],
                                     PSX.t[:, k * NQ:(k + 1) * NQ],
                                     MASKS.t[:, 128 - 128 * di: 128 - 128 * di + NQ],
                                     r=PSX.res[k // 2:k // 2 + 1] + MASKS.res,
                                     w=BC.res[bci:bci + 1])
                    psy = PS4 if sl % 2 == 0 else PS5
                    for tt in range(ntt):
                        for hs in range(2):
                            bci = hs * TT + tt
                            b.op("t", "matmul",
                                 psy.t[hs * 64:(hs + 1) * 64, 0:NQ],
                                 EST.t[:, T + tt * 128 + hs * 64: T + tt * 128 + hs * 64 + 64],
                                 BC.t[:, bci * NQ: (bci + 1) * NQ],
                                 start=(tt == 0), stop=(tt == ntt - 1),
                                 tile_position=(0, hs * 64),
                                 r=EST.res[1:2] + BC.res[bci:bci + 1], w=psy.res)
                    zcols = c * T + sl * NQ
                    b.op("v", "tensor_add", BB.t[:, zcols: zcols + NQ],
                         BB.t[:, zcols: zcols + NQ], psy.t[:, 0:NQ],
                         r=psy.res + BB.rr(zcols, zcols + NQ),
                         w=BB.rr(zcols, zcols + NQ))
                    if inject:
                        inject.pop(0)()

            loads, ch, after = prep_chunks(0)
            for f in loads + ch + after:
                f()
            for c in range(C):
                if c + 1 < C:
                    nloads, nxt, anxt = prep_chunks(c + 1)
                    for f in nloads:
                        f()
                else:
                    nxt, anxt = [], []
                attn_pair(c, nxt)
                for f in nxt:
                    f()
                wterm_pair(c, anxt)
                for f in anxt:
                    f()
            for d in range(C):
                wp_s = (d & 1)
                wslot_load(wp_s, "Wproj", d)

                def epi(ps_ap, ps_res, sl2, sap, sres):
                    b.op("v", "tensor_copy", sap, ps_ap, r=ps_res, w=sres)

                fm_to_tm(d, wp_s,
                         lambda t0, nt: (BAv[:, t0:t0 + nt, :], BA.res[t0:t0 + nt]),
                         BB, epi)
            post_ln([(BA.t[:, tt * 1024: tt * 1024 + D], BA.res[tt:tt + 1])
                     for tt in range(TT)])

        def l1_ffn():
            pre_ln()
            for d in range(C):
                ws = (d & 1)
                wslot_load(ws, "Wsign", d)
                b.op("v", "tensor_scalar", wslice(ws), wslice(ws), 0.0, None,
                     ALU.is_ge, r=wres(ws), w=wres(ws))
                b.op("v", "tensor_scalar", wslice(ws), wslice(ws), 2.0, -1.0,
                     ALU.mult, ALU.add, r=wres(ws), w=wres(ws))

                def epi(ps_ap, ps_res, sl2, sap, sres, d=d):
                    b.op("v", "tensor_scalar", TF.t[:, 0:512],
                         ps_ap, MUB.t[:, 2 * C + d: 2 * C + d + 1],
                         MUB.t[:, 3 * C + d: 3 * C + d + 1], ALU.mult, ALU.add,
                         r=ps_res + MUB.res[2:4], w=TF.res[0:2])
                    b.op("v", "tensor_scalar", sap, TF.t[:, 0:512],
                         100.0, -100.0, ALU.min, ALU.max,
                         r=TF.res[0:2], w=sres)

                fm_to_tm(d, ws,
                         lambda t0, nt: (BAv[:, t0:t0 + nt, :], BA.res[t0:t0 + nt]),
                         XN, epi)
            post_ln([(BA.t[:, tt * 1024: tt * 1024 + D], BA.res[tt:tt + 1])
                     for tt in range(TT)])

        for name, buf in (("ident", IDENT), ("masks", MASKS), ("onesa", ONESA),
                          ("onesb", ONESB), ("ones64f", ONES64F)):
            b.op("q", "dma_start", w=buf.res, stream="qc",
                 out=buf.t[:, :], in_=dram[name][:, :])
        for i, nm in enumerate(("mur_t", "muk_t", "scale_t", "bias_t")):
            b.op("q", "dma_start", w=MUB.res[i:i + 1], stream="qc",
                 out=MUB.t[:, i * C:(i + 1) * C], in_=dram[nm][:, :])
        vv_ones = VST.t[:, :].rearrange("p (t h e) -> p t h e", t=TT, h=2)
        b.op("v", "memset", vv_ones[:, :, :, 64:65], 1.0, w=VST.res[0:1])

        SUBS = {"l0a": l0_attn, "l0f": l0_ffn, "l1a": l1_attn, "l1f": l1_ffn}
        for bi in range(cfg.B):
            for tt in range(TT):
                b.op("g", "dma_start", w=X.res[tt:tt + 1], stream="gx",
                     out=X.t[:, tt * 1024: tt * 1024 + D],
                     in_=dram["x"][bi, tt * 128:(tt + 1) * 128, :])
            for sbk in cfg.sub_blocks:
                SUBS[sbk]()
            for tt in range(TT):
                b.op("g", "dma_start", r=X.res[tt:tt + 1], stream="go",
                     out=out_d[bi, tt * 128:(tt + 1) * 128, :],
                     in_=X.t[:, tt * 1024: tt * 1024 + D])

        b.op("g", None, deps=[("d", "go", b.streams.get("go", 0))])
        b.op("q", None, deps=[("d", "go", b.streams.get("go", 0))])

        b.finalize()
        with contextlib.ExitStack() as sctx:
            ctr_sems = {e: sctx.enter_context(nc.semaphore(f"c_{e}")) for e in ENGINES}
            stream_sems = {s: sctx.enter_context(nc.semaphore(f"d_{s}"))
                           for s in b.streams}
            with nc.Block() as block:
                @block.tensor
                def _(eng):
                    b.replay("t", eng, ctr_sems, stream_sems)

                @block.vector
                def _(eng):
                    b.replay("v", eng, ctr_sems, stream_sems)

                @block.scalar
                def _(eng):
                    b.replay("s", eng, ctr_sems, stream_sems)

                @block.gpsimd
                def _(eng):
                    b.replay("g", eng, ctr_sems, stream_sems)

                @block.sync
                def _(eng):
                    b.replay("q", eng, ctr_sems, stream_sems)

    return nc



def prep_shared(cfg, inputs):
    shared = {}
    shared["Wq"] = tile_w(inputs["L0_attn_Wq"])
    shared["Wk"] = tile_w(inputs["L0_attn_Wk"])
    shared["Wv"] = tile_w(inputs["L0_attn_Wv"])
    shared["Wo"] = tile_w(inputs["L0_attn_Wo"])
    shared["Wr"] = tile_w(np.ascontiguousarray(np.asarray(inputs["L0_ffn_W_r"]).T))
    shared["Wkk"] = tile_w(np.ascontiguousarray(np.asarray(inputs["L0_ffn_W_k"]).T))
    shared["Wvv"] = tile_w(np.ascontiguousarray(np.asarray(inputs["L0_ffn_W_v"]).T))
    shared["Wattn"] = tile_w(inputs["L1_attn_W_attn"])
    shared["Wk2"] = tile_w(inputs["L1_attn_W_k2"])
    shared["Wproj"] = tile_w(inputs["L1_attn_W_proj"])
    shared["Wsign"] = tile_w(np.ascontiguousarray(np.asarray(inputs["L1_ffn_weights"]).T))
    shared["mur_t"] = vec_tile(inputs["L0_ffn_mu_r"])
    shared["muk_t"] = vec_tile(inputs["L0_ffn_mu_k"])
    shared["scale_t"] = vec_tile(inputs["L1_ffn_scale"])
    shared["bias_t"] = vec_tile(inputs["L1_ffn_bias"])
    shared.update(host_consts(cfg))
    return shared


def kernel(**inputs):
    cfg = Cfg()
    shared = prep_shared(cfg, inputs)
    x = np.ascontiguousarray(np.asarray(inputs["L0_attn_x"], np.float32))
    nb_total = x.shape[0]
    ncores = 8
    per = nb_total // ncores
    specs = {k: (v.shape, v.dtype) for k, v in shared.items()}
    specs["x"] = ((per, cfg.T, cfg.D), np.float32)
    nc = build_program(cfg, specs)
    in_maps = []
    for i in range(ncores):
        m = dict(shared)
        m["x"] = np.ascontiguousarray(x[i * per:(i + 1) * per])
        in_maps.append(m)
    res = run_bass_kernel_spmd(nc, in_maps, core_ids=list(range(ncores)))
    out = np.concatenate([np.asarray(r["out"]) for r in res.results], axis=0)
    return out.astype(np.float32)


# version 17 (final)
# speedup vs baseline: 1.0408x; 1.0012x over previous
import contextlib
import numpy as np
import ml_dtypes
import concourse.bass as bass
import concourse.mybir as mybir
from concourse.bass_utils import run_bass_kernel_spmd

F32 = mybir.dt.float32
BF16 = mybir.dt.bfloat16
AX = mybir.AxisListType.X
ALU = mybir.AluOpType
ACT = mybir.ActivationFunctionType
BF = ml_dtypes.bfloat16

ENGINES = ("t", "v", "s", "g", "q")


class Cfg:
    def __init__(self, B=2, T=2048, D=1024, H=16, NQ=256,
                 sub_blocks=("l0a", "l0f", "l1a", "l1f")):
        self.B, self.T, self.D, self.H, self.NQ = B, T, D, H, NQ
        self.DH = D // H
        assert self.DH == 64 and NQ == 256
        self.C = D // 128
        self.TT = T // 128
        self.NSL = T // NQ
        self.SL2 = T // 512
        self.sub_blocks = tuple(sub_blocks)


class Res:
    __slots__ = ("w", "rs")

    def __init__(self):
        self.w = None
        self.rs = []


class Buf:
    def __init__(self, t, ncols, gran):
        self.t = t
        self.gran = gran
        self.res = [Res() for _ in range(-(-ncols // gran))]

    def rr(self, lo, hi):
        return self.res[lo // self.gran: -(-hi // self.gran)]


class Bld:
    def __init__(self):
        self.ops = {e: [] for e in ENGINES}
        self.streams = {}
        self.need_inc = {e: set() for e in ENGINES}

    def op(self, eng, fname, *args, r=(), w=(), deps=(), stream=None, **kw):
        deph = []
        for res in r:
            if res.w is not None:
                deph.append(res.w)
        for res in w:
            if res.w is not None:
                deph.append(res.w)
            deph.extend(res.rs)
        deph.extend(deps)
        deph = [h if h[0] == "e" else ("d", h[1], self.streams[h[1]]) for h in deph]
        seen, dd = set(), []
        for h in deph:
            if h not in seen:
                seen.add(h)
                dd.append(h)
        idx = len(self.ops[eng])
        if stream is not None:
            self.streams[stream] = self.streams.get(stream, 0) + 1
            handle = ("d", stream, self.streams[stream])
        else:
            handle = ("e", eng, idx)
        if eng == "t":
            dd = [h for h in dd if not (h[0] == "e" and h[1] == "t")]
        self.ops[eng].append(dict(f=fname, a=args, k=kw, deps=dd, stream=stream))
        for h in dd:
            if h[0] == "e":
                self.need_inc[h[1]].add(h[2])
        for res in r:
            res.rs.append(handle)
        for res in w:
            res.w = handle
            res.rs = []
        return handle

    def finalize(self):
        self.prefix = {}
        for e in ENGINES:
            pr, c = [], 0
            for i in range(len(self.ops[e])):
                if i in self.need_inc[e]:
                    c += 1
                pr.append(c)
            self.prefix[e] = pr

    def replay(self, eng, obj, ctr_sems, stream_sems):
        covered = {}
        for i, rec in enumerate(self.ops[eng]):
            for h in rec["deps"]:
                if h[0] == "e":
                    key, val, sem = ("e", h[1]), self.prefix[h[1]][h[2]], ctr_sems[h[1]]
                else:
                    key, val, sem = ("d", h[1]), 16 * h[2], stream_sems[h[1]]
                if covered.get(key, 0) < val:
                    obj.wait_ge(sem, val)
                    covered[key] = val
            if rec["f"] is None:
                continue
            try:
                instr = getattr(obj, rec["f"])(*rec["a"], **rec["k"])
            except Exception:
                import sys
                print(f"REPLAY FAIL {eng}[{i}] {rec['f']} args={[getattr(a,'shape',a) for a in rec['a']]}",
                      file=sys.stderr)
                raise
            if rec["stream"] is not None:
                instr.then_inc(stream_sems[rec["stream"]], 16)
            elif i in self.need_inc[eng]:
                instr.then_inc(ctr_sems[eng], 1)



def tile_w(W):
    Din, Dout = W.shape
    return np.ascontiguousarray(
        np.asarray(W, np.float32).reshape(Din // 128, 128, Dout // 128, 128)
        .transpose(2, 0, 1, 3))


def vec_tile(v):
    return np.ascontiguousarray(np.asarray(v, np.float32).reshape(-1, 128).T)


def host_consts(cfg):
    NQ = cfg.NQ
    c = {}
    c["ident"] = np.eye(128, dtype=BF)
    tp = np.arange(128)[:, None]
    u = np.arange(NQ + 130)[None, :]
    c["masks"] = np.ascontiguousarray((u - 129 - tp >= 0).astype(np.float32)).astype(BF)
    c["onesa"] = np.full((128, 1), 1.0 / 63.0, dtype=BF)
    c["onesb"] = np.full((128, 1), 4096.0 / (63.0 * 64.0), dtype=BF)
    c["ones64f"] = np.ones((1, 64), dtype=np.float32)
    return c



def build_program(cfg, input_specs):
    nc = bass.Bass("TRN2", target_bir_lowering=False)
    dram = {}
    for name, (shape, dt) in input_specs.items():
        mdt = F32 if np.dtype(dt) == np.float32 else BF16
        dram[name] = nc.dram_tensor(name, list(shape), mdt, kind="ExternalInput").ap()
    out_d = nc.dram_tensor("out", [cfg.B, cfg.T, cfg.D], F32, kind="ExternalOutput").ap()

    b = Bld()
    T, D, C, TT, NQ, NSL, SL2 = (cfg.T, cfg.D, cfg.C, cfg.TT, cfg.NQ,
                                 cfg.NSL, cfg.SL2)
    ctx = contextlib.ExitStack()

    def S(name, shape, dt=BF16, gran=None):
        t = ctx.enter_context(nc.sbuf_tensor(name, list(shape), dt))
        return Buf(t, shape[-1], gran or shape[-1])

    with ctx:
        X = S("X", [128, TT * 1024], gran=1024)
        XN = S("XN", [128, C * T], gran=T)
        BA = S("BA", [128, max(C * T, TT * 1024)], gran=1024)
        BB = S("BB", [128, max(C * T, TT * 1024)], gran=1024)
        BC = S("BCb", [128, max(2 * TT * NQ, min(TT, 8) * 1024)], gran=NQ)
        CH = S("CH", [128, 5 * T], gran=T)
        VST = S("VST", [128, TT * 130], gran=TT * 130)
        QST = S("QST", [128, 2 * T], gran=T)
        EST = QST
        KST = S("KST", [128, 2 * T], gran=T)
        ODD = S("ODD", [64, 2 * NQ], gran=NQ)
        WP = S("WP", [128, 6 * 1024], gran=1024)
        TBF = S("TBF", [128, 2 * 512], gran=512)
        XT = S("XT", [128, 2 * 1024], gran=1024)
        TF = S("TF", [128, 512], F32, gran=256)
        RRW = TF
        STT = S("STT", [128, 2 * 12], F32, gran=12)
        MV = S("MV", [128, 2 * TT], F32, gran=1)
        RSD = S("RSD", [128, TT], F32)
        BIASN = S("BIASN", [128, TT], F32)
        SUMS = S("SUMS", [128, 2 * TT], F32, gran=1)
        SCR = S("SCR", [128, 3 * TT], F32)

        MUB = S("MUB", [128, 4 * C], F32, gran=C)
        IDENT = S("IDENTB", [128, 128])
        MASKS = S("MASKSB", [128, NQ + 130])
        ONESA = S("ONESAB", [128, 1])
        ONESB = S("ONESBB", [128, 1])
        ONES64F = S("ONES64FB", [1, 64], F32)

        PSA = Buf(ctx.enter_context(nc.psum_tensor("psa", [128, 1024], F32)), 1024, 512)
        PSB = Buf(ctx.enter_context(nc.psum_tensor("psb", [128, 1024], F32)), 1024, 512)
        PS4 = Buf(ctx.enter_context(nc.psum_tensor("ps4", [128, 512], F32)), 512, 512)
        PS5 = Buf(ctx.enter_context(nc.psum_tensor("ps5", [128, 512], F32)), 512, 512)
        PS6 = Buf(ctx.enter_context(nc.psum_tensor("ps6", [128, 512], F32)), 512, 512)
        PS7 = Buf(ctx.enter_context(nc.psum_tensor("ps7", [128, 512], F32)), 512, 512)
        PT = Buf(PS7.t[:, 0:512].bitcast(BF16), 1024, 1024)
        PT.res = PS7.res

        ident_ap = IDENT.t[:, :]
        MVm = lambda tt: MV.t[:, tt:tt + 1]
        BAv = BA.t[:, 0:TT * 1024].rearrange("p (t j) -> p t j", j=1024)
        BBv = BB.t[:, 0:TT * 1024].rearrange("p (t j) -> p t j", j=1024)
        BCv = BC.t[:, 0:min(TT, 8) * 1024].rearrange("p (t j) -> p t j", j=1024)
        CHv = CH.t[:, 0:(TT - min(TT, 8)) * 1024].rearrange("p (t j) -> p t j", j=1024) if TT > 8 else None
        XNv = XN.t.rearrange("p (c j) -> p c j", c=C)

        def wslot_load(slot, wname, d, nchunk=None):
            nck = nchunk or C
            src = dram[wname][d].transpose([1, 0, 2])
            dst = WP.t[:, slot * 1024: slot * 1024 + nck * 128]
            b.op("g", "dma_start", w=WP.res[slot:slot + 1], stream=f"wg{slot}",
                 out=dst.rearrange("p (c j) -> p c j", c=nck), in_=src)

        def wap(slot, c2):
            return WP.t[:, slot * 1024 + c2 * 128: slot * 1024 + (c2 + 1) * 128]

        def wslice(slot, ncols=None):
            return WP.t[:, slot * 1024: slot * 1024 + (ncols or C * 128)]

        def wres(slot):
            return WP.res[slot:slot + 1]

        def bank_ap(sl2):
            tb = PSA if (sl2 % 4) < 2 else PSB
            return tb, (sl2 % 2) * 512

        def rstd_chain():
            b.op("v", "tensor_scalar_add", SCR.t[:, 0:TT], MV.t[:, TT:2 * TT], 1e-5,
                 r=MV.res[TT:2 * TT], w=SCR.res)
            b.op("v", "reciprocal", SCR.t[:, TT:2 * TT], SCR.t[:, 0:TT],
                 r=SCR.res, w=SCR.res)
            b.op("s", "activation", RSD.t[:, :], SCR.t[:, TT:2 * TT], ACT.Sqrt,
                 r=SCR.res, w=RSD.res)

        def ln_stats(tile_aps):
            ng = -(-D // 512)
            odd = [tt for tt in range(TT) if tt & 1]
            for tt in range(TT):
                ap, res_l = tile_aps[tt]
                st = (tt // 2) & 1
                if tt & 1:
                    xs = (tt // 2) & 1
                    b.op("s", "activation", XT.t[:, xs * 1024: xs * 1024 + D],
                         ap, ACT.Copy, accum_out=SUMS.t[:, tt:tt + 1],
                         r=res_l, w=XT.res[xs:xs + 1] + SUMS.res[tt:tt + 1])
                    b.op("s", "activation", XT.t[:, xs * 1024: xs * 1024 + D],
                         ap, ACT.Square, accum_out=SUMS.t[:, TT + tt:TT + tt + 1],
                         r=res_l,
                         w=XT.res[xs:xs + 1] + SUMS.res[TT + tt:TT + tt + 1])
                    continue
                for g in range(ng):
                    wseg = min(512, D - g * 512)
                    b.op("v", "bn_stats",
                         STT.t[:, st * 12 + g * 6: st * 12 + g * 6 + 6],
                         ap[:, g * 512: g * 512 + wseg], r=res_l, w=STT.res[st:st + 1])
                mvout = MV.t.rearrange("p (two t) -> p two t", two=2)[:, :, tt]
                b.op("v", "bn_aggr", mvout, STT.t[:, st * 12: st * 12 + ng * 6],
                     r=STT.res[st:st + 1],
                     w=MV.res[tt:tt + 1] + MV.res[TT + tt:TT + tt + 1])
            mv2 = MV.t.rearrange("p (two t2 o) -> p two t2 o", two=2, o=2)
            s2v = SUMS.t.rearrange("p (two t2 o) -> p two t2 o", two=2, o=2)
            b.op("v", "tensor_scalar_mul", mv2[:, 0, :, 1], s2v[:, 0, :, 1],
                 1.0 / D, r=[SUMS.res[tt] for tt in odd],
                 w=[MV.res[tt] for tt in odd])
            b.op("v", "tensor_scalar_mul", mv2[:, 1, :, 1], s2v[:, 1, :, 1],
                 1.0 / D, r=[SUMS.res[TT + tt] for tt in odd],
                 w=[MV.res[TT + tt] for tt in odd])
            b.op("v", "tensor_mul", SCR.t[:, 0:TT // 2], mv2[:, 0, :, 1],
                 mv2[:, 0, :, 1], r=[MV.res[tt] for tt in odd], w=SCR.res)
            b.op("v", "tensor_sub", mv2[:, 1, :, 1], mv2[:, 1, :, 1],
                 SCR.t[:, 0:TT // 2],
                 r=SCR.res + [MV.res[TT + tt] for tt in odd],
                 w=[MV.res[TT + tt] for tt in odd])
            rstd_chain()
            b.op("v", "scalar_tensor_tensor", BIASN.t[:, :], MV.t[:, 0:TT], -1.0,
                 RSD.t[:, :], ALU.mult, ALU.mult,
                 r=MV.res[0:TT] + RSD.res, w=BIASN.res)

        def pre_ln(dest_tm=None):
            dest_tm = dest_tm or BA
            ln_stats([(X.t[:, tt * 1024: tt * 1024 + D], X.res[tt:tt + 1])
                      for tt in range(TT)])
            for tt in range(TT):
                dst_ap = dest_tm.t[:, tt * 1024: tt * 1024 + D]
                dst_res = dest_tm.rr(tt * 1024, tt * 1024 + D)
                b.op("s", "activation", dst_ap, X.t[:, tt * 1024: tt * 1024 + D],
                     ACT.Identity, bias=BIASN.t[:, tt:tt + 1],
                     scale=RSD.t[:, tt:tt + 1],
                     r=list(X.res[tt:tt + 1]) + BIASN.res + RSD.res,
                     w=dst_res)
                ngrp = -(-C // 8)
                for gi in range(ngrp):
                    cs = list(range(gi * 8, min(C, gi * 8 + 8)))
                    for k, c2 in enumerate(cs):
                        b.op("t", "matmul", PT.t[:, k * 128:(k + 1) * 128],
                             dst_ap[:, c2 * 128:(c2 + 1) * 128], ident_ap,
                             is_transpose=True, start=(k == 0), stop=(k == len(cs) - 1),
                             r=list(dst_res) + IDENT.res, w=PT.res)
                    b.op("v", "tensor_copy",
                         XNv[:, gi * 8: gi * 8 + len(cs), tt * 128: tt * 128 + 128],
                         PT.t[:, 0: len(cs) * 128].rearrange("p (c j) -> p c j", c=len(cs)),
                         r=PT.res, w=XN.res[gi * 8: gi * 8 + len(cs)])

        def post_ln(src_tiles):
            ln_stats(src_tiles)
            for tt in range(TT):
                ap, res_l = src_tiles[tt]
                xs = tt & 1
                b.op("s", "activation", XT.t[:, xs * 1024: xs * 1024 + D],
                     ap, ACT.Identity, bias=BIASN.t[:, tt:tt + 1],
                     scale=RSD.t[:, tt:tt + 1],
                     r=list(res_l) + BIASN.res + RSD.res,
                     w=XT.res[xs:xs + 1])
                b.op("v", "tensor_add", X.t[:, tt * 1024: tt * 1024 + D],
                     X.t[:, tt * 1024: tt * 1024 + D],
                     XT.t[:, xs * 1024: xs * 1024 + D],
                     r=XT.res[xs:xs + 1] + X.res[tt:tt + 1], w=X.res[tt:tt + 1])

        def proj4(wslot, rhs=None, stop=True):
            src = rhs or XN
            for c2 in range(C):
                for sl2 in range(SL2):
                    tb, col = bank_ap(sl2)
                    b.op("t", "matmul", tb.t[:, col:col + 512], wap(wslot, c2),
                         src.t[:, c2 * T + sl2 * 512: c2 * T + sl2 * 512 + 512],
                         start=(c2 == 0), stop=(stop and c2 == C - 1),
                         r=wres(wslot) + src.rr(c2 * T + sl2 * 512,
                                                c2 * T + sl2 * 512 + 512),
                         w=tb.res[col // 512: col // 512 + 1])

        def proj_segments():
            segs = []
            off = 0
            while off < SL2 * 512:
                tb = PSA if (off // 512) % 4 < 2 else PSB
                wd = min(1024, SL2 * 512 - off)
                segs.append((tb, 0, wd, off))
                off += wd
            return segs

        def proj2(wslot, sl2s, rhs=None):
            src = rhs or XN
            out = []
            for j, sl2 in enumerate(sl2s):
                out.append((PS6 if j == 0 else PS7, sl2))
            for c2 in range(C):
                for j, sl2 in enumerate(sl2s):
                    psb = PS6 if j == 0 else PS7
                    b.op("t", "matmul", psb.t[:, 0:512], wap(wslot, c2),
                         src.t[:, c2 * T + sl2 * 512: c2 * T + sl2 * 512 + 512],
                         start=(c2 == 0), stop=(c2 == C - 1),
                         r=wres(wslot) + src.rr(c2 * T + sl2 * 512,
                                                c2 * T + sl2 * 512 + 512),
                         w=psb.res)
            return out

        def sl2_pairs():
            return [tuple(range(p, min(p + 2, SL2))) for p in range(0, SL2, 2)]

        def transpose_strip(src_ap, src_res, k0, start, stop):
            for k in range(4):
                b.op("t", "matmul", PT.t[:, (k0 + k) * 128:(k0 + k + 1) * 128],
                     src_ap[:, k * 128:(k + 1) * 128], ident_ap, is_transpose=True,
                     start=(start and k == 0), stop=(stop and k == 3),
                     r=list(src_res) + IDENT.res, w=PT.res)

        def fm_to_tm(d, wslot, tiles_of, rhs_buf, epilogue):
            proj4(wslot, rhs=rhs_buf)
            ngrp = 0
            for sl2 in range(SL2):
                tb, col = bank_ap(sl2)
                ps_ap = tb.t[:, col:col + 512]
                ps_res = tb.res[col // 512: col // 512 + 1]
                strip_res = TBF.res[sl2 & 1: (sl2 & 1) + 1]
                strip_ap = TBF.t[:, (sl2 & 1) * 512: (sl2 & 1) * 512 + 512]
                epilogue(ps_ap, ps_res, sl2, strip_ap, strip_res)
                close = (ngrp == 1) or (sl2 == SL2 - 1)
                transpose_strip(strip_ap, strip_res, 4 * ngrp,
                                start=(ngrp == 0), stop=close)
                ngrp += 1
                if close:
                    nt = 4 * ngrp
                    t0 = sl2 * 4 - (nt - 4)
                    dstv, dres = tiles_of(t0, nt)
                    b.op("v", "tensor_copy",
                         dstv[:, :, d * 128:(d + 1) * 128],
                         PT.t[:, 0:nt * 128].rearrange("p (c j) -> p c j", c=nt),
                         r=PT.res, w=dres)
                    ngrp = 0

        def l0_attn():
            pre_ln()
            vstv = VST.t[:, :].rearrange("p (t h e) -> p t h e", t=TT, h=2)
            nh = 2 * C

            def prep_chunks(c):
                wq_s, wk_s, wv_s = (c & 1), 2 + (c & 1), 4 + (c & 1)
                qch, kch, sqch = (c & 1), 2 + (c & 1), 4
                loads = [lambda: wslot_load(wq_s, "Wq", c),
                         lambda: wslot_load(wk_s, "Wk", c),
                         lambda: wslot_load(wv_s, "Wv", c)]
                ch = []

                def load_prep_q(c=c, wq_s=wq_s):
                    w3 = wslice(wq_s).rearrange("p (h e) -> p h e", e=64)
                    b.op("v", "tensor_reduce", SCR.t[:, 0:nh], w3, AX, ALU.add,
                         r=wres(wq_s), w=SCR.res)
                    b.op("v", "tensor_scalar_mul", SCR.t[:, 0:nh], SCR.t[:, 0:nh],
                         1.0 / 64.0, r=SCR.res, w=SCR.res)
                    b.op("v", "tensor_sub", w3, w3,
                         SCR.t[:, 0:nh].unsqueeze(2).broadcast_to([128, nh, 64]),
                         r=wres(wq_s) + SCR.res, w=wres(wq_s))
                ch.append(load_prep_q)

                def load_prep_k(c=c, wk_s=wk_s):
                    k3 = wslice(wk_s).rearrange("p (h e) -> p h e", e=64)
                    b.op("v", "tensor_reduce", SCR.t[:, 0:nh], k3, AX, ALU.add,
                         r=wres(wk_s), w=SCR.res)
                    b.op("v", "tensor_scalar_mul", SCR.t[:, 0:nh], SCR.t[:, 0:nh],
                         1.0 / (64.0 * 64.0), r=SCR.res, w=SCR.res)
                    b.op("v", "tensor_scalar_mul", wslice(wk_s), wslice(wk_s),
                         1.0 / 64.0, r=wres(wk_s), w=wres(wk_s))
                    b.op("v", "tensor_sub", k3, k3,
                         SCR.t[:, 0:nh].unsqueeze(2).broadcast_to([128, nh, 64]),
                         r=wres(wk_s) + SCR.res, w=wres(wk_s))
                ch.append(load_prep_k)

                def mk_proj(wsl, dst_ch, pair):
                    def go(wsl=wsl, dst_ch=dst_ch, pair=pair):
                        for psb, sl2 in proj2(wsl, pair):
                            b.op("v", "tensor_copy",
                                 CH.t[:, dst_ch * T + sl2 * 512:
                                      dst_ch * T + sl2 * 512 + 512],
                                 psb.t[:, 0:512], r=psb.res,
                                 w=CH.res[dst_ch:dst_ch + 1])
                    return go

                for pair in sl2_pairs():
                    ch.append(mk_proj(wq_s, qch, pair))
                for pair in sl2_pairs():
                    ch.append(mk_proj(wk_s, kch, pair))

                def mk_projv(pair):
                    def go(pair=pair, wv_s=wv_s):
                        segs = proj2(wv_s, pair)
                        nt = 4 * len(segs)
                        for j, (psb, sl2) in enumerate(segs):
                            b.op("v", "tensor_copy",
                                 TBF.t[:, j * 512:(j + 1) * 512], psb.t[:, 0:512],
                                 r=psb.res, w=TBF.res[j:j + 1])
                        for j, (psb, sl2) in enumerate(segs):
                            transpose_strip(TBF.t[:, j * 512:(j + 1) * 512],
                                            TBF.res[j:j + 1], 4 * j,
                                            start=(j == 0), stop=(j == len(segs) - 1))
                        t0 = segs[0][1] * 4
                        b.op("v", "tensor_copy",
                             vstv[:, t0:t0 + nt, :, 0:64],
                             PT.t[:, 0:nt * 128].rearrange("p (t h e) -> p t h e",
                                                           t=nt, h=2),
                             r=PT.res, w=VST.res[0:1])
                    return go

                vch = [mk_projv(pair) for pair in sl2_pairs()]

                def mk_sqvar(ch_src, stage, onesb_, sname):
                    def go(ch_src=ch_src, stage=stage, onesb_=onesb_, sname=sname,
                           sqch=sqch):
                        b.op("v", "tensor_mul", CH.t[:, sqch * T: sqch * T + T],
                             CH.t[:, ch_src * T: ch_src * T + T],
                             CH.t[:, ch_src * T: ch_src * T + T],
                             r=CH.res[ch_src:ch_src + 1], w=CH.res[sqch:sqch + 1])
                        banks = [PS4, PS5, PS6, PS7]
                        units = [(hs, g) for hs in range(2) for g in range(SL2)]

                        def mm(k):
                            hs, g = units[k]
                            pb_ = banks[k % 4]
                            b.op("t", "matmul", pb_.t[0:1, 0:512],
                                 onesb_.t[hs * 64:(hs + 1) * 64, 0:1],
                                 CH.t[hs * 64:(hs + 1) * 64,
                                      sqch * T + g * 512: sqch * T + g * 512 + 512],
                                 start=True, stop=True,
                                 r=CH.res[sqch:sqch + 1] + onesb_.res, w=pb_.res)

                        def evac(k):
                            hs, g = units[k]
                            pb_ = banks[k % 4]
                            dst = stage.t[0:1, hs * T + g * 512: hs * T + g * 512 + 512]
                            if k % 2 == 0:
                                b.op("s", "activation", dst, pb_.t[0:1, 0:512],
                                     ACT.Copy, r=pb_.res, w=stage.res[hs:hs + 1])
                            else:
                                b.op("v", "tensor_copy", dst, pb_.t[0:1, 0:512],
                                     r=pb_.res, w=stage.res[hs:hs + 1])

                        for k in range(len(units)):
                            if k >= 4:
                                evac(k - 4)
                            mm(k)
                        for k in range(len(units) - 4, len(units)):
                            evac(k)
                        for hs in range(2):
                            b.op("g", "dma_start", stream=f"s{sname}{hs}",
                                 r=CH.res[ch_src:ch_src + 1], w=stage.res[hs:hs + 1],
                                 out=stage.t[1:65, hs * T: hs * T + T],
                                 in_=CH.t[hs * 64: hs * 64 + 64,
                                          ch_src * T: ch_src * T + T])
                    return go

                vch.append(mk_sqvar(qch, QST, ONESA, "q"))
                vch.append(mk_sqvar(kch, KST, ONESB, "k"))
                return loads, ch, vch

            def attn_pair(c, inject):
                qch = c & 1
                for sl in range(NSL):
                    for grp in range(TT // 4):
                        for hs in range(2):
                            PSX = PSA if hs == 0 else PSB
                            for k in range(4):
                                tt = grp * 4 + k
                                b.op("t", "matmul", PSX.t[:, k * NQ:(k + 1) * NQ],
                                     KST.t[0:65, hs * T + tt * 128: hs * T + tt * 128 + 128],
                                     QST.t[0:65, hs * T + sl * NQ: hs * T + sl * NQ + NQ],
                                     start=(k % 2 == 0), stop=(k % 2 == 1),
                                     r=KST.res[hs:hs + 1] + QST.res[hs:hs + 1],
                                     w=PSX.res[k // 2:k // 2 + 1])
                            bci0 = hs * TT + grp * 4
                            b.op("s", "activation",
                                 BC.t[:, bci0 * NQ: (bci0 + 4) * NQ],
                                 PSX.t[:, 0:4 * NQ], ACT.Exp,
                                 r=PSX.res[0:2], w=BC.res[bci0:bci0 + 4])
                    yps = PS4 if sl % 2 == 0 else PS5
                    for hs in range(2):
                        for tt in range(TT):
                            bci = hs * TT + tt
                            b.op("t", "matmul", yps.t[0:65, hs * NQ:(hs + 1) * NQ],
                                 VST.t[:, tt * 130 + hs * 65: tt * 130 + hs * 65 + 65],
                                 BC.t[:, bci * NQ: (bci + 1) * NQ],
                                 start=(tt == 0), stop=(tt == TT - 1),
                                 r=VST.res[0:1] + BC.res[bci:bci + 1],
                                 w=yps.res)
                    b.op("v", "reciprocal", RRW.t[0:1, 0:2 * NQ],
                         yps.t[64:65, 0:2 * NQ], r=yps.res, w=RRW.res[0:2])
                    rb = TF.t[64:128, 0:2 * NQ]
                    b.op("q", "dma_start", stream=f"rb{sl & 1}",
                         r=RRW.res[0:2], w=TF.res[0:2],
                         out=rb.unsqueeze(1),
                         in_=RRW.t[0:1, 0:2 * NQ]
                         .unsqueeze(1).broadcast_to([1, 64, 2 * NQ]))
                    ycols = c * T + sl * NQ
                    b.op("v", "tensor_mul", BA.t[0:64, ycols: ycols + NQ],
                         yps.t[0:64, 0:NQ], TF.t[64:128, 0:NQ],
                         r=yps.res + TF.res[0:1],
                         w=BA.rr(ycols, ycols + NQ))
                    oslot = sl & 1
                    b.op("v", "tensor_mul",
                         ODD.t[0:64, oslot * NQ: oslot * NQ + NQ],
                         yps.t[0:64, NQ:2 * NQ], TF.t[64:128, NQ:2 * NQ],
                         r=yps.res + TF.res[1:2],
                         w=ODD.res[oslot:oslot + 1])
                    b.op("g", "dma_start", stream=f"od{oslot}",
                         r=ODD.res[oslot:oslot + 1], w=BA.rr(ycols, ycols + NQ),
                         out=BA.t[64:128, ycols: ycols + NQ],
                         in_=ODD.t[0:64, oslot * NQ: oslot * NQ + NQ])
                    if inject:
                        inject.pop(0)()

            loads, ch, vch = prep_chunks(0)
            for f in loads + ch + vch:
                f()
            for c in range(C):
                if c + 1 < C:
                    nloads, nxt, vnxt = prep_chunks(c + 1)
                    for f in nloads:
                        f()
                else:
                    nxt, vnxt = [], []
                attn_pair(c, nxt)
                for f in nxt:
                    f()
                for f in vnxt:
                    f()
            for d in range(C):
                wo_s = (d & 1)
                wslot_load(wo_s, "Wo", d)

                def epi(ps_ap, ps_res, sl2, sap, sres):
                    b.op("v", "tensor_copy", sap, ps_ap, r=ps_res, w=sres)

                fm_to_tm(d, wo_s,
                         lambda t0, nt: (BBv[:, t0:t0 + nt, :], BB.res[t0:t0 + nt]),
                         BA, epi)
            post_ln([(BB.t[:, tt * 1024: tt * 1024 + D], BB.res[tt:tt + 1])
                     for tt in range(TT)])

        def l0_ffn():
            pre_ln()
            for half, wnm, mu_off, dstbuf in (("r", "Wr", 0, BA), ("k", "Wkk", C, BB)):
                for d in range(C):
                    raw, w1s, w2s = (d & 1), 2 + (d & 1), 4 + (d & 1)
                    wslot_load(raw, wnm, d)
                    for c2 in range(C):
                        b.op("v", "tensor_scalar_mul", wap(w1s, c2), wap(raw, c2),
                             MUB.t[:, mu_off + c2: mu_off + c2 + 1],
                             r=wres(raw) + MUB.res[mu_off // C: mu_off // C + 1],
                             w=wres(w1s))
                    b.op("v", "tensor_sub", wslice(w2s), wslice(raw), wslice(w1s),
                         r=wres(raw) + wres(w1s), w=wres(w2s))
                    proj4(w1s, stop=False)
                    for c2 in range(C):
                        for sl2 in range(SL2):
                            tb, col = bank_ap(sl2)
                            wr_ps = tb.res[col // 512: col // 512 + 1]
                            if sl2 == 0:
                                b.op("t", "matmul", tb.t[:, col + 1:col + 512],
                                     wap(w2s, c2), XN.t[:, c2 * T: c2 * T + 511],
                                     start=False, stop=(c2 == C - 1),
                                     r=wres(w2s) + XN.res[c2:c2 + 1], w=wr_ps)
                            else:
                                b.op("t", "matmul", tb.t[:, col:col + 512],
                                     wap(w2s, c2),
                                     XN.t[:, c2 * T + sl2 * 512 - 1: c2 * T + sl2 * 512 + 511],
                                     start=False, stop=(c2 == C - 1),
                                     r=wres(w2s) + XN.res[c2:c2 + 1], w=wr_ps)
                    for tb, col0, wd, off in proj_segments():
                        dcol = d * T + off
                        if half == "r":
                            b.op("s", "activation", dstbuf.t[:, dcol: dcol + wd],
                                 tb.t[:, col0:col0 + wd], ACT.Sigmoid,
                                 r=tb.rr(col0, col0 + wd), w=dstbuf.rr(dcol, dcol + wd))
                        else:
                            b.op("v", "tensor_scalar_max", dstbuf.t[:, dcol: dcol + wd],
                                 tb.t[:, col0:col0 + wd], 0.0,
                                 r=tb.rr(col0, col0 + wd), w=dstbuf.rr(dcol, dcol + wd))
                            b.op("s", "activation", dstbuf.t[:, dcol: dcol + wd],
                                 dstbuf.t[:, dcol: dcol + wd], ACT.Square,
                                 r=dstbuf.rr(dcol, dcol + wd),
                                 w=dstbuf.rr(dcol, dcol + wd))
            LO = min(TT, 8)

            def tiles_of(t0, nt):
                if t0 < LO:
                    return (BCv[:, t0:t0 + nt, :], BC.rr(t0 * 1024, (t0 + nt) * 1024))
                return (CHv[:, t0 - LO:t0 - LO + nt, :],
                        CH.rr((t0 - LO) * 1024, (t0 - LO + nt) * 1024))

            for d in range(C):
                wv_s = (d & 1)
                wslot_load(wv_s, "Wvv", d)

                def epi(ps_ap, ps_res, sl2, sap, sres, d=d):
                    dcol = d * T + sl2 * 512
                    b.op("v", "tensor_mul", sap, ps_ap,
                         BA.t[:, dcol: dcol + 512],
                         r=ps_res + BA.rr(dcol, dcol + 512), w=sres)

                fm_to_tm(d, wv_s, tiles_of, BB, epi)
            src = [(BC.t[:, tt * 1024: tt * 1024 + D], BC.rr(tt * 1024, tt * 1024 + D))
                   if tt < LO else
                   (CH.t[:, (tt - LO) * 1024: (tt - LO) * 1024 + D],
                    CH.rr((tt - LO) * 1024, (tt - LO) * 1024 + D))
                   for tt in range(TT)]
            post_ln(src)

        def l1_attn():
            pre_ln(dest_tm=BA)
            vstv = VST.t[:, :].rearrange("p (t h e) -> p t h e", t=TT, h=2)

            def prep_chunks(c):
                wq_s, wk_s, wk2_s = (c & 1), 2 + (c & 1), 4 + (c & 1)
                qch, kch, kach = (c & 1), 2 + (c & 1), (c & 1)
                loads = [lambda: wslot_load(wq_s, "Wattn", c),
                         lambda: wslot_load(wk_s, "Wattn", C + c),
                         lambda: wslot_load(wk2_s, "Wk2", c)]
                ch = []

                def mk_proj(wsl, dst_ch, pair, scale):
                    def go(wsl=wsl, dst_ch=dst_ch, pair=pair, scale=scale):
                        for psb, sl2 in proj2(wsl, pair):
                            if scale is None:
                                b.op("v", "tensor_copy",
                                     CH.t[:, dst_ch * T + sl2 * 512:
                                          dst_ch * T + sl2 * 512 + 512],
                                     psb.t[:, 0:512], r=psb.res,
                                     w=CH.res[dst_ch:dst_ch + 1])
                            else:
                                b.op("v", "tensor_scalar_mul",
                                     CH.t[:, dst_ch * T + sl2 * 512:
                                          dst_ch * T + sl2 * 512 + 512],
                                     psb.t[:, 0:512], scale, r=psb.res,
                                     w=CH.res[dst_ch:dst_ch + 1])
                    return go

                for pair in sl2_pairs():
                    ch.append(mk_proj(wq_s, qch, pair, None))
                for pair in sl2_pairs():
                    ch.append(mk_proj(wk_s, kch, pair, 0.125))

                after = []

                def mk_projk2(pair):
                    def go(pair=pair, wk2_s=wk2_s, kach=kach):
                        for psb, sl2 in proj2(wk2_s, pair):
                            b.op("s", "activation",
                                 KST.t[:, kach * T + sl2 * 512:
                                       kach * T + sl2 * 512 + 512],
                                 psb.t[:, 0:512], ACT.Sigmoid, scale=0.0025,
                                 r=psb.res, w=KST.res[kach:kach + 1])
                    return go

                for pair in sl2_pairs():
                    after.append(mk_projk2(pair))

                def vstage(c=c):
                    b.op("v", "tensor_copy", vstv[:, :, :, 0:64],
                         BAv[:, :, c * 128:(c + 1) * 128]
                         .rearrange("p t (h e) -> p t h e", h=2),
                         r=BA.res[:TT], w=VST.res[0:1])
                after.append(vstage)
                return loads, ch, after

            def attn_pair(c, inject):
                qch, kch = (c & 1), 2 + (c & 1)
                for sl in range(NSL):
                    ntt = min(2 * sl + 2, TT)
                    for grp in range(-(-ntt // 4)):
                        rem = min(4, ntt - grp * 4)
                        for hs in range(2):
                            PSX = PSA if hs == 0 else PSB
                            for k in range(rem):
                                tt = grp * 4 + k
                                b.op("t", "matmul", PSX.t[:, k * NQ:(k + 1) * NQ],
                                     CH.t[hs * 64:(hs + 1) * 64,
                                          kch * T + tt * 128: kch * T + tt * 128 + 128],
                                     CH.t[hs * 64:(hs + 1) * 64,
                                          qch * T + sl * NQ: qch * T + sl * NQ + NQ],
                                     start=(k % 2 == 0),
                                     stop=(k % 2 == 1 or k == rem - 1),
                                     tile_position=(hs * 64, 0),
                                     r=CH.res[kch:kch + 1] + CH.res[qch:qch + 1],
                                     w=PSX.res[k // 2:k // 2 + 1])
                            bci0 = hs * TT + grp * 4
                            b.op("s", "activation",
                                 BC.t[:, bci0 * NQ: (bci0 + rem) * NQ],
                                 PSX.t[:, 0:rem * NQ], ACT.Exp,
                                 r=PSX.res[0:(rem + 1) // 2],
                                 w=BC.res[bci0:bci0 + rem])
                            for k in range(rem):
                                tt = grp * 4 + k
                                if tt >= 2 * sl:
                                    di = tt - 2 * sl
                                    bci = hs * TT + tt
                                    b.op("v", "tensor_mul",
                                         BC.t[:, bci * NQ: (bci + 1) * NQ],
                                         BC.t[:, bci * NQ: (bci + 1) * NQ],
                                         MASKS.t[:, 129 - 128 * di: 129 - 128 * di + NQ],
                                         r=BC.res[bci:bci + 1] + MASKS.res,
                                         w=BC.res[bci:bci + 1])
                    yps = PS4 if sl % 2 == 0 else PS5
                    for hs in range(2):
                        for tt in range(ntt):
                            bci = hs * TT + tt
                            b.op("t", "matmul", yps.t[0:65, hs * NQ:(hs + 1) * NQ],
                                 VST.t[:, tt * 130 + hs * 65: tt * 130 + hs * 65 + 65],
                                 BC.t[:, bci * NQ: (bci + 1) * NQ],
                                 start=(tt == 0), stop=(tt == ntt - 1),
                                 r=VST.res[0:1] + BC.res[bci:bci + 1],
                                 w=yps.res)
                    b.op("v", "reciprocal", RRW.t[0:1, 0:2 * NQ],
                         yps.t[64:65, 0:2 * NQ], r=yps.res, w=RRW.res[0:2])
                    rb = TF.t[64:128, 0:2 * NQ]
                    b.op("q", "dma_start", stream=f"rb{sl & 1}",
                         r=RRW.res[0:2], w=TF.res[0:2],
                         out=rb.unsqueeze(1),
                         in_=RRW.t[0:1, 0:2 * NQ]
                         .unsqueeze(1).broadcast_to([1, 64, 2 * NQ]))
                    ycols = c * T + sl * NQ
                    b.op("v", "tensor_mul", BB.t[0:64, ycols: ycols + NQ],
                         yps.t[0:64, 0:NQ], TF.t[64:128, 0:NQ],
                         r=yps.res + TF.res[0:1],
                         w=BB.rr(ycols, ycols + NQ))
                    oslot = sl & 1
                    b.op("v", "tensor_mul",
                         ODD.t[0:64, oslot * NQ: oslot * NQ + NQ],
                         yps.t[0:64, NQ:2 * NQ], TF.t[64:128, NQ:2 * NQ],
                         r=yps.res + TF.res[1:2],
                         w=ODD.res[oslot:oslot + 1])
                    b.op("g", "dma_start", stream=f"od{oslot}",
                         r=ODD.res[oslot:oslot + 1], w=BB.rr(ycols, ycols + NQ),
                         out=BB.t[64:128, ycols: ycols + NQ],
                         in_=ODD.t[0:64, oslot * NQ: oslot * NQ + NQ])
                    if inject:
                        inject.pop(0)()

            def wterm_pair(c, inject):
                qch, kch, kach = (c & 1), 2 + (c & 1), (c & 1)
                b.op("v", "tensor_sub", EST.t[:, 0: T - 1],
                     XN.t[:, c * T + 1: c * T + T], BB.t[:, c * T: c * T + T - 1],
                     r=XN.res[c:c + 1] + BB.rr(c * T, c * T + T), w=EST.res[0:1])
                b.op("v", "memset", EST.t[:, T - 1: T], 0.0, w=EST.res[0:1])
                for g in range(-(-TT // 8)):
                    nt = min(8, TT - g * 8)
                    for k in range(nt):
                        tt = g * 8 + k
                        b.op("t", "matmul", PT.t[:, k * 128:(k + 1) * 128],
                             EST.t[:, tt * 128:(tt + 1) * 128], ident_ap,
                             is_transpose=True, start=(k == 0), stop=(k == nt - 1),
                             r=EST.res[0:1] + IDENT.res, w=PT.res)
                    b.op("v", "tensor_copy",
                         EST.t[:, T + g * 1024: T + g * 1024 + nt * 128],
                         PT.t[:, 0:nt * 128], r=PT.res, w=EST.res[1:2])
                b.op("v", "tensor_scalar", EST.t[:, 0:T],
                     CH.t[:, qch * T: qch * T + T], 0.0, 0.125, ALU.min, ALU.mult,
                     r=CH.res[qch:qch + 1], w=EST.res[0:1])
                b.op("v", "tensor_scalar", CH.t[:, qch * T: qch * T + T],
                     CH.t[:, qch * T: qch * T + T], 0.0, 0.0025, ALU.max, ALU.mult,
                     r=CH.res[qch:qch + 1], w=CH.res[qch:qch + 1])
                b.op("v", "tensor_add", CH.t[:, qch * T: qch * T + T],
                     CH.t[:, qch * T: qch * T + T], EST.t[:, 0:T],
                     r=CH.res[qch:qch + 1] + EST.res[0:1], w=CH.res[qch:qch + 1])
                for sl in range(NSL):
                    ntt = min(2 * sl + 2, TT)
                    for grp in range(-(-ntt // 4)):
                        rem = min(4, ntt - grp * 4)
                        for hs in range(2):
                            PSX = PSA if hs == 0 else PSB
                            for k in range(rem):
                                tt = grp * 4 + k
                                b.op("t", "matmul", PSX.t[:, k * NQ:(k + 1) * NQ],
                                     KST.t[hs * 64:(hs + 1) * 64,
                                           kach * T + tt * 128: kach * T + tt * 128 + 128],
                                     CH.t[hs * 64:(hs + 1) * 64,
                                          qch * T + sl * NQ: qch * T + sl * NQ + NQ],
                                     start=(k % 2 == 0),
                                     stop=(k % 2 == 1 or k == rem - 1),
                                     tile_position=(hs * 64, 0),
                                     r=KST.res[kach:kach + 1] + CH.res[qch:qch + 1],
                                     w=PSX.res[k // 2:k // 2 + 1])
                            bci0 = hs * TT + grp * 4
                            kd = max(0, min(2 * sl - grp * 4, rem))
                            if kd > 0:
                                b.op("s", "activation",
                                     BC.t[:, bci0 * NQ: (bci0 + kd) * NQ],
                                     PSX.t[:, 0:kd * NQ], ACT.Copy,
                                     r=PSX.res[0:(kd + 1) // 2],
                                     w=BC.res[bci0:bci0 + kd])
                            for k in range(kd, rem):
                                tt = grp * 4 + k
                                di = tt - 2 * sl
                                bci = hs * TT + tt
                                b.op("v", "tensor_mul",
                                     BC.t[:, bci * NQ: (bci + 1) * NQ],
                                     PSX.t[:, k * NQ:(k + 1) * NQ],
                                     MASKS.t[:, 128 - 128 * di: 128 - 128 * di + NQ],
                                     r=PSX.res[k // 2:k // 2 + 1] + MASKS.res,
                                     w=BC.res[bci:bci + 1])
                    psy = PS4 if sl % 2 == 0 else PS5
                    for tt in range(ntt):
                        for hs in range(2):
                            bci = hs * TT + tt
                            b.op("t", "matmul",
                                 psy.t[hs * 64:(hs + 1) * 64, 0:NQ],
                                 EST.t[:, T + tt * 128 + hs * 64: T + tt * 128 + hs * 64 + 64],
                                 BC.t[:, bci * NQ: (bci + 1) * NQ],
                                 start=(tt == 0), stop=(tt == ntt - 1),
                                 tile_position=(0, hs * 64),
                                 r=EST.res[1:2] + BC.res[bci:bci + 1], w=psy.res)
                    zcols = c * T + sl * NQ
                    b.op("v", "tensor_add", BB.t[:, zcols: zcols + NQ],
                         BB.t[:, zcols: zcols + NQ], psy.t[:, 0:NQ],
                         r=psy.res + BB.rr(zcols, zcols + NQ),
                         w=BB.rr(zcols, zcols + NQ))
                    if inject:
                        inject.pop(0)()

            loads, ch, after = prep_chunks(0)
            for f in loads + ch + after:
                f()
            for c in range(C):
                if c + 1 < C:
                    nloads, nxt, anxt = prep_chunks(c + 1)
                    for f in nloads:
                        f()
                else:
                    nxt, anxt = [], []
                attn_pair(c, nxt)
                for f in nxt:
                    f()
                wterm_pair(c, anxt)
                for f in anxt:
                    f()
            for d in range(C):
                wp_s = (d & 1)
                wslot_load(wp_s, "Wproj", d)

                def epi(ps_ap, ps_res, sl2, sap, sres):
                    b.op("v", "tensor_copy", sap, ps_ap, r=ps_res, w=sres)

                fm_to_tm(d, wp_s,
                         lambda t0, nt: (BAv[:, t0:t0 + nt, :], BA.res[t0:t0 + nt]),
                         BB, epi)
            post_ln([(BA.t[:, tt * 1024: tt * 1024 + D], BA.res[tt:tt + 1])
                     for tt in range(TT)])

        def l1_ffn():
            pre_ln()
            for d in range(C):
                ws = (d & 1)
                wslot_load(ws, "Wsign", d)
                b.op("v", "tensor_scalar", wslice(ws), wslice(ws), 0.0, None,
                     ALU.is_ge, r=wres(ws), w=wres(ws))
                b.op("v", "tensor_scalar", wslice(ws), wslice(ws), 2.0, -1.0,
                     ALU.mult, ALU.add, r=wres(ws), w=wres(ws))

                def epi(ps_ap, ps_res, sl2, sap, sres, d=d):
                    b.op("v", "tensor_scalar", TF.t[:, 0:512],
                         ps_ap, MUB.t[:, 2 * C + d: 2 * C + d + 1],
                         MUB.t[:, 3 * C + d: 3 * C + d + 1], ALU.mult, ALU.add,
                         r=ps_res + MUB.res[2:4], w=TF.res[0:2])
                    b.op("v", "tensor_scalar", sap, TF.t[:, 0:512],
                         100.0, -100.0, ALU.min, ALU.max,
                         r=TF.res[0:2], w=sres)

                fm_to_tm(d, ws,
                         lambda t0, nt: (BAv[:, t0:t0 + nt, :], BA.res[t0:t0 + nt]),
                         XN, epi)
            post_ln([(BA.t[:, tt * 1024: tt * 1024 + D], BA.res[tt:tt + 1])
                     for tt in range(TT)])

        for name, buf in (("ident", IDENT), ("masks", MASKS), ("onesa", ONESA),
                          ("onesb", ONESB), ("ones64f", ONES64F)):
            b.op("q", "dma_start", w=buf.res, stream="qc",
                 out=buf.t[:, :], in_=dram[name][:, :])
        for i, nm in enumerate(("mur_t", "muk_t", "scale_t", "bias_t")):
            b.op("q", "dma_start", w=MUB.res[i:i + 1], stream="qc",
                 out=MUB.t[:, i * C:(i + 1) * C], in_=dram[nm][:, :])
        vv_ones = VST.t[:, :].rearrange("p (t h e) -> p t h e", t=TT, h=2)
        b.op("v", "memset", vv_ones[:, :, :, 64:65], 1.0, w=VST.res[0:1])

        SUBS = {"l0a": l0_attn, "l0f": l0_ffn, "l1a": l1_attn, "l1f": l1_ffn}
        for bi in range(cfg.B):
            for tt in range(TT):
                b.op("g", "dma_start", w=X.res[tt:tt + 1], stream="gx",
                     out=X.t[:, tt * 1024: tt * 1024 + D],
                     in_=dram["x"][bi, tt * 128:(tt + 1) * 128, :])
            for sbk in cfg.sub_blocks:
                SUBS[sbk]()
            for tt in range(TT):
                b.op("g", "dma_start", r=X.res[tt:tt + 1], stream="go",
                     out=out_d[bi, tt * 128:(tt + 1) * 128, :],
                     in_=X.t[:, tt * 1024: tt * 1024 + D])

        b.op("g", None, deps=[("d", "go", b.streams.get("go", 0))])
        b.op("q", None, deps=[("d", "go", b.streams.get("go", 0))])

        b.finalize()
        with contextlib.ExitStack() as sctx:
            ctr_sems = {e: sctx.enter_context(nc.semaphore(f"c_{e}")) for e in ENGINES}
            stream_sems = {s: sctx.enter_context(nc.semaphore(f"d_{s}"))
                           for s in b.streams}
            with nc.Block() as block:
                @block.tensor
                def _(eng):
                    b.replay("t", eng, ctr_sems, stream_sems)

                @block.vector
                def _(eng):
                    b.replay("v", eng, ctr_sems, stream_sems)

                @block.scalar
                def _(eng):
                    b.replay("s", eng, ctr_sems, stream_sems)

                @block.gpsimd
                def _(eng):
                    b.replay("g", eng, ctr_sems, stream_sems)

                @block.sync
                def _(eng):
                    b.replay("q", eng, ctr_sems, stream_sems)

    return nc



def prep_shared(cfg, inputs):
    shared = {}
    shared["Wq"] = tile_w(inputs["L0_attn_Wq"])
    shared["Wk"] = tile_w(inputs["L0_attn_Wk"])
    shared["Wv"] = tile_w(inputs["L0_attn_Wv"])
    shared["Wo"] = tile_w(inputs["L0_attn_Wo"])
    shared["Wr"] = tile_w(np.ascontiguousarray(np.asarray(inputs["L0_ffn_W_r"]).T))
    shared["Wkk"] = tile_w(np.ascontiguousarray(np.asarray(inputs["L0_ffn_W_k"]).T))
    shared["Wvv"] = tile_w(np.ascontiguousarray(np.asarray(inputs["L0_ffn_W_v"]).T))
    shared["Wattn"] = tile_w(inputs["L1_attn_W_attn"])
    shared["Wk2"] = tile_w(inputs["L1_attn_W_k2"])
    shared["Wproj"] = tile_w(inputs["L1_attn_W_proj"])
    shared["Wsign"] = tile_w(np.ascontiguousarray(np.asarray(inputs["L1_ffn_weights"]).T))
    shared["mur_t"] = vec_tile(inputs["L0_ffn_mu_r"])
    shared["muk_t"] = vec_tile(inputs["L0_ffn_mu_k"])
    shared["scale_t"] = vec_tile(inputs["L1_ffn_scale"])
    shared["bias_t"] = vec_tile(inputs["L1_ffn_bias"])
    shared.update(host_consts(cfg))
    return shared


def kernel(**inputs):
    cfg = Cfg()
    shared = prep_shared(cfg, inputs)
    x = np.ascontiguousarray(np.asarray(inputs["L0_attn_x"], np.float32))
    nb_total = x.shape[0]
    ncores = 8
    per = nb_total // ncores
    specs = {k: (v.shape, v.dtype) for k, v in shared.items()}
    specs["x"] = ((per, cfg.T, cfg.D), np.float32)
    nc = build_program(cfg, specs)
    in_maps = []
    for i in range(ncores):
        m = dict(shared)
        m["x"] = np.ascontiguousarray(x[i * per:(i + 1) * per])
        in_maps.append(m)
    res = run_bass_kernel_spmd(nc, in_maps, core_ids=list(range(ncores)))
    out = np.concatenate([np.asarray(r["out"]) for r in res.results], axis=0)
    return out.astype(np.float32)
